# version 04
# baseline (speedup 1.0000x reference)
import numpy as np
import ml_dtypes
from contextlib import ExitStack

import concourse.bass as bass
import concourse.mybir as mybir
from concourse.bass_utils import run_bass_kernel_spmd

F32 = mybir.dt.float32
BF16 = mybir.dt.bfloat16
I32 = mybir.dt.int32
AF = mybir.ActivationFunctionType
ALU = mybir.AluOpType
AX = mybir.AxisListType
BF = ml_dtypes.bfloat16

D = 1024
T = 2048
NCORE = 8
SEQ_PER_CORE = 4
NTOK = SEQ_PER_CORE * T
TILE = 512
NTILE = NTOK // TILE
TPS = T // TILE
CH = 128
NCH = TILE // CH
KC = D // 128
EPS = 1e-5

ENGINES = ("sync", "tensor", "vector", "scalar", "gpsimd")


_DEF_SEM = {"sync": ("Lw", 16), "gpsimd": ("S0", 16),
            "tensor": ("T", 1), "vector": ("V", 1), "scalar": ("C", 1)}


class Walk:

    def __init__(self, sems, serial=False):
        self.mode = "plan"
        self.eng = None
        self.sems = sems
        self.serial = serial
        self.counts = {}
        self.live = {}
        self.labels = {}
        self.waited = {}

    def plan(self):
        self.mode = "plan"
        self.counts = {k: 0 for k in self.sems}
        self.labels = {}

    def start_emit(self, engine_name, eng_obj):
        self.mode = engine_name
        self.eng = eng_obj
        self.live = {k: 0 for k in self.sems}

    def _slot(self, engine, inc, k):
        if inc is not None:
            return inc, k
        if self.serial:
            return _DEF_SEM[engine]
        return None, 0

    def op(self, engine, fn, inc=None, k=1, label=None):
        sem, kk = self._slot(engine, inc, k)
        if self.mode == "plan":
            if sem is not None:
                self.counts[sem] += kk
                if label is not None:
                    self.labels[label] = (sem, self.counts[sem])
            else:
                assert label is None
            return
        if self.mode == engine:
            if self.serial:
                for s, v in self.live.items():
                    if v > 0 and self.waited.get((engine, s), -1) < v:
                        self.waited[(engine, s)] = v
                        self.eng.wait_ge(self.sems[s], v)
            ins = fn(self.eng)
            if sem is not None:
                ins.then_inc(self.sems[sem], kk)
        if sem is not None:
            self.live[sem] = self.live.get(sem, 0) + kk

    def wait(self, engine, label, delta=0):
        if self.mode == "plan" or self.mode != engine or self.serial:
            return
        sem_name, val = self.labels[label]
        val += delta
        if val <= 0:
            return
        key = (engine, sem_name)
        if self.waited.get(key, -1) >= val:
            return
        self.waited[key] = val
        self.eng.wait_ge(self.sems[sem_name], val)

    def wait_count(self, engine, sem_name, val):
        if self.mode == "plan" or self.mode != engine:
            return
        if val <= 0:
            return
        key = (engine, sem_name)
        if self.waited.get(key, -1) >= val:
            return
        self.waited[key] = val
        self.eng.wait_ge(self.sems[sem_name], val)

    def drain(self, engine):
        if self.mode == engine:
            self.eng.drain()

    def final_waits(self, engine):
        if self.mode == "plan" or self.mode != engine:
            return
        for s, v in self.totals.items():
            if v > 0 and self.waited.get((engine, s), -1) < v:
                self.waited[(engine, s)] = v
                self.eng.wait_ge(self.sems[s], v)


def build_constants(nc):
    c = {}
    ut = (np.arange(128)[:, None] <= np.arange(128)[None, :])
    c["ones_col_bf"] = nc.inline_tensor(np.ones((128, 1), dtype=BF), name="ones_col_bf").ap()
    c["ones_row_bf"] = nc.inline_tensor(np.ones((1, 128), dtype=BF), name="ones_row_bf").ap()
    m4 = np.tile(ut.astype(np.float32), (1, 4))
    c["maskUT4_f32"] = nc.inline_tensor(m4, name="maskUT4_f32").ap()
    c["cums_bf"] = nc.inline_tensor(ut.astype(BF), name="cums_bf").ap()
    mneg = np.where(ut, 0.0, -1e30).astype(np.float32)
    c["maskNEGT_bf"] = nc.inline_tensor(np.ascontiguousarray(mneg.T).astype(BF), name="maskNEGT_bf").ap()
    c["ident_bf"] = nc.inline_tensor(np.eye(128).astype(BF), name="ident_bf").ap()
    oh = np.zeros((16, 16, 128), dtype=BF)
    for h in range(16):
        oh[h, h, :] = 1
    c["oh16"] = [nc.inline_tensor(oh[h], name=f"oh16_{h}").ap() for h in range(16)]
    r127 = np.zeros((128, 128), dtype=BF)
    r127[127, :] = 1
    c["row127_bf"] = nc.inline_tensor(r127, name="row127_bf").ap()
    c["eps11"] = nc.inline_tensor(np.full((1, 1), EPS, dtype=np.float32), name="eps11").ap()
    c["ident_f32"] = nc.inline_tensor(np.eye(128, dtype=np.float32), name="ident_f32").ap()
    c["ones_row_f32"] = nc.inline_tensor(np.ones((1, 128), dtype=np.float32), name="ones_row_f32").ap()
    oh8 = np.zeros((8, 8, 128), dtype=BF)
    for e in range(8):
        oh8[e, e, :] = 1
    c["oh8"] = [nc.inline_tensor(oh8[e], name=f"oh8_{e}").ap() for e in range(8)]
    oh16f = np.zeros((16, 16, 128), dtype=np.float32)
    for h in range(16):
        oh16f[h, h, :] = 1
    c["oh16f"] = [nc.inline_tensor(oh16f[h], name=f"oh16f_{h}").ap() for h in range(16)]
    prsel = np.zeros((8, 16, 128), dtype=BF)
    for p in range(8):
        prsel[p, 2 * p, 0:64] = 1
        prsel[p, 2 * p + 1, 64:128] = 1
    c["prsel"] = [nc.inline_tensor(prsel[p], name=f"prsel_{p}").ap() for p in range(8)]
    c["cums_f32"] = nc.inline_tensor(ut.astype(np.float32), name="cums_f32").ap()
    r127f = np.zeros((128, 128), dtype=np.float32)
    r127f[127, :] = 1
    c["row127_f32"] = nc.inline_tensor(r127f, name="row127_f32").ap()
    return c


def rstd_row(w, pfx, i, tag, ps_stat, sb, scale, evac_label_prefix=None):
    w.op("vector", lambda e: e.tensor_scalar(
        out=sb["row_a"][0:1, :], in0=ps_stat[0:1, 0:512], scalar1=scale, scalar2=EPS,
        op0=ALU.mult, op1=ALU.add), inc=w.V, label=f"{pfx}.{tag}ssn.{i}")
    w.wait("scalar", f"{pfx}.{tag}ssn.{i}")
    w.op("scalar", lambda e: e.activation(
        out=sb["row_b"][0:1, :], in_=sb["row_a"][0:1, :], func=AF.Sqrt),
        inc=w.C, label=f"{pfx}.{tag}sqr.{i}")
    w.wait("vector", f"{pfx}.{tag}sqr.{i}")
    w.op("vector", lambda e: e.reciprocal(
        out=sb["row_bf"][0:1, :], in_=sb["row_b"][0:1, :]), inc=w.V, label=f"{pfx}.{tag}rstd.{i}")


def p1_emit(w, nc, bufs, xin, xout, wbf, smalls, pfx="p1"):
    sb = bufs["sb"]
    ps = bufs["ps"]
    cst = bufs["cst"]
    TQ, V, C = w.TQ, w.V, w.C

    for name in ("wq", "wk", "wv", "wo"):
        for k in range(KC):
            w.op("sync", lambda e, n=name, k=k: e.dma_start(
                out=sb[n][:, k * 1024:(k + 1) * 1024], in_=wbf[n][k * 128:(k + 1) * 128, :]),
                inc="Lw", k=16, label=f"{pfx}.w.{name}.{k}")
    w.op("sync", lambda e: e.dma_start(
        out=sb["nw1"][:, 0:KC], in_=smalls["attn_norm_w"][:, :]),
        inc="Lw", k=16, label=f"{pfx}.w.nw1")
    w.op("sync", lambda e: e.dma_start(
        out=sb["onw"][:, 0:1], in_=smalls["out_norm_w"][:, :]),
        inc="Lw", k=16, label=f"{pfx}.w.onw")
    w.wait("tensor", f"{pfx}.w.onw")
    w.wait("vector", f"{pfx}.w.onw")
    w.wait("scalar", f"{pfx}.w.onw")

    for i in range(NTILE):
        seq_start = (i % TPS == 0)
        t0 = i * TILE
        xb = sb["x0"] if i % 2 == 0 else sb["x1"]

        if i >= 2:
            w.wait("sync", f"{pfx}.st.{i-2}.{KC-1}")
        for kc in range(KC):
            w.op("sync", lambda e, kc=kc, t0=t0, xb=xb: e.dma_start(
                out=xb[:, kc * 512:(kc + 1) * 512],
                in_=xin[kc * 128:(kc + 1) * 128, t0:t0 + TILE]),
                inc=f"Lx{i % 2}", k=16, label=f"{pfx}.x.{i}.{kc}")

        w.wait("scalar", f"{pfx}.x.{i}.{KC-1}")
        w.op("scalar", lambda e, xb=xb: e.activation(
            out=sb["xsq"][:], in_=xb[:], func=AF.Square),
            inc=C, label=f"{pfx}.sq.{i}")
        w.wait("tensor", f"{pfx}.sq.{i}")
        if i >= 1:
            w.wait("tensor", f"{pfx}.h.{i-1}")
        for kc in range(KC):
            w.op("tensor", lambda e, kc=kc: e.matmul(
                ps[6][0:1, 0:512], cst["ones_col"], sb["xsq"][:, kc * 512:(kc + 1) * 512],
                start=(kc == 0), stop=(kc == KC - 1)),
                inc=(TQ if kc == KC - 1 else None),
                label=(f"{pfx}.ss.{i}" if kc == KC - 1 else None))
        w.wait("vector", f"{pfx}.ss.{i}")
        if i >= 1:
            w.wait("vector", f"{pfx}.osqr.{i-1}.7")
            w.wait("vector", f"{pfx}.obc.{i-1}.7")
        rstd_row(w, pfx, i, "", ps[6], sb, 1.0 / D)
        w.wait("tensor", f"{pfx}.rstd.{i}")
        w.op("tensor", lambda e: e.matmul(
            ps[6][:, 0:512], cst["ones_row"], sb["row_bf"][0:1, 0:512], start=True, stop=True),
            inc=TQ, label=f"{pfx}.bc.{i}")
        w.wait("vector", f"{pfx}.bc.{i}")
        if i >= 1:
            w.wait("vector", f"{pfx}.vt.mm.{i-1}.7")
        for kc in range(KC):
            w.op("vector", lambda e, kc=kc, xb=xb: e.scalar_tensor_tensor(
                out=sb["h"][:, kc * 512:(kc + 1) * 512], in0=xb[:, kc * 512:(kc + 1) * 512],
                scalar=sb["nw1"][:, kc:kc + 1], in1=ps[6][:, 0:512],
                op0=ALU.mult, op1=ALU.mult),
                inc=(V if kc == KC - 1 else None),
                label=(f"{pfx}.h.{i}" if kc == KC - 1 else None))

        w.wait("tensor", f"{pfx}.h.{i}")

        def proj_fm(dst, wname, evac_eng, tag, prev_tag):
            for m in range(KC):
                bank = ps[m % 2]
                if m >= 2:
                    w.wait("tensor", f"{pfx}.{tag}.ev.{i}.{m-2}")
                elif prev_tag is not None:
                    w.wait("tensor", f"{pfx}.{prev_tag}.ev.{i}.{6+m}")
                for k in range(KC):
                    w.op("tensor", lambda e, m=m, k=k, wname=wname, bank=bank: e.matmul(
                        bank[:, 0:512], sb[wname][:, k * 1024 + m * 128: k * 1024 + (m + 1) * 128],
                        sb["h"][:, k * 512:(k + 1) * 512],
                        start=(k == 0), stop=(k == KC - 1)),
                        inc=(TQ if k == KC - 1 else None),
                        label=(f"{pfx}.{tag}.mm.{i}.{m}" if k == KC - 1 else None))
                w.wait(evac_eng, f"{pfx}.{tag}.mm.{i}.{m}")
                semn = V if evac_eng == "vector" else C
                if evac_eng == "vector":
                    w.op("vector", lambda e, m=m, dst=dst, bank=bank: e.tensor_copy(
                        out=dst[:, m * 512:(m + 1) * 512], in_=bank[:, 0:512]),
                        inc=semn, label=f"{pfx}.{tag}.ev.{i}.{m}")
                else:
                    w.op("scalar", lambda e, m=m, dst=dst, bank=bank: e.copy(
                        out=dst[:, m * 512:(m + 1) * 512], in_=bank[:, 0:512]),
                        inc=semn, label=f"{pfx}.{tag}.ev.{i}.{m}")

        def proj_tm(dst, wname, evac_eng, tag, prev_tag):
            for c in range(NCH):
                for hg in range(2):
                    gi = c * 2 + hg
                    bank = ps[gi % 2]
                    if gi >= 2:
                        w.wait("tensor", f"{pfx}.{tag}.ev.{i}.{gi-2}")
                    elif prev_tag is not None:
                        w.wait("tensor", f"{pfx}.{prev_tag}.ev.{i}.{6+gi}")
                    for k in range(KC):
                        w.op("tensor", lambda e, c=c, hg=hg, k=k, wname=wname, bank=bank: e.matmul(
                            bank[:, 0:512],
                            sb["h"][:, k * 512 + c * 128: k * 512 + (c + 1) * 128],
                            sb[wname][:, k * 1024 + hg * 512: k * 1024 + (hg + 1) * 512],
                            start=(k == 0), stop=(k == KC - 1)),
                            inc=(TQ if k == KC - 1 else None),
                            label=(f"{pfx}.{tag}.mm.{i}.{gi}" if k == KC - 1 else None))
                    w.wait(evac_eng, f"{pfx}.{tag}.mm.{i}.{gi}")
                    semn = V if evac_eng == "vector" else C
                    dsl = dst[:, c * 1024 + hg * 512: c * 1024 + (hg + 1) * 512]
                    if evac_eng == "vector":
                        w.op("vector", lambda e, dsl=dsl, bank=bank: e.tensor_copy(
                            out=dsl, in_=bank[:, 0:512]),
                            inc=semn, label=f"{pfx}.{tag}.ev.{i}.{gi}")
                    else:
                        w.op("scalar", lambda e, dsl=dsl, bank=bank: e.copy(
                            out=dsl, in_=bank[:, 0:512]),
                            inc=semn, label=f"{pfx}.{tag}.ev.{i}.{gi}")

        if i >= 1:
            w.wait("tensor", f"{pfx}.res.{i-1}.{KC-2}")
            w.wait("tensor", f"{pfx}.res.{i-1}.{KC-1}")
        proj_fm(sb["q"], "wq", "scalar", "q", None)
        proj_fm(sb["k"], "wk", "vector", "k", "q")
        proj_tm(sb["kt"], "wk", "scalar", "kt", "k")
        proj_tm(sb["vt"], "wv", "vector", "vt", "kt")

        w.wait("tensor", f"{pfx}.q.ev.{i}.{KC-1}")
        w.wait("tensor", f"{pfx}.k.ev.{i}.{KC-1}")
        w.wait("tensor", f"{pfx}.kt.ev.{i}.7")
        w.wait("tensor", f"{pfx}.vt.ev.{i}.7")
        for c in range(NCH):
            first = seq_start and c == 0
            for g in range(2):
                gi = c * 2 + g
                b_at = ps[gi % 2]
                b_o = ps[2 + gi % 2]
                b_sd = ps[4 + gi % 2]
                heads = range(4 * g, 4 * g + 4)

                if gi >= 2:
                    w.wait("tensor", f"{pfx}.at.ev.{i}.{gi-2}")
                elif i >= 1:
                    w.wait("tensor", f"{pfx}.at.ev.{i-1}.{6+gi}")
                for hh, h in enumerate(heads):
                    qs = sb["q"][:, h * 512 + c * 128: h * 512 + (c + 1) * 128]
                    ks = sb["k"][:, h * 512 + c * 128: h * 512 + (c + 1) * 128]
                    w.op("tensor", lambda e, ks=ks, qs=qs, hh=hh, b_at=b_at: e.matmul(
                        b_at[:, hh * 128:(hh + 1) * 128], ks, qs, start=True, stop=True),
                        inc=(TQ if hh == 3 else None),
                        label=(f"{pfx}.at.mm.{i}.{gi}" if hh == 3 else None))
                w.wait("vector", f"{pfx}.at.mm.{i}.{gi}")
                w.op("vector", lambda e, g=g, b_at=b_at: e.tensor_mul(
                    out=sb["at_bf"][:, g * 512:(g + 1) * 512], in0=b_at[:, 0:512],
                    in1=cst["maskUT4"]),
                    inc=V, label=f"{pfx}.at.ev.{i}.{gi}")

                if gi >= 2:
                    w.wait("tensor", f"{pfx}.o.ev.{i}.{gi-2}")
                elif i >= 1:
                    w.wait("tensor", f"{pfx}.on.{i-1}.{6+gi}")
                w.wait("tensor", f"{pfx}.at.ev.{i}.{gi}")
                if not first:
                    w.wait("tensor", f"{pfx}.Sbf.{i}.{gi-2}" if c > 0 else f"{pfx}.Sbf.{i-1}.{6+g}")
                for hh, h in enumerate(heads):
                    vts = sb["vt"][:, c * 1024 + h * 128: c * 1024 + (h + 1) * 128]
                    qs = sb["q"][:, h * 512 + c * 128: h * 512 + (c + 1) * 128]
                    atb = sb["at_bf"][:, g * 512 + hh * 128: g * 512 + (hh + 1) * 128]
                    Ssl_b = sb["S_bf"][:, h * 128:(h + 1) * 128]
                    if first:
                        w.op("tensor", lambda e, vts=vts, atb=atb, hh=hh, b_o=b_o: e.matmul(
                            b_o[:, hh * 128:(hh + 1) * 128], vts, atb, start=True, stop=True),
                            inc=(TQ if hh == 3 else None),
                            label=(f"{pfx}.o.mm.{i}.{gi}" if hh == 3 else None))
                    else:
                        w.op("tensor", lambda e, vts=vts, atb=atb, hh=hh, b_o=b_o: e.matmul(
                            b_o[:, hh * 128:(hh + 1) * 128], vts, atb, start=True, stop=False))
                        w.op("tensor", lambda e, qs=qs, Ssl_b=Ssl_b, hh=hh, b_o=b_o: e.matmul(
                            b_o[:, hh * 128:(hh + 1) * 128], Ssl_b, qs, start=False, stop=True),
                            inc=(TQ if hh == 3 else None),
                            label=(f"{pfx}.o.mm.{i}.{gi}" if hh == 3 else None))
                w.wait("vector", f"{pfx}.o.mm.{i}.{gi}")
                if gi == 0 and i >= 1:
                    w.wait("vector", f"{pfx}.osq.{i-1}")
                for hh, h in enumerate(heads):
                    w.op("vector", lambda e, c=c, h=h, hh=hh, b_o=b_o: e.tensor_copy(
                        out=sb["o"][:, h * 512 + c * 128: h * 512 + (c + 1) * 128],
                        in_=b_o[:, hh * 128:(hh + 1) * 128]),
                        inc=(V if hh == 3 else None),
                        label=(f"{pfx}.o.ev.{i}.{gi}" if hh == 3 else None))

                if gi >= 2:
                    w.wait("tensor", f"{pfx}.Sadd.{i}.{gi-2}")
                elif i >= 1:
                    w.wait("tensor", f"{pfx}.ossn.{i-1}.{6+gi}")
                for hh, h in enumerate(heads):
                    kts = sb["kt"][:, c * 1024 + h * 128: c * 1024 + (h + 1) * 128]
                    vts = sb["vt"][:, c * 1024 + h * 128: c * 1024 + (h + 1) * 128]
                    w.op("tensor", lambda e, kts=kts, vts=vts, hh=hh, b_sd=b_sd: e.matmul(
                        b_sd[:, hh * 128:(hh + 1) * 128], kts, vts, start=True, stop=True),
                        inc=(TQ if hh == 3 else None),
                        label=(f"{pfx}.Sd.{i}.{gi}" if hh == 3 else None))
                w.wait("vector", f"{pfx}.Sd.{i}.{gi}")
                if not (seq_start and c == 0):
                    w.wait("vector", f"{pfx}.Sadd.{i}.{gi-2}" if c > 0 else f"{pfx}.Sadd.{i-1}.{6+g}")
                Sf = sb["S_f32"][:, g * 512:(g + 1) * 512]
                Sb = sb["S_bf"][:, g * 512:(g + 1) * 512]
                if first:
                    w.op("vector", lambda e, Sf=Sf, b_sd=b_sd: e.tensor_copy(
                        out=Sf, in_=b_sd[:, 0:512]),
                        inc=V, label=f"{pfx}.Sadd.{i}.{gi}")
                else:
                    w.op("vector", lambda e, Sf=Sf, b_sd=b_sd: e.tensor_add(
                        out=Sf, in0=Sf, in1=b_sd[:, 0:512]),
                        inc=V, label=f"{pfx}.Sadd.{i}.{gi}")
                w.wait("scalar", f"{pfx}.Sadd.{i}.{gi}")
                w.op("scalar", lambda e, Sf=Sf, Sb=Sb: e.copy(out=Sb, in_=Sf),
                    inc=C, label=f"{pfx}.Sbf.{i}.{gi}")

        w.wait("scalar", f"{pfx}.o.ev.{i}.7")
        w.wait("vector", f"{pfx}.o.ev.{i}.7")
        w.op("scalar", lambda e: e.activation(
            out=sb["xsq"][:], in_=sb["o"][:], func=AF.Square),
            inc=C, label=f"{pfx}.osq.{i}")
        w.wait("tensor", f"{pfx}.osq.{i}")
        for blk in range(8):
            b_ss = ps[4 + blk % 2]
            b_bc = ps[2 + blk % 2]
            if blk >= 2:
                w.wait("tensor", f"{pfx}.ossn.{i}.{blk-2}")
                w.wait("tensor", f"{pfx}.on.{i}.{blk-2}")
            else:
                w.wait("tensor", f"{pfx}.Sadd.{i}.{6+blk}")
                w.wait("tensor", f"{pfx}.o.ev.{i}.{6+blk}")
            w.op("tensor", lambda e, blk=blk, b_ss=b_ss: e.matmul(
                b_ss[0:1, 0:512], cst["ones_col"], sb["xsq"][:, blk * 512:(blk + 1) * 512],
                start=True, stop=True),
                inc=TQ, label=f"{pfx}.oss.{i}.{blk}")
            w.wait("vector", f"{pfx}.oss.{i}.{blk}")
            if blk >= 1:
                w.wait("vector", f"{pfx}.osqr.{i}.{blk-1}")
            else:
                w.wait("vector", f"{pfx}.sqr.{i}")
            w.op("vector", lambda e, b_ss=b_ss: e.tensor_scalar(
                out=sb["row_a"][0:1, :], in0=b_ss[0:1, 0:512], scalar1=1.0 / 128, scalar2=EPS,
                op0=ALU.mult, op1=ALU.add), inc=V, label=f"{pfx}.ossn.{i}.{blk}")
            w.wait("scalar", f"{pfx}.ossn.{i}.{blk}")
            w.op("scalar", lambda e: e.activation(
                out=sb["row_b"][0:1, :], in_=sb["row_a"][0:1, :], func=AF.Sqrt),
                inc=C, label=f"{pfx}.osqr.{i}.{blk}")
            w.wait("vector", f"{pfx}.osqr.{i}.{blk}")
            if blk >= 1:
                w.wait("vector", f"{pfx}.obc.{i}.{blk-1}")
            else:
                w.wait("vector", f"{pfx}.bc.{i}")
            w.op("vector", lambda e: e.reciprocal(
                out=sb["row_bf"][0:1, :], in_=sb["row_b"][0:1, :]),
                inc=V, label=f"{pfx}.orstd.{i}.{blk}")
            w.wait("tensor", f"{pfx}.orstd.{i}.{blk}")
            w.op("tensor", lambda e, b_bc=b_bc: e.matmul(
                b_bc[:, 0:512], cst["ones_row"], sb["row_bf"][0:1, 0:512],
                start=True, stop=True),
                inc=TQ, label=f"{pfx}.obc.{i}.{blk}")
            w.wait("vector", f"{pfx}.obc.{i}.{blk}")
            w.op("vector", lambda e, blk=blk, b_bc=b_bc: e.scalar_tensor_tensor(
                out=sb["on_bf"][:, blk * 512:(blk + 1) * 512],
                in0=sb["o"][:, blk * 512:(blk + 1) * 512],
                scalar=sb["onw"][:, 0:1], in1=b_bc[:, 0:512],
                op0=ALU.mult, op1=ALU.mult),
                inc=V, label=f"{pfx}.on.{i}.{blk}")

        w.wait("tensor", f"{pfx}.on.{i}.7")
        for m in range(KC):
            bank = ps[m % 2]
            if m >= 2:
                w.wait("tensor", f"{pfx}.res.{i}.{m-2}")
            else:
                w.wait("tensor", f"{pfx}.at.ev.{i}.{6+m}")
            for h in range(KC):
                w.op("tensor", lambda e, m=m, h=h, bank=bank: e.matmul(
                    bank[:, 0:512], sb["wo"][:, h * 1024 + m * 128: h * 1024 + (m + 1) * 128],
                    sb["on_bf"][:, h * 512:(h + 1) * 512],
                    start=(h == 0), stop=(h == KC - 1)),
                    inc=(TQ if h == KC - 1 else None),
                    label=(f"{pfx}.wo.{i}.{m}" if h == KC - 1 else None))
            w.wait("vector", f"{pfx}.wo.{i}.{m}")
            w.op("vector", lambda e, m=m, xb=xb, bank=bank: e.tensor_add(
                out=xb[:, m * 512:(m + 1) * 512], in0=xb[:, m * 512:(m + 1) * 512],
                in1=bank[:, 0:512]),
                inc=V, label=f"{pfx}.res.{i}.{m}")

        for kc in range(KC):
            w.wait("gpsimd", f"{pfx}.res.{i}.{kc}")
            w.op("gpsimd", lambda e, kc=kc, t0=t0, xb=xb: e.dma_start(
                out=xout[kc * 128:(kc + 1) * 128, t0:t0 + TILE],
                in_=xb[:, kc * 512:(kc + 1) * 512]),
                inc=f"S{i % 2}", k=16, label=f"{pfx}.st.{i}.{kc}")

    w.wait_count("gpsimd", "S0", 16 * KC * (NTILE // 2))
    w.wait_count("gpsimd", "S1", 16 * KC * (NTILE // 2))


def postln_tile(w, pfx, i, sb, ps, cst, xb):
    V, C, TQ = w.V, w.C, w.TQ
    w.op("scalar", lambda e: e.copy(out=sb["xbf"][:], in_=xb[:]),
         inc=C, label=f"{pfx}.pl.cast.{i}")
    w.op("scalar", lambda e: e.activation(out=sb["xsq"][:], in_=xb[:], func=AF.Square),
         inc=C, label=f"{pfx}.pl.sq.{i}")
    w.wait("tensor", f"{pfx}.pl.sq.{i}")
    for kc in range(KC):
        w.op("tensor", lambda e, kc=kc: e.matmul(
            ps[4][0:1, 0:512], cst["ones_col"], sb["xbf"][:, kc * 512:(kc + 1) * 512],
            start=(kc == 0), stop=(kc == KC - 1)),
            inc=(TQ if kc == KC - 1 else None),
            label=(f"{pfx}.pl.sum.{i}" if kc == KC - 1 else None))
    for kc in range(KC):
        w.op("tensor", lambda e, kc=kc: e.matmul(
            ps[5][0:1, 0:512], cst["ones_col"], sb["xsq"][:, kc * 512:(kc + 1) * 512],
            start=(kc == 0), stop=(kc == KC - 1)),
            inc=(TQ if kc == KC - 1 else None),
            label=(f"{pfx}.pl.ss.{i}" if kc == KC - 1 else None))
    w.wait("vector", f"{pfx}.pl.ss.{i}")
    w.op("vector", lambda e: e.tensor_scalar(
        out=sb["row_mu"][0:1, :], in0=ps[4][0:1, 0:512], scalar1=1.0 / D, scalar2=None,
        op0=ALU.mult))
    w.op("vector", lambda e: e.tensor_scalar(
        out=sb["row_a"][0:1, :], in0=ps[5][0:1, 0:512], scalar1=1.0 / D, scalar2=None,
        op0=ALU.mult))
    w.drain("vector")
    w.op("vector", lambda e: e.tensor_mul(
        out=sb["row_c"][0:1, :], in0=sb["row_mu"][0:1, :], in1=sb["row_mu"][0:1, :]))
    w.op("vector", lambda e: e.tensor_copy(out=sb["row_mu_bf"][0:1, :], in_=sb["row_mu"][0:1, :]))
    w.drain("vector")
    w.op("vector", lambda e: e.tensor_sub(
        out=sb["row_b"][0:1, :], in0=sb["row_a"][0:1, :], in1=sb["row_c"][0:1, :]),
        inc=V, label=f"{pfx}.pl.var.{i}")
    w.wait("scalar", f"{pfx}.pl.var.{i}")
    w.op("scalar", lambda e: e.activation(
        out=sb["row_a"][0:1, :], in_=sb["row_b"][0:1, :], func=AF.Sqrt, bias=cst["eps11"][0:1, 0:1]),
        inc=C, label=f"{pfx}.pl.sqr.{i}")
    w.wait("vector", f"{pfx}.pl.sqr.{i}")
    w.op("vector", lambda e: e.reciprocal(
        out=sb["row_bf"][0:1, :], in_=sb["row_a"][0:1, :]),
        inc=V, label=f"{pfx}.pl.rstd.{i}")
    w.wait("tensor", f"{pfx}.pl.rstd.{i}")
    w.op("tensor", lambda e: e.matmul(
        ps[6][:, 0:512], cst["ones_row"], sb["row_mu_bf"][0:1, 0:512], start=True, stop=True),
        inc=TQ, label=f"{pfx}.pl.mubc.{i}")
    w.op("tensor", lambda e: e.matmul(
        ps[7][:, 0:512], cst["ones_row"], sb["row_bf"][0:1, 0:512], start=True, stop=True),
        inc=TQ, label=f"{pfx}.pl.rsbc.{i}")
    w.wait("vector", f"{pfx}.pl.rsbc.{i}")
    for kc in range(KC):
        w.op("vector", lambda e, kc=kc, xb=xb: e.tensor_sub(
            out=sb["tmp2"][:, kc * 512:(kc + 1) * 512],
            in0=xb[:, kc * 512:(kc + 1) * 512], in1=ps[6][:, 0:512]),
            inc=(V if kc == KC - 1 else None),
            label=(f"{pfx}.pl.sub.{i}" if kc == KC - 1 else None))
    w.wait("vector", f"{pfx}.pl.sub.{i}")
    for kc in range(KC):
        w.op("vector", lambda e, kc=kc, xb=xb: e.tensor_mul(
            out=xb[:, kc * 512:(kc + 1) * 512], in0=sb["tmp2"][:, kc * 512:(kc + 1) * 512],
            in1=ps[7][:, 0:512]),
            inc=(V if kc == KC - 1 else None),
            label=(f"{pfx}.pl.app.{i}" if kc == KC - 1 else None))


def mlp_emit(w, nc, bufs, xin, xout, wbf, smalls, cfg, pfx, xin0=None):
    sb = bufs["sb"]
    ps = bufs["ps"]
    cst = bufs["cst"]
    TQ, V, C = w.TQ, w.V, w.C
    swiglu = cfg["kind"] == "swiglu"
    NQ = 4
    if swiglu:
        parcels = []
        for q in range(NQ):
            parcels += [("wg", q), ("wu", q)]
        parcels += [("wd", q) for q in range(NQ)]
    else:
        parcels = [("wg", q) for q in range(NQ)] + [("wd", q) for q in range(NQ)]
    NP = len(parcels)

    def pdone_label(ii, jj):
        name, q = parcels[jj]
        if name == "wg":
            return f"{pfx}.g.mm.{ii}.{q*8+7}"
        if name == "wu":
            return f"{pfx}.u.mm.{ii}.{q*8+7}"
        return f"{pfx}.dq.{ii}.{q}"

    for name, ap in smalls.items():
        w.op("sync", lambda e, name=name, ap=ap: e.dma_start(
            out=sb[name][:, :], in_=ap), inc="Lw", k=16, label=f"{pfx}.w.{name}")
    last_small = list(smalls.keys())[-1]
    w.wait("vector", f"{pfx}.w.{last_small}")
    w.wait("scalar", f"{pfx}.w.{last_small}")
    if not swiglu:
        w.op("vector", lambda e: e.tensor_scalar(
            out=sb["b2h"][:, :], in0=sb["b2"][:, :], scalar1=0.5, scalar2=None, op0=ALU.mult),
            inc=V, label=f"{pfx}.b2h")
        w.wait("scalar", f"{pfx}.b2h")

    def parcel_load(i, j):
        P = i * NP + j
        name, q = parcels[j]
        slot = sb["wslot"][P % 4]
        if P >= 4:
            jj = (P - 4) % NP
            ii = (P - 4) // NP
            w.wait("sync", pdone_label(ii, jj))
        lsem = f"Lp{P % 4}"
        if name == "wd":
            srcw = wbf["wd"]
            for kk in range(8):
                w.op("sync", lambda e, kk=kk, q=q, slot=slot, srcw=srcw: e.dma_start(
                    out=slot[:, kk * 1024:(kk + 1) * 1024],
                    in_=srcw[q * 1024 + kk * 128: q * 1024 + (kk + 1) * 128, :]),
                    inc=lsem, k=16, label=(f"{pfx}.pld.{i}.{j}" if kk == 7 else None))
        else:
            srcw = wbf[name]
            for kk in range(8):
                w.op("sync", lambda e, kk=kk, q=q, slot=slot, srcw=srcw: e.dma_start(
                    out=slot[:, kk * 1024:(kk + 1) * 1024],
                    in_=srcw[kk * 128:(kk + 1) * 128, q * 1024:(q + 1) * 1024]),
                    inc=lsem, k=16, label=(f"{pfx}.pld.{i}.{j}" if kk == 7 else None))

    for i in range(NTILE):
        t0 = i * TILE
        xb = sb["x0"] if i % 2 == 0 else sb["x1"]

        if i >= 2:
            w.wait("sync", f"{pfx}.st.{i-2}.{KC-1}")
        for kc in range(KC):
            w.op("sync", lambda e, kc=kc, t0=t0, xb=xb: e.dma_start(
                out=xb[:, kc * 512:(kc + 1) * 512],
                in_=xin[kc * 128:(kc + 1) * 128, t0:t0 + TILE]),
                inc=f"Lx{i % 2}", k=16, label=f"{pfx}.x.{i}.{kc}")
        if xin0 is not None:
            if i >= 1:
                w.wait("sync", f"{pfx}.res2.{i-1}")
            for kc in range(KC):
                w.op("sync", lambda e, kc=kc, t0=t0: e.dma_start(
                    out=sb["x0s"][:, kc * 512:(kc + 1) * 512],
                    in_=xin0[kc * 128:(kc + 1) * 128, t0:t0 + TILE]),
                    inc="Lx2", k=16, label=f"{pfx}.x0.{i}.{kc}")

        w.wait("scalar", f"{pfx}.x.{i}.{KC-1}")
        if cfg["pre"] == "rms":
            w.op("scalar", lambda e, xb=xb: e.activation(
                out=sb["xsq"][:], in_=xb[:], func=AF.Square),
                inc=C, label=f"{pfx}.sq.{i}")
            w.wait("tensor", f"{pfx}.sq.{i}")
            if i >= 1:
                w.wait("tensor", f"{pfx}.pl.app.{i-1}")
                w.wait("tensor", f"{pfx}.h.{i-1}")
            for kc in range(KC):
                w.op("tensor", lambda e, kc=kc: e.matmul(
                    ps[6][0:1, 0:512], cst["ones_col"], sb["xsq"][:, kc * 512:(kc + 1) * 512],
                    start=(kc == 0), stop=(kc == KC - 1)),
                    inc=(TQ if kc == KC - 1 else None),
                    label=(f"{pfx}.ss.{i}" if kc == KC - 1 else None))
            w.wait("vector", f"{pfx}.ss.{i}")
            rstd_row(w, pfx, i, "", ps[6], sb, 1.0 / D)
            w.wait("tensor", f"{pfx}.rstd.{i}")
            w.op("tensor", lambda e: e.matmul(
                ps[6][:, 0:512], cst["ones_row"], sb["row_bf"][0:1, 0:512], start=True, stop=True),
                inc=TQ, label=f"{pfx}.bc.{i}")
            w.wait("vector", f"{pfx}.bc.{i}")
            if i >= 1:
                w.wait("vector", f"{pfx}.u.mm.{i-1}.31")
            for kc in range(KC):
                w.op("vector", lambda e, kc=kc, xb=xb: e.scalar_tensor_tensor(
                    out=sb["h"][:, kc * 512:(kc + 1) * 512], in0=xb[:, kc * 512:(kc + 1) * 512],
                    scalar=sb["nw"][:, kc:kc + 1], in1=ps[6][:, 0:512],
                    op0=ALU.mult, op1=ALU.mult),
                    inc=(V if kc == KC - 1 else None),
                    label=(f"{pfx}.h.{i}" if kc == KC - 1 else None))
        else:
            w.op("scalar", lambda e, xb=xb: e.copy(out=sb["xbf"][:], in_=xb[:]),
                 inc=C, label=f"{pfx}.cast.{i}")
            w.op("scalar", lambda e, xb=xb: e.activation(
                out=sb["xsq"][:], in_=xb[:], func=AF.Square),
                inc=C, label=f"{pfx}.sq.{i}")
            w.wait("tensor", f"{pfx}.sq.{i}")
            if i >= 1:
                w.wait("tensor", f"{pfx}.pl.app.{i-1}")
                w.wait("tensor", f"{pfx}.h.{i-1}")
            for kc in range(KC):
                w.op("tensor", lambda e, kc=kc: e.matmul(
                    ps[4][0:1, 0:512], cst["ones_col"], sb["xbf"][:, kc * 512:(kc + 1) * 512],
                    start=(kc == 0), stop=(kc == KC - 1)),
                    inc=(TQ if kc == KC - 1 else None),
                    label=(f"{pfx}.lnsum.{i}" if kc == KC - 1 else None))
            for kc in range(KC):
                w.op("tensor", lambda e, kc=kc: e.matmul(
                    ps[5][0:1, 0:512], cst["ones_col"], sb["xsq"][:, kc * 512:(kc + 1) * 512],
                    start=(kc == 0), stop=(kc == KC - 1)),
                    inc=(TQ if kc == KC - 1 else None),
                    label=(f"{pfx}.lnss.{i}" if kc == KC - 1 else None))
            w.wait("vector", f"{pfx}.lnss.{i}")
            w.op("vector", lambda e: e.tensor_scalar(
                out=sb["row_mu"][0:1, :], in0=ps[4][0:1, 0:512], scalar1=1.0 / D, scalar2=None,
                op0=ALU.mult))
            w.op("vector", lambda e: e.tensor_scalar(
                out=sb["row_a"][0:1, :], in0=ps[5][0:1, 0:512], scalar1=1.0 / D, scalar2=None,
                op0=ALU.mult))
            w.drain("vector")
            w.op("vector", lambda e: e.tensor_mul(
                out=sb["row_c"][0:1, :], in0=sb["row_mu"][0:1, :], in1=sb["row_mu"][0:1, :]))
            w.op("vector", lambda e: e.tensor_copy(out=sb["row_mu_bf"][0:1, :], in_=sb["row_mu"][0:1, :]))
            w.drain("vector")
            w.op("vector", lambda e: e.tensor_sub(
                out=sb["row_b"][0:1, :], in0=sb["row_a"][0:1, :], in1=sb["row_c"][0:1, :]),
                inc=V, label=f"{pfx}.lnvar.{i}")
            w.wait("scalar", f"{pfx}.lnvar.{i}")
            w.op("scalar", lambda e: e.activation(
                out=sb["row_a"][0:1, :], in_=sb["row_b"][0:1, :], func=AF.Sqrt,
                bias=cst["eps11"][0:1, 0:1]),
                inc=C, label=f"{pfx}.lnsqr.{i}")
            w.wait("vector", f"{pfx}.lnsqr.{i}")
            w.op("vector", lambda e: e.reciprocal(
                out=sb["row_bf"][0:1, :], in_=sb["row_a"][0:1, :]),
                inc=V, label=f"{pfx}.lnrstd.{i}")
            w.wait("tensor", f"{pfx}.lnrstd.{i}")
            w.op("tensor", lambda e: e.matmul(
                ps[6][:, 0:512], cst["ones_row"], sb["row_mu_bf"][0:1, 0:512], start=True, stop=True),
                inc=TQ, label=f"{pfx}.lnmubc.{i}")
            w.op("tensor", lambda e: e.matmul(
                ps[7][:, 0:512], cst["ones_row"], sb["row_bf"][0:1, 0:512], start=True, stop=True),
                inc=TQ, label=f"{pfx}.lnrsbc.{i}")
            w.wait("vector", f"{pfx}.lnrsbc.{i}")
            for kc in range(KC):
                w.op("vector", lambda e, kc=kc, xb=xb: e.tensor_sub(
                    out=sb["tmp2"][:, kc * 512:(kc + 1) * 512],
                    in0=xb[:, kc * 512:(kc + 1) * 512], in1=ps[6][:, 0:512]),
                    inc=(V if kc == KC - 1 else None),
                    label=(f"{pfx}.lnsub.{i}" if kc == KC - 1 else None))
            w.wait("vector", f"{pfx}.lnsub.{i}")
            for kc in range(KC):
                w.op("vector", lambda e, kc=kc: e.tensor_mul(
                    out=sb["tmp2"][:, kc * 512:(kc + 1) * 512],
                    in0=sb["tmp2"][:, kc * 512:(kc + 1) * 512],
                    in1=ps[7][:, 0:512]),
                    inc=(V if kc == KC - 1 else None),
                    label=(f"{pfx}.lnz.{i}" if kc == KC - 1 else None))
            w.wait("vector", f"{pfx}.lnz.{i}")
            if i >= 1:
                w.wait("vector", f"{pfx}.g.mm.{i-1}.31")
            for kc in range(KC):
                w.op("vector", lambda e, kc=kc: e.tensor_scalar(
                    out=sb["h"][:, kc * 512:(kc + 1) * 512],
                    in0=sb["tmp2"][:, kc * 512:(kc + 1) * 512],
                    scalar1=sb["gamma"][:, kc:kc + 1], scalar2=sb["beta"][:, kc:kc + 1],
                    op0=ALU.mult, op1=ALU.add),
                    inc=(V if kc == KC - 1 else None),
                    label=(f"{pfx}.h.{i}" if kc == KC - 1 else None))

        dbg = cfg.get("dbg")
        if dbg is not None and i == 0:
            for kc in range(KC):
                w.op("gpsimd", lambda e, kc=kc: e.dma_start(
                    out=dbg["h"][:, kc * 512:(kc + 1) * 512],
                    in_=sb["h"][:, kc * 512:(kc + 1) * 512]),
                    inc="S1", k=16)

        w.wait("tensor", f"{pfx}.h.{i}")
        for q in range(NQ):
            jg = q * 2 if swiglu else q
            parcel_load(i, jg)
            if swiglu:
                parcel_load(i, jg + 1)
            slot_g = sb["wslot"][(i * NP + jg) % 4]
            slot_u = sb["wslot"][(i * NP + jg + 1) % 4] if swiglu else None
            w.wait("tensor", f"{pfx}.pld.{i}.{jg}")
            if swiglu:
                w.wait("tensor", f"{pfx}.pld.{i}.{jg+1}")
            for ii in range(8):
                gi = q * 8 + ii
                bank_g = ps[gi % 2]
                bank_u = ps[2 + gi % 2] if swiglu else None
                if gi >= 2:
                    w.wait("tensor", f"{pfx}.tgu.{i}.{gi-2}")
                elif i >= 1:
                    w.wait("tensor", f"{pfx}.pl.app.{i-1}")
                for k in range(KC):
                    w.op("tensor", lambda e, k=k, ii=ii, slot_g=slot_g, bank_g=bank_g: e.matmul(
                        bank_g[:, 0:512], slot_g[:, k * 1024 + ii * 128: k * 1024 + (ii + 1) * 128],
                        sb["h"][:, k * 512:(k + 1) * 512],
                        start=(k == 0), stop=(k == KC - 1)),
                        inc=(TQ if k == KC - 1 else None),
                        label=(f"{pfx}.g.mm.{i}.{gi}" if k == KC - 1 else None))
                if swiglu:
                    for k in range(KC):
                        w.op("tensor", lambda e, k=k, ii=ii, slot_u=slot_u, bank_u=bank_u: e.matmul(
                            bank_u[:, 0:512], slot_u[:, k * 1024 + ii * 128: k * 1024 + (ii + 1) * 128],
                            sb["h"][:, k * 512:(k + 1) * 512],
                            start=(k == 0), stop=(k == KC - 1)),
                            inc=(TQ if k == KC - 1 else None),
                            label=(f"{pfx}.u.mm.{i}.{gi}" if k == KC - 1 else None))
                if swiglu:
                    w.wait("scalar", f"{pfx}.g.mm.{i}.{gi}")
                    if gi >= 2:
                        w.wait("scalar", f"{pfx}.tgu.{i}.{gi-2}")
                    sgb = sb["sg0"] if gi % 2 == 0 else sb["sg1"]
                    w.op("scalar", lambda e, sgb=sgb, bank_g=bank_g: e.activation(
                        out=sgb[:, 0:512], in_=bank_g[:, 0:512], func=AF.Silu),
                        inc=C, label=f"{pfx}.sg.{i}.{gi}")
                    w.wait("vector", f"{pfx}.sg.{i}.{gi}")
                    w.wait("vector", f"{pfx}.u.mm.{i}.{gi}")
                    if gi <= 1 and i >= 1:
                        w.wait("vector", f"{pfx}.dq.{i-1}.3")
                    w.op("vector", lambda e, gi=gi, sgb=sgb, bank_u=bank_u: e.tensor_mul(
                        out=sb["tgu"][:, gi * 512:(gi + 1) * 512], in0=sgb[:, 0:512],
                        in1=bank_u[:, 0:512]),
                        inc=V, label=f"{pfx}.tgu.{i}.{gi}")
                else:
                    w.wait("scalar", f"{pfx}.g.mm.{i}.{gi}")
                    if gi <= 1 and i >= 1:
                        w.wait("scalar", f"{pfx}.dq.{i-1}.3")
                    w.op("scalar", lambda e, gi=gi, bank_g=bank_g: e.activation(
                        out=sb["tgu"][:, gi * 512:(gi + 1) * 512], in_=bank_g[:, 0:512],
                        func=AF.Silu, bias=sb["b1"][:, gi:gi + 1]),
                        inc=C, label=f"{pfx}.tgu.{i}.{gi}")

        if dbg is not None and i == 0:
            for b in range(4):
                w.op("gpsimd", lambda e, b=b: e.dma_start(
                    out=dbg["tgu"][:, b * 4096:(b + 1) * 4096],
                    in_=sb["tgu"][:, b * 4096:(b + 1) * 4096]),
                    inc="S1", k=16)

        tgudone = f"{pfx}.tgu.{i}.31"
        w.wait("tensor", tgudone)
        dnlab = f"{pfx}.dq.{i}.3"
        if i >= 1:
            w.wait("tensor", f"{pfx}.pl.app.{i-1}")
            w.wait("tensor", f"{pfx}.res.{i-1}.{KC-1}")
        for q in range(NQ):
            jd = (2 * NQ if swiglu else NQ) + q
            parcel_load(i, jd)
            slot_d = sb["wslot"][(i * NP + jd) % 4]
            w.wait("tensor", f"{pfx}.pld.{i}.{jd}")
            for ii in range(8):
                gi = q * 8 + ii
                for o in range(KC):
                    w.op("tensor", lambda e, ii=ii, o=o, gi=gi, slot_d=slot_d: e.matmul(
                        ps[o][:, 0:512], slot_d[:, ii * 1024 + o * 128: ii * 1024 + (o + 1) * 128],
                        sb["tgu"][:, gi * 512:(gi + 1) * 512],
                        start=(gi == 0), stop=(gi == 31)),
                        inc=(TQ if (ii == 7 and o == KC - 1) else None),
                        label=(f"{pfx}.dq.{i}.{q}" if (ii == 7 and o == KC - 1) else None))
        w.wait("vector", f"{pfx}.dq.{i}.3")
        w.wait("scalar", f"{pfx}.dq.{i}.3")
        for o in range(KC):
            if swiglu:
                w.op("vector", lambda e, o=o, xb=xb: e.tensor_add(
                    out=xb[:, o * 512:(o + 1) * 512], in0=xb[:, o * 512:(o + 1) * 512],
                    in1=ps[o][:, 0:512]),
                    inc=V, label=f"{pfx}.res.{i}.{o}")
            else:
                w.op("vector", lambda e, o=o: e.tensor_scalar(
                    out=sb["tmp2"][:, o * 512:(o + 1) * 512], in0=ps[o][:, 0:512],
                    scalar1=0.5, scalar2=sb["b2h"][:, o:o + 1],
                    op0=ALU.mult, op1=ALU.add),
                    inc=V, label=f"{pfx}.zh.{i}.{o}")
        if not swiglu:
            w.wait("vector", f"{pfx}.zh.{i}.{KC-1}")
            for o in range(KC):
                w.op("vector", lambda e, o=o, xb=xb: e.scalar_tensor_tensor(
                    out=xb[:, o * 512:(o + 1) * 512], in0=xb[:, o * 512:(o + 1) * 512],
                    scalar=2.0, in1=sb["tmp2"][:, o * 512:(o + 1) * 512],
                    op0=ALU.mult, op1=ALU.add),
                    inc=V, label=f"{pfx}.res.{i}.{o}")

        if dbg is not None and i == 0:
            for kc in range(KC):
                w.op("gpsimd", lambda e, kc=kc, xb=xb: e.dma_start(
                    out=dbg["xres"][:, kc * 512:(kc + 1) * 512],
                    in_=xb[:, kc * 512:(kc + 1) * 512]),
                    inc="S1", k=16)

        if xin0 is not None:
            w.wait("vector", f"{pfx}.x0.{i}.{KC-1}")
            w.wait("vector", f"{pfx}.res.{i}.{KC-1}")
            for o in range(KC):
                w.op("vector", lambda e, o=o, xb=xb: e.tensor_add(
                    out=xb[:, o * 512:(o + 1) * 512], in0=xb[:, o * 512:(o + 1) * 512],
                    in1=sb["x0s"][:, o * 512:(o + 1) * 512]),
                    inc=(V if o == KC - 1 else None),
                    label=(f"{pfx}.res2.{i}" if o == KC - 1 else None))
            resdone = f"{pfx}.res2.{i}"
        else:
            resdone = f"{pfx}.res.{i}.{KC-1}"

        w.wait("scalar", resdone)
        w.wait("tensor", resdone)
        postln_tile(w, pfx, i, sb, ps, cst, xb)

        w.wait("gpsimd", f"{pfx}.pl.app.{i}")
        for kc in range(KC):
            w.op("gpsimd", lambda e, kc=kc, t0=t0, xb=xb: e.dma_start(
                out=xout[kc * 128:(kc + 1) * 128, t0:t0 + TILE],
                in_=xb[:, kc * 512:(kc + 1) * 512]),
                inc=f"S{i % 2}", k=16, label=f"{pfx}.st.{i}.{kc}")

    w.wait_count("gpsimd", "S0", 16 * KC * (NTILE // 2))
    w.wait_count("gpsimd", "S1", 16 * KC * (NTILE // 2))


def build_p1_program():
    nc = bass.Bass("TRN2", target_bir_lowering=False)
    nc.ctx.enter_context(nc.allow_low_precision(
        reason="bf16 matmul operands with fp32 accumulation; intentional"))
    xin = nc.dram_tensor("xin", [D, NTOK], F32, kind="ExternalInput").ap()
    xout = nc.dram_tensor("xout", [D, NTOK], F32, kind="ExternalOutput").ap()
    wbf = {n: nc.dram_tensor(n, [D, D], BF16, kind="ExternalInput").ap()
           for n in ("wq", "wk", "wv", "wo")}
    smalls = {
        "attn_norm_w": nc.dram_tensor("attn_norm_w", [128, KC], F32, kind="ExternalInput").ap(),
        "out_norm_w": nc.dram_tensor("out_norm_w", [128, 1], F32, kind="ExternalInput").ap(),
    }
    consts = build_constants(nc)

    with ExitStack() as stack:
        E = stack.enter_context

        def sbuf(name, shape, dt):
            return E(nc.sbuf_tensor(name, shape, dt))[:, :]

        sb = {}
        sb["x0"] = sbuf("x0", [128, KC * 512], F32)
        sb["x1"] = sbuf("x1", [128, KC * 512], F32)
        for n in ("wq", "wk", "wv", "wo"):
            sb[n] = sbuf(n + "_sb", [128, KC * 1024], BF16)
        sb["h"] = sbuf("h_sb", [128, KC * 512], BF16)
        sb["xsq"] = sbuf("xsq_sb", [128, KC * 512], BF16)
        sb["q"] = sbuf("q_sb", [128, 8 * 512], BF16)
        sb["k"] = sbuf("k_sb", [128, 8 * 512], BF16)
        sb["kt"] = sbuf("kt_sb", [128, NCH * 1024], BF16)
        sb["vt"] = sbuf("vt_sb", [128, NCH * 1024], BF16)
        sb["o"] = sbuf("o_sb", [128, 8 * 512], F32)
        sb["on_bf"] = sbuf("on_sb", [128, 8 * 512], BF16)
        sb["at_bf"] = sbuf("at_sb", [128, 1024], BF16)
        sb["S_f32"] = sbuf("S_f32", [128, 8 * 128], F32)
        sb["S_bf"] = sbuf("S_bf", [128, 8 * 128], BF16)
        sb["nw1"] = sbuf("nw1", [128, KC], F32)
        sb["onw"] = sbuf("onw", [128, 1], F32)
        sb["row_a"] = sbuf("row_a", [1, 512], F32)
        sb["row_b"] = sbuf("row_b", [1, 512], F32)
        sb["row_bf"] = sbuf("row_bf", [1, 512], BF16)

        cst = {}
        cst["ones_col"] = sbuf("ones_col_sb", [128, 1], BF16)
        cst["ones_row"] = sbuf("ones_row_sb", [1, 128], BF16)
        cst["maskUT4"] = sbuf("maskUT4_sb", [128, 512], F32)

        ps = [E(nc.psum_tensor(f"psb{j}", [128, 512], F32))[:, :] for j in range(7)]

        semnames = ("Lw", "Lx0", "Lx1", "S0", "S1", "T", "V", "C")
        sems = {name: E(nc.semaphore(name)) for name in semnames}
        block = E(nc.Block())

        w = Walk(sems)
        w.TQ, w.V, w.C = "T", "V", "C"
        bufs = {"sb": sb, "ps": ps, "cst": cst}

        def full(walk):
            walk.op("sync", lambda e: e.dma_start(out=cst["ones_col"], in_=consts["ones_col_bf"]),
                    inc="Lw", k=16)
            walk.op("sync", lambda e: e.dma_start(out=cst["ones_row"], in_=consts["ones_row_bf"]),
                    inc="Lw", k=16)
            walk.op("sync", lambda e: e.dma_start(out=cst["maskUT4"], in_=consts["maskUT4_f32"]),
                    inc="Lw", k=16, label="consts")
            p1_emit(walk, nc, bufs, xin, xout, wbf, smalls)

        w.plan()
        full(w)

        @block.sync
        def _(q):
            w.start_emit("sync", q)
            full(w)

        @block.tensor
        def _(t):
            w.start_emit("tensor", t)
            full(w)

        @block.vector
        def _(v):
            w.start_emit("vector", v)
            full(w)

        @block.scalar
        def _(s):
            w.start_emit("scalar", s)
            full(w)

        @block.gpsimd
        def _(g):
            w.start_emit("gpsimd", g)
            full(w)

    return nc


def p4_emit(w, nc, bufs, xin, xout, wbf, smalls, pfx="p4"):
    sb = bufs["sb"]
    ps = bufs["ps"]
    cst = bufs["cst"]
    TQ, V, C = w.TQ, w.V, w.C

    for name in ("wq", "wk", "wv", "wo"):
        for k in range(KC):
            w.op("sync", lambda e, n=name, k=k: e.dma_start(
                out=sb[n][:, k * 1024:(k + 1) * 1024], in_=wbf[n][k * 128:(k + 1) * 128, :]),
                inc="Lw", k=16)
    for k in range(KC):
        w.op("sync", lambda e, k=k: e.dma_start(
            out=sb["wg"][:, k * 16:(k + 1) * 16], in_=smalls["wg"][k * 128:(k + 1) * 128, :]),
            inc="Lw", k=16)
    w.op("sync", lambda e: e.dma_start(out=sb["bg"][:, :], in_=smalls["bg"]),
         inc="Lw", k=16, label=f"{pfx}.w.done")
    w.wait("tensor", f"{pfx}.w.done")
    w.wait("vector", f"{pfx}.w.done")
    w.wait("scalar", f"{pfx}.w.done")

    for i in range(NTILE):
        seq_start = (i % TPS == 0)
        t0 = i * TILE
        xb = sb["x0"] if i % 2 == 0 else sb["x1"]

        if i >= 2:
            w.wait("sync", f"{pfx}.st.{i-2}.{KC-1}")
        for kc in range(KC):
            w.op("sync", lambda e, kc=kc, t0=t0, xb=xb: e.dma_start(
                out=xb[:, kc * 512:(kc + 1) * 512],
                in_=xin[kc * 128:(kc + 1) * 128, t0:t0 + TILE]),
                inc=f"Lx{i % 2}", k=16, label=f"{pfx}.x.{i}.{kc}")
        w.wait("scalar", f"{pfx}.x.{i}.{KC-1}")
        if i >= 1:
            w.wait("scalar", f"{pfx}.vt.mm.{i-1}.7")
        w.op("scalar", lambda e, xb=xb: e.copy(out=sb["xbf"][:], in_=xb[:]),
             inc=C, label=f"{pfx}.cast.{i}")

        w.wait("tensor", f"{pfx}.cast.{i}")
        if i >= 1:
            w.wait("tensor", f"{pfx}.res.{i-1}.{KC-2}")
            w.wait("tensor", f"{pfx}.res.{i-1}.{KC-1}")

        def proj_fm(dst, wname, evac_eng, tag, prev_tag):
            for m in range(KC):
                bank = ps[m % 2]
                if m >= 2:
                    w.wait("tensor", f"{pfx}.{tag}.ev.{i}.{m-2}")
                elif prev_tag is not None:
                    w.wait("tensor", f"{pfx}.{prev_tag}.ev.{i}.{6+m}")
                for k in range(KC):
                    w.op("tensor", lambda e, m=m, k=k, wname=wname, bank=bank: e.matmul(
                        bank[:, 0:512], sb[wname][:, k * 1024 + m * 128: k * 1024 + (m + 1) * 128],
                        sb["xbf"][:, k * 512:(k + 1) * 512],
                        start=(k == 0), stop=(k == KC - 1)),
                        inc=(TQ if k == KC - 1 else None),
                        label=(f"{pfx}.{tag}.mm.{i}.{m}" if k == KC - 1 else None))
                w.wait(evac_eng, f"{pfx}.{tag}.mm.{i}.{m}")
                semn = V if evac_eng == "vector" else C
                if evac_eng == "vector":
                    w.op("vector", lambda e, m=m, dst=dst, bank=bank: e.tensor_copy(
                        out=dst[:, m * 512:(m + 1) * 512], in_=bank[:, 0:512]),
                        inc=semn, label=f"{pfx}.{tag}.ev.{i}.{m}")
                else:
                    w.op("scalar", lambda e, m=m, dst=dst, bank=bank: e.copy(
                        out=dst[:, m * 512:(m + 1) * 512], in_=bank[:, 0:512]),
                        inc=semn, label=f"{pfx}.{tag}.ev.{i}.{m}")

        def proj_tm(dst, wname, evac_eng, tag, prev_tag):
            for c in range(NCH):
                for hg in range(2):
                    gi = c * 2 + hg
                    bank = ps[gi % 2]
                    if gi >= 2:
                        w.wait("tensor", f"{pfx}.{tag}.ev.{i}.{gi-2}")
                    elif prev_tag is not None:
                        w.wait("tensor", f"{pfx}.{prev_tag}.ev.{i}.{6+gi}")
                    for k in range(KC):
                        w.op("tensor", lambda e, c=c, hg=hg, k=k, wname=wname, bank=bank: e.matmul(
                            bank[:, 0:512],
                            sb["xbf"][:, k * 512 + c * 128: k * 512 + (c + 1) * 128],
                            sb[wname][:, k * 1024 + hg * 512: k * 1024 + (hg + 1) * 512],
                            start=(k == 0), stop=(k == KC - 1)),
                            inc=(TQ if k == KC - 1 else None),
                            label=(f"{pfx}.{tag}.mm.{i}.{gi}" if k == KC - 1 else None))
                    w.wait(evac_eng, f"{pfx}.{tag}.mm.{i}.{gi}")
                    semn = V if evac_eng == "vector" else C
                    dsl = dst[:, c * 1024 + hg * 512: c * 1024 + (hg + 1) * 512]
                    if evac_eng == "vector":
                        w.op("vector", lambda e, dsl=dsl, bank=bank: e.tensor_copy(
                            out=dsl, in_=bank[:, 0:512]),
                            inc=semn, label=f"{pfx}.{tag}.ev.{i}.{gi}")
                    else:
                        w.op("scalar", lambda e, dsl=dsl, bank=bank: e.copy(
                            out=dsl, in_=bank[:, 0:512]),
                            inc=semn, label=f"{pfx}.{tag}.ev.{i}.{gi}")

        proj_fm(sb["qr"], "wq", "scalar", "q", None)
        proj_fm(sb["kr"], "wk", "vector", "k", "q")
        proj_tm(sb["kt"], "wk", "scalar", "kt", "k")
        proj_tm(sb["vt"], "wv", "vector", "vt", "kt")

        prev_fm = None
        for tag, buf in (("q", "qr"), ("k", "kr"), ("kt", "kt")):
            w.wait("vector", f"{pfx}.{tag}.ev.{i}.7")
            if prev_fm is not None:
                w.wait("vector", f"{pfx}.fm2.{prev_fm}.{i}")
            elif i >= 1:
                w.wait("vector", f"{pfx}.fm2.kt.{i-1}")
                w.wait("vector", f"{pfx}.pl.ss.{i-1}")
            prev_fm = tag
            w.op("vector", lambda e, buf=buf: e.tensor_scalar_min(
                out=sb["fmin"][:], in0=sb[buf][:], scalar1=0.0),
                inc=V, label=f"{pfx}.fm1.{tag}.{i}")
            w.wait("scalar", f"{pfx}.fm1.{tag}.{i}")
            if prev_fm == "q" and i >= 1:
                w.wait("scalar", f"{pfx}.wo.{i-1}.{KC-1}")
            w.op("scalar", lambda e: e.activation(
                out=sb["fexp"][:], in_=sb["fmin"][:], func=AF.Exp),
                inc=C, label=f"{pfx}.fm2.{tag}.{i}")
            w.wait("vector", f"{pfx}.fm2.{tag}.{i}")
            w.op("vector", lambda e, buf=buf: e.scalar_tensor_tensor(
                out=sb[buf][:], in0=sb[buf][:], scalar=0.0, in1=sb["fexp"][:],
                op0=ALU.max, op1=ALU.add),
                inc=V, label=f"{pfx}.fm3.{tag}.{i}")

        w.wait("tensor", f"{pfx}.fm3.kt.{i}")
        w.wait("tensor", f"{pfx}.vt.ev.{i}.7")
        for c in range(NCH):
            cq = i * NCH + c
            b_g = ps[4 + c % 2]

            if cq >= 1:
                w.wait("tensor", f"{pfx}.Sadd.{cq-1}.{2 + c % 2}")
            w.wait("tensor", f"{pfx}.cast.{i}")
            for k in range(KC):
                w.op("tensor", lambda e, c=c, k=k, b_g=b_g, xb=xb: e.matmul(
                    b_g[:, 0:16], xb[:, k * 512 + c * 128: k * 512 + (c + 1) * 128],
                    sb["wg"][:, k * 16:(k + 1) * 16],
                    start=(k == 0), stop=False))
            w.op("tensor", lambda e, b_g=b_g: e.matmul(
                b_g[:, 0:16], cst["ones_row_f32"], sb["bg"][0:1, 0:16],
                start=False, stop=True),
                inc=TQ, label=f"{pfx}.gmm.{cq}")
            w.wait("scalar", f"{pfx}.gmm.{cq}")
            if cq >= 1:
                w.wait("scalar", f"{pfx}.LT.{cq-1}")
            w.op("scalar", lambda e, b_g=b_g: e.activation(
                out=sb["gate"][:, 0:16], in_=b_g[:, 0:16], func=AF.Sigmoid),
                inc=C, label=f"{pfx}.sig.{cq}")
            w.wait("scalar", f"{pfx}.sig.{cq}")
            w.op("scalar", lambda e: e.activation(
                out=sb["logg"][:, 0:16], in_=sb["gate"][:, 0:16], func=AF.Ln),
                inc=C, label=f"{pfx}.logg.{cq}")
            w.wait("tensor", f"{pfx}.logg.{cq}")
            if cq >= 1:
                w.wait("tensor", f"{pfx}.eLb.{cq-1}")
                w.wait("tensor", f"{pfx}.qt.{cq-1}.7")
                w.wait("tensor", f"{pfx}.Bsub.{cq-1}")
            w.op("tensor", lambda e, b_g=b_g: e.matmul(
                ps[6][:, 0:16], cst["cums_f32"], sb["logg"][:, 0:16], start=True, stop=True),
                inc=TQ, label=f"{pfx}.LTmm.{cq}")
            w.wait("vector", f"{pfx}.LTmm.{cq}")
            if cq >= 1:
                w.wait("vector", f"{pfx}.Dpre.{cq-1}.3")
                w.wait("vector", f"{pfx}.Bsub.{cq-1}")
            w.op("vector", lambda e: e.tensor_copy(out=sb["LT"][:, 0:16], in_=ps[6][:, 0:16]),
                 inc=V, label=f"{pfx}.LT.{cq}")
            w.op("vector", lambda e: e.tensor_scalar_mul(
                out=sb["NLT"][:, 0:16], in0=ps[6][:, 0:16], scalar1=-1.0),
                inc=V, label=f"{pfx}.NLT.{cq}")
            w.wait("tensor", f"{pfx}.LT.{cq}")
            if cq >= 1:
                w.wait("tensor", f"{pfx}.qt.{cq-1}.6")
                w.wait("tensor", f"{pfx}.Erow.{cq-1}")
                w.wait("tensor", f"{pfx}.Lrow.{cq-1}")
            w.op("tensor", lambda e: e.transpose(
                ps[7][0:16, 0:128], sb["LT"][:, 0:16], cst["ident_f32"]),
                inc=TQ, label=f"{pfx}.Lrowmm.{cq}")
            w.wait("vector", f"{pfx}.Lrowmm.{cq}")
            if cq >= 1:
                w.wait("vector", f"{pfx}.Amm.{cq-1}.3")
            w.op("vector", lambda e: e.tensor_copy(out=sb["Lrow"][0:16, 0:128], in_=ps[7][0:16, 0:128]),
                 inc=V, label=f"{pfx}.Lrow.{cq}")
            w.wait("scalar", f"{pfx}.Lrowmm.{cq}")
            if cq >= 1:
                w.wait("scalar", f"{pfx}.Ebcmm.{cq-1}.7")
            w.op("scalar", lambda e: e.activation(
                out=sb["Erow"][0:16, 0:128], in_=ps[7][0:16, 0:128], func=AF.Exp),
                inc=C, label=f"{pfx}.Erow.{cq}")
            w.wait("tensor", f"{pfx}.NLT.{cq}")
            w.op("tensor", lambda e: e.matmul(
                ps[6][:, 0:16], cst["row127_f32"], sb["LT"][:, 0:16], start=True, stop=True),
                inc=TQ, label=f"{pfx}.Bmm.{cq}")
            w.wait("vector", f"{pfx}.Bmm.{cq}")
            if cq >= 1:
                w.wait("vector", f"{pfx}.khat.{cq-1}.3")
            w.op("vector", lambda e: e.tensor_sub(
                out=sb["Bm"][:, 0:16], in0=ps[6][:, 0:16], in1=sb["LT"][:, 0:16]),
                inc=V, label=f"{pfx}.Bsub.{cq}")
            w.wait("scalar", f"{pfx}.Bsub.{cq}")
            if cq >= 1:
                w.wait("scalar", f"{pfx}.Sadd.{cq-1}.3")
            w.op("scalar", lambda e: e.activation(
                out=sb["call"][:, 0:16], in_=sb["Bm"][:, 0:16], func=AF.Exp),
                inc=C, label=f"{pfx}.call.{cq}")
            w.op("scalar", lambda e: e.activation(
                out=sb["eLb"][:, 0:16], in_=ps[6][:, 0:16], func=AF.Exp),
                inc=C, label=f"{pfx}.eLb.{cq}")

            for q4 in range(4):
                quad = cq * 4 + q4
                heads = range(4 * q4, 4 * q4 + 4)
                b_pt = ps[q4 % 2]
                b_ay = ps[2 + q4 % 2]
                b_sd = ps[4 + q4 % 2]
                Dq = sb["D0"] if q4 % 2 == 0 else sb["D1"]
                PDq = sb["PD0"] if q4 % 2 == 0 else sb["PD1"]
                t1q = sb["t10"] if q4 % 2 == 0 else sb["t11"]
                khq = sb["kh0"] if q4 % 2 == 0 else sb["kh1"]

                if quad >= 2:
                    w.wait("tensor", f"{pfx}.Yev.{quad-2}")
                w.wait("tensor", f"{pfx}.Lrow.{cq}")
                for hh, h in enumerate(heads):
                    w.op("tensor", lambda e, hh=hh, h=h, b_ay=b_ay: e.matmul(
                        b_ay[:, hh * 128:(hh + 1) * 128], cst[f"oh16f_{h}"],
                        sb["Lrow"][0:16, 0:128], start=True, stop=True),
                        inc=(TQ if hh == 3 else None),
                        label=(f"{pfx}.Amm.{cq}.{q4}" if hh == 3 else None))
                w.wait("vector", f"{pfx}.Amm.{cq}.{q4}")
                w.wait("vector", f"{pfx}.NLT.{cq}")
                if quad >= 2:
                    w.wait("vector", f"{pfx}.D.{quad-2}")
                dpre = sb["tmp2"][:, (q4 % 2) * 512:(q4 % 2) * 512 + 512]
                for hh, h in enumerate(heads):
                    w.op("vector", lambda e, hh=hh, h=h, b_ay=b_ay, dpre=dpre: e.tensor_scalar(
                        out=dpre[:, hh * 128:(hh + 1) * 128],
                        in0=b_ay[:, hh * 128:(hh + 1) * 128],
                        scalar1=sb["NLT"][:, h:h + 1], scalar2=0.0,
                        op0=ALU.add, op1=ALU.min),
                        inc=(V if hh == 3 else None),
                        label=(f"{pfx}.Dpre.{cq}.{q4}" if hh == 3 else None))
                w.wait("scalar", f"{pfx}.Dpre.{cq}.{q4}")
                if quad >= 2:
                    w.wait("scalar", f"{pfx}.PD.{quad-2}")
                w.op("scalar", lambda e, Dq=Dq, dpre=dpre: e.activation(
                    out=Dq[:], in_=dpre, func=AF.Exp),
                    inc=C, label=f"{pfx}.D.{quad}")

                if quad >= 2:
                    w.wait("tensor", f"{pfx}.PD.{quad-2}")
                elif i >= 1 and c == 0 and q4 < 2:
                    pass
                w.wait("tensor", f"{pfx}.fm3.k.{i}")
                w.wait("tensor", f"{pfx}.fm3.q.{i}")
                for hh, h in enumerate(heads):
                    base = (h % 2) * 64
                    mc = h // 2
                    qs = sb["qr"][base:base + 64, mc * 512 + c * 128: mc * 512 + (c + 1) * 128]
                    ks = sb["kr"][base:base + 64, mc * 512 + c * 128: mc * 512 + (c + 1) * 128]
                    w.op("tensor", lambda e, hh=hh, ks=ks, qs=qs, b_pt=b_pt: e.matmul(
                        b_pt[:, hh * 128:(hh + 1) * 128], ks, qs, start=True, stop=True),
                        inc=(TQ if hh == 3 else None),
                        label=(f"{pfx}.PT.{cq}.{q4}" if hh == 3 else None))
                w.wait("vector", f"{pfx}.PT.{cq}.{q4}")
                w.wait("vector", f"{pfx}.D.{quad}")
                if quad >= 2:
                    w.wait("vector", f"{pfx}.Yi.{quad-2}.3")
                w.op("vector", lambda e, b_pt=b_pt, t1q=t1q: e.tensor_mul(
                    out=t1q[:], in0=b_pt[:, 0:512], in1=cst["maskUT4"]),
                    inc=V, label=f"{pfx}.PDa.{quad}")
                w.wait("vector", f"{pfx}.PDa.{quad}")
                w.op("vector", lambda e, t1q=t1q, PDq=PDq, Dq=Dq: e.tensor_mul(
                    out=PDq[:], in0=t1q[:], in1=Dq[:]),
                    inc=V, label=f"{pfx}.PD.{quad}")

                pairs = (2 * q4, 2 * q4 + 1)
                for pp, pr in enumerate(pairs):
                    b_e = ps[7] if pr % 2 == 0 else ps[6]
                    if pr >= 2:
                        w.wait("tensor", f"{pfx}.qt.{cq}.{pr-2}")
                    elif pr == 0:
                        w.wait("tensor", f"{pfx}.Lrow.{cq}")
                        if cq >= 1:
                            w.wait("tensor", f"{pfx}.qt.{cq-1}.6")
                    else:
                        w.wait("tensor", f"{pfx}.Bsub.{cq}")
                        w.wait("tensor", f"{pfx}.eLb.{cq}")
                        if cq >= 1:
                            w.wait("tensor", f"{pfx}.qt.{cq-1}.7")
                    w.wait("tensor", f"{pfx}.Erow.{cq}")
                    w.op("tensor", lambda e, pr=pr, b_e=b_e: e.matmul(
                        b_e[:, 0:128], cst[f"prsel_{pr}"], sb["Erow"][0:16, 0:128],
                        start=True, stop=True),
                        inc=TQ, label=f"{pfx}.Ebcmm.{cq}.{pr}")
                    w.wait("vector", f"{pfx}.Ebcmm.{cq}.{pr}")
                    qtb = sb["qt0"] if pr % 2 == 0 else sb["qt1"]
                    if pr >= 2:
                        w.wait("vector", f"{pfx}.Ypair.{cq}.{pr-2}")
                    elif cq >= 1:
                        w.wait("vector", f"{pfx}.Ypair.{cq-1}.{6 + pr}")
                    mc = pr
                    w.op("vector", lambda e, mc=mc, c=c, b_e=b_e, qtb=qtb: e.tensor_mul(
                        out=qtb[:], in0=sb["qr"][:, mc * 512 + c * 128: mc * 512 + (c + 1) * 128],
                        in1=b_e[:, 0:128]),
                        inc=V, label=f"{pfx}.qt.{cq}.{pr}")

                w.wait("tensor", f"{pfx}.Dpre.{cq}.{q4}")
                w.wait("tensor", f"{pfx}.PD.{quad}")
                first = seq_start and c == 0
                if not first:
                    w.wait("tensor", f"{pfx}.Sbf.{cq-1}")
                for hh, h in enumerate(heads):
                    pr = h // 2
                    base = (h % 2) * 64
                    vslice = sb["vt"][:, c * 1024 + h * 64: c * 1024 + (h + 1) * 64]
                    ysl = b_ay[base:base + 64, (hh // 2) * 128:(hh // 2) * 128 + 128]
                    if first:
                        w.op("tensor", lambda e, vslice=vslice, PDq=PDq, hh=hh, ysl=ysl: e.matmul(
                            ysl, vslice, PDq[:, hh * 128:(hh + 1) * 128], start=True, stop=True),
                            inc=(TQ if hh == 3 else None),
                            label=(f"{pfx}.Yi.{quad}.{hh}" if hh == 3 else None))
                    else:
                        qtb = sb["qt0"] if pr % 2 == 0 else sb["qt1"]
                        w.wait("tensor", f"{pfx}.qt.{cq}.{pr}")
                        Ssl = sb["S_bf"][base:base + 64, (h // 2) * 64:(h // 2) * 64 + 64]
                        w.op("tensor", lambda e, vslice=vslice, PDq=PDq, hh=hh, ysl=ysl: e.matmul(
                            ysl, vslice, PDq[:, hh * 128:(hh + 1) * 128], start=True, stop=False))
                        w.op("tensor", lambda e, Ssl=Ssl, qtb=qtb, base=base, ysl=ysl: e.matmul(
                            ysl, Ssl, qtb[base:base + 64, :], start=False, stop=True),
                            inc=(TQ if hh == 3 else None),
                            label=(f"{pfx}.Yi.{quad}.{hh}" if hh == 3 else None))
                if w.mode == "plan":
                    w.labels[f"{pfx}.Ypair.{cq}.{2*q4}"] = w.labels[f"{pfx}.Yi.{quad}.3"]
                    w.labels[f"{pfx}.Ypair.{cq}.{2*q4+1}"] = w.labels[f"{pfx}.Yi.{quad}.3"]
                w.wait("scalar", f"{pfx}.Yi.{quad}.3")
                for pp, pr in enumerate(pairs):
                    w.op("scalar", lambda e, pp=pp, pr=pr, c=c, b_ay=b_ay: e.copy(
                        out=sb["o"][:, pr * 512 + c * 128: pr * 512 + (c + 1) * 128],
                        in_=b_ay[:, pp * 128:(pp + 1) * 128]),
                        inc=(C if pp == 1 else None),
                        label=(f"{pfx}.Yev.{quad}" if pp == 1 else None))

                w.wait("vector", f"{pfx}.call.{cq}")
                if quad >= 2:
                    w.wait("vector", f"{pfx}.Sd.{quad-2}")
                for hh, h in enumerate(heads):
                    w.op("vector", lambda e, hh=hh, h=h, c=c, khq=khq: e.tensor_scalar_mul(
                        out=khq[:, hh * 64:(hh + 1) * 64],
                        in0=sb["kt"][:, c * 1024 + h * 64: c * 1024 + (h + 1) * 64],
                        scalar1=sb["call"][:, h:h + 1]),
                        inc=(V if hh == 3 else None),
                        label=(f"{pfx}.khat.{cq}.{q4}" if hh == 3 else None))
                if quad >= 2:
                    w.wait("tensor", f"{pfx}.Sadd.{cq}.{q4-2}" if q4 >= 2 else f"{pfx}.Sadd.{cq-1}.{2+q4}")
                w.wait("tensor", f"{pfx}.khat.{cq}.{q4}")
                for hh, h in enumerate(heads):
                    base = (h % 2) * 64
                    vslice = sb["vt"][:, c * 1024 + h * 64: c * 1024 + (h + 1) * 64]
                    sdsl = b_sd[base:base + 64, (hh // 2) * 64:(hh // 2) * 64 + 64]
                    w.op("tensor", lambda e, hh=hh, khq=khq, vslice=vslice, sdsl=sdsl: e.matmul(
                        sdsl, khq[:, hh * 64:(hh + 1) * 64], vslice, start=True, stop=True),
                        inc=(TQ if hh == 3 else None),
                        label=(f"{pfx}.Sd.{quad}" if hh == 3 else None))
                w.wait("vector", f"{pfx}.Sd.{quad}")
                w.wait("vector", f"{pfx}.eLb.{cq}")
                if not (seq_start and c == 0):
                    w.wait("vector", f"{pfx}.Sadd.{cq-1}.{q4}")
                for hh, h in enumerate(heads):
                    base = (h % 2) * 64
                    Sfs = sb["S_f32"][base:base + 64, (h // 2) * 64:(h // 2) * 64 + 64]
                    sdsl = b_sd[base:base + 64, (hh // 2) * 64:(hh // 2) * 64 + 64]
                    if seq_start and c == 0:
                        w.op("vector", lambda e, Sfs=Sfs, sdsl=sdsl: e.tensor_copy(
                            out=Sfs, in_=sdsl),
                            inc=(V if hh == 3 else None),
                            label=(f"{pfx}.Sadd.{cq}.{q4}" if hh == 3 else None))
                    else:
                        w.op("vector", lambda e, Sfs=Sfs, sdsl=sdsl, h=h, base=base: e.scalar_tensor_tensor(
                            out=Sfs, in0=Sfs, scalar=sb["eLb"][base:base + 64, h:h + 1], in1=sdsl,
                            op0=ALU.mult, op1=ALU.add),
                            inc=(V if hh == 3 else None),
                            label=(f"{pfx}.Sadd.{cq}.{q4}" if hh == 3 else None))

            w.wait("scalar", f"{pfx}.Sadd.{cq}.3")
            w.wait("scalar", f"{pfx}.Sadd.{cq}.2")
            w.op("scalar", lambda e: e.copy(out=sb["S_bf"][:, 0:512], in_=sb["S_f32"][:, 0:512]),
                 inc=C, label=f"{pfx}.Sbf.{cq}")

        w.wait("tensor", f"{pfx}.Yev.{(i * NCH + NCH - 1) * 4 + 3}")
        for m in range(KC):
            bank = ps[m % 2]
            if m >= 2:
                w.wait("tensor", f"{pfx}.res.{i}.{m-2}")
            else:
                w.wait("tensor", f"{pfx}.PD.{(i * NCH + NCH - 1) * 4 + 2 + m}")
            for h in range(KC):
                w.op("tensor", lambda e, m=m, h=h, bank=bank: e.matmul(
                    bank[:, 0:512], sb["wo"][:, h * 1024 + m * 128: h * 1024 + (m + 1) * 128],
                    sb["o"][:, h * 512:(h + 1) * 512],
                    start=(h == 0), stop=(h == KC - 1)),
                    inc=(TQ if h == KC - 1 else None),
                    label=(f"{pfx}.wo.{i}.{m}" if h == KC - 1 else None))
            w.wait("vector", f"{pfx}.wo.{i}.{m}")
            w.op("vector", lambda e, m=m, xb=xb, bank=bank: e.tensor_add(
                out=xb[:, m * 512:(m + 1) * 512], in0=xb[:, m * 512:(m + 1) * 512],
                in1=bank[:, 0:512]),
                inc=V, label=f"{pfx}.res.{i}.{m}")

        w.wait("scalar", f"{pfx}.res.{i}.{KC-1}")
        w.wait("tensor", f"{pfx}.res.{i}.{KC-1}")
        postln_tile(w, pfx, i, sb, ps, cst, xb)
        w.wait("gpsimd", f"{pfx}.pl.app.{i}")
        for kc in range(KC):
            w.op("gpsimd", lambda e, kc=kc, t0=t0, xb=xb: e.dma_start(
                out=xout[kc * 128:(kc + 1) * 128, t0:t0 + TILE],
                in_=xb[:, kc * 512:(kc + 1) * 512]),
                inc=f"S{i % 2}", k=16, label=f"{pfx}.st.{i}.{kc}")

    w.wait_count("gpsimd", "S0", 16 * KC * (NTILE // 2))
    w.wait_count("gpsimd", "S1", 16 * KC * (NTILE // 2))


def p5_emit(w, nc, bufs, xin, xout, wbf, smalls, pfx="p5"):
    sb = bufs["sb"]
    ps = bufs["ps"]
    cst = bufs["cst"]
    TQ, V, C = w.TQ, w.V, w.C

    for e in range(8):
        for k in range(KC):
            w.op("sync", lambda e_, e2=e, k=k: e_.dma_start(
                out=sb["we"][:, (e2 * 8 + k) * 1024:(e2 * 8 + k + 1) * 1024],
                in_=wbf["we"][e2 * 1024 + k * 128: e2 * 1024 + (k + 1) * 128, :]),
                inc="Lw", k=16)
    for k in range(KC):
        w.op("sync", lambda e_, k=k: e_.dma_start(
            out=sb["wr"][:, k * 8:(k + 1) * 8], in_=wbf["wr"][k * 128:(k + 1) * 128, :]),
            inc="Lw", k=16)
    w.op("sync", lambda e_: e_.dma_start(out=sb["rb_f"][:, :], in_=smalls["rb"]),
         inc="Lw", k=16)
    w.op("sync", lambda e_: e_.dma_start(out=sb["eb_f"][:, :], in_=smalls["eb"]),
         inc="Lw", k=16, label=f"{pfx}.w.done")
    w.wait("scalar", f"{pfx}.w.done")
    w.op("scalar", lambda e_: e_.copy(out=sb["eb_bf"][:, :], in_=sb["eb_f"][:, :]),
         inc=C, label=f"{pfx}.ebcast")

    for i in range(NTILE):
        t0 = i * TILE
        xb = sb["x0"]

        if i >= 1:
            w.wait("sync", f"{pfx}.st.{i-1}.{KC-1}")
        for kc in range(KC):
            w.op("sync", lambda e_, kc=kc, t0=t0, xb=xb: e_.dma_start(
                out=xb[:, kc * 512:(kc + 1) * 512],
                in_=xin[kc * 128:(kc + 1) * 128, t0:t0 + TILE]),
                inc=f"Lx{i % 2}", k=16, label=f"{pfx}.x.{i}.{kc}")

        w.wait("scalar", f"{pfx}.x.{i}.{KC-1}")
        if i >= 1:
            w.wait("scalar", f"{pfx}.ymm.{i-1}")
        w.op("scalar", lambda e_, xb=xb: e_.copy(out=sb["xbf"][:], in_=xb[:]),
             inc=C, label=f"{pfx}.cast.{i}")

        w.wait("tensor", f"{pfx}.cast.{i}")
        w.wait("tensor", f"{pfx}.w.done")
        if i >= 1:
            w.wait("tensor", f"{pfx}.pl.app.{i-1}")
        for c in range(NCH):
            b_r = ps[4 + c % 2]
            if c >= 2:
                w.wait("tensor", f"{pfx}.m00.{i}.{c-2}")
            for k in range(KC):
                w.op("tensor", lambda e_, c=c, k=k, b_r=b_r, xb=xb: e_.matmul(
                    b_r[:, 0:8], xb[:, k * 512 + c * 128: k * 512 + (c + 1) * 128],
                    sb["wr"][:, k * 8:(k + 1) * 8],
                    start=(k == 0), stop=False))
            w.op("tensor", lambda e_, c=c, b_r=b_r: e_.matmul(
                b_r[:, 0:8], cst["ones_row_f32"], sb["rb_f"][0:1, 0:8],
                start=False, stop=True),
                inc=TQ, label=f"{pfx}.rmm.{i}.{c}")
            w.wait("vector", f"{pfx}.rmm.{i}.{c}")
            if c >= 1:
                w.wait("vector", f"{pfx}.eexp.{i}.{c-1}")
                w.wait("vector", f"{pfx}.mtp.{i}.{c-1}")
            w.op("vector", lambda e_, b_r=b_r: e_.tensor_reduce(
                out=sb["nmx"][:, 0:1], in_=b_r[:, 0:8], axis=AX.X, op=ALU.max, negate=True),
                inc=V, label=f"{pfx}.nmx.{i}.{c}")
            w.wait("scalar", f"{pfx}.nmx.{i}.{c}")
            w.op("scalar", lambda e_, b_r=b_r: e_.activation(
                out=sb["eexp"][:, 0:8], in_=b_r[:, 0:8], func=AF.Exp, bias=sb["nmx"][:, 0:1]),
                inc=C, label=f"{pfx}.eexp.{i}.{c}")
            w.wait("vector", f"{pfx}.eexp.{i}.{c}")
            w.op("vector", lambda e_: e_.tensor_reduce(
                out=sb["srow"][:, 0:1], in_=sb["eexp"][:, 0:8], axis=AX.X, op=ALU.add),
                inc=V, label=f"{pfx}.srow.{i}.{c}")
            w.wait("vector", f"{pfx}.srow.{i}.{c}")
            w.op("vector", lambda e_: e_.reciprocal(
                out=sb["wrow"][:, 0:1], in_=sb["srow"][:, 0:1]),
                inc=V, label=f"{pfx}.wrec.{i}.{c}")
            w.wait("vector", f"{pfx}.wrec.{i}.{c}")
            w.op("vector", lambda e_, b_r=b_r: e_.tensor_scalar(
                out=sb["m0"][:, 0:8], in0=b_r[:, 0:8], scalar1=sb["nmx"][:, 0:1], scalar2=0.0,
                op0=ALU.add, op1=ALU.is_equal),
                inc=V, label=f"{pfx}.m00.{i}.{c}")
            w.wait("vector", f"{pfx}.m00.{i}.{c}")
            w.op("vector", lambda e_: e_.tensor_scalar(
                out=sb["m1"][:, 0:8], in0=sb["m0"][:, 0:8], scalar1=sb["wrow"][:, 0:1],
                scalar2=None, op0=ALU.mult),
                inc=V, label=f"{pfx}.m0.{i}.{c}")
            w.wait("tensor", f"{pfx}.m0.{i}.{c}")
            w.op("tensor", lambda e_, c=c: e_.transpose(
                ps[7][0:8, c * 128:(c + 1) * 128], sb["m1"][:, 0:8], cst["ident_f32"]),
                inc=TQ, label=f"{pfx}.mtp.{i}.{c}")
        w.wait("vector", f"{pfx}.mtp.{i}.{NCH-1}")
        w.op("vector", lambda e_: e_.tensor_copy(out=sb["mT"][0:8, 0:512], in_=ps[7][0:8, 0:512]),
             inc=V, label=f"{pfx}.mt.{i}")

        w.wait("tensor", f"{pfx}.mt.{i}")
        for e in range(8):
            b_m = ps[6] if e % 2 == 0 else ps[4]
            if e >= 2:
                w.wait("tensor", f"{pfx}.mbc.ev.{i}.{e-2}")
            w.op("tensor", lambda e_, e2=e, b_m=b_m: e_.matmul(
                b_m[:, 0:512], cst[f"oh8_{e2}"], sb["mT"][0:8, 0:512], start=True, stop=True),
                inc=TQ, label=f"{pfx}.mbc.{i}.{e}")
            w.wait("scalar", f"{pfx}.mbc.{i}.{e}")
            if i >= 1:
                w.wait("scalar", f"{pfx}.xm.{i-1}.{e}")
            w.op("scalar", lambda e_, e2=e, b_m=b_m: e_.copy(
                out=sb["maskbc"][:, e2 * 512:(e2 + 1) * 512], in_=b_m[:, 0:512]),
                inc=C, label=f"{pfx}.mbc.ev.{i}.{e}")

        w.wait("tensor", f"{pfx}.mbc.ev.{i}.6")
        w.wait("tensor", f"{pfx}.mbc.ev.{i}.7")
        w.wait("tensor", f"{pfx}.ebcast")
        if i >= 1:
            w.wait("tensor", f"{pfx}.res.{i-1}.{KC-1}")
        for o in range(KC):
            w.op("tensor", lambda e_, o=o: e_.matmul(
                ps[o][:, 0:512], sb["eb_bf"][0:8, o * 128:(o + 1) * 128], sb["mT"][0:8, 0:512],
                start=True, stop=False))
        for e in range(8):
            xm = sb["xm0"] if e % 2 == 0 else sb["xm1"]
            w.wait("vector", f"{pfx}.mbc.ev.{i}.{e}")
            if e >= 2:
                w.wait("vector", f"{pfx}.emm.{i}.{e-2}")
            elif i >= 1:
                w.wait("vector", f"{pfx}.emm.{i-1}.{6+e}")
                if e == 1:
                    w.wait("vector", f"{pfx}.pl.ss.{i-1}")
            for kc in range(KC):
                w.op("vector", lambda e_, e2=e, kc=kc, xm=xm: e_.tensor_mul(
                    out=xm[:, kc * 512:(kc + 1) * 512], in0=sb["xbf"][:, kc * 512:(kc + 1) * 512],
                    in1=sb["maskbc"][:, e2 * 512:(e2 + 1) * 512]),
                    inc=(V if kc == KC - 1 else None),
                    label=(f"{pfx}.xm.{i}.{e}" if kc == KC - 1 else None))
            w.wait("tensor", f"{pfx}.xm.{i}.{e}")
            for o in range(KC):
                for k in range(KC):
                    w.op("tensor", lambda e_, e2=e, o=o, k=k, xm=xm: e_.matmul(
                        ps[o][:, 0:512],
                        sb["we"][:, (e2 * 8 + k) * 1024 + o * 128: (e2 * 8 + k) * 1024 + (o + 1) * 128],
                        xm[:, k * 512:(k + 1) * 512],
                        start=False, stop=(e2 == 7 and k == KC - 1)),
                        inc=(TQ if (o == KC - 1 and k == KC - 1) else None),
                        label=(f"{pfx}.emm.{i}.{e}" if (o == KC - 1 and k == KC - 1) else None))
        if w.mode == "plan":
            w.labels[f"{pfx}.ymm.{i}"] = w.labels[f"{pfx}.emm.{i}.7"]

        w.wait("vector", f"{pfx}.emm.{i}.7")
        for o in range(KC):
            w.op("vector", lambda e_, o=o, xb=xb: e_.tensor_add(
                out=xb[:, o * 512:(o + 1) * 512], in0=xb[:, o * 512:(o + 1) * 512],
                in1=ps[o][:, 0:512]),
                inc=V, label=f"{pfx}.res.{i}.{o}")
        w.wait("scalar", f"{pfx}.res.{i}.{KC-1}")
        w.wait("tensor", f"{pfx}.res.{i}.{KC-1}")
        postln_tile(w, pfx, i, sb, ps, cst, xb)
        w.wait("gpsimd", f"{pfx}.pl.app.{i}")
        for kc in range(KC):
            w.op("gpsimd", lambda e_, kc=kc, t0=t0, xb=xb: e_.dma_start(
                out=xout[kc * 128:(kc + 1) * 128, t0:t0 + TILE],
                in_=xb[:, kc * 512:(kc + 1) * 512]),
                inc=f"S{i % 2}", k=16, label=f"{pfx}.st.{i}.{kc}")

    w.wait_count("gpsimd", "S0", 16 * KC * (NTILE // 2))
    w.wait_count("gpsimd", "S1", 16 * KC * (NTILE // 2))


def build_mlp_program(kind, debug=False):
    swiglu = kind == "swiglu"
    nc = bass.Bass("TRN2", target_bir_lowering=False)
    nc.ctx.enter_context(nc.allow_low_precision(
        reason="bf16 matmul operands with fp32 accumulation; intentional"))
    xin = nc.dram_tensor("xin", [D, NTOK], F32, kind="ExternalInput").ap()
    xin0 = (nc.dram_tensor("xin0", [D, NTOK], F32, kind="ExternalInput").ap()
            if swiglu else None)
    xout = nc.dram_tensor("xout", [D, NTOK], F32, kind="ExternalOutput").ap()
    wbf = {}
    wbf["wg"] = nc.dram_tensor("wg", [D, 4096], BF16, kind="ExternalInput").ap()
    if swiglu:
        wbf["wu"] = nc.dram_tensor("wu", [D, 4096], BF16, kind="ExternalInput").ap()
    wbf["wd"] = nc.dram_tensor("wd", [4096, D], BF16, kind="ExternalInput").ap()
    smalls = {}
    if swiglu:
        smalls["nw"] = nc.dram_tensor("nw", [128, KC], F32, kind="ExternalInput").ap()
    else:
        smalls["gamma"] = nc.dram_tensor("gamma", [128, KC], F32, kind="ExternalInput").ap()
        smalls["beta"] = nc.dram_tensor("beta", [128, KC], F32, kind="ExternalInput").ap()
        smalls["b1"] = nc.dram_tensor("b1", [128, 32], F32, kind="ExternalInput").ap()
        smalls["b2"] = nc.dram_tensor("b2", [128, KC], F32, kind="ExternalInput").ap()
    consts = build_constants(nc)

    with ExitStack() as stack:
        E = stack.enter_context

        def sbuf(name, shape, dt):
            return E(nc.sbuf_tensor(name, shape, dt))[:, :]

        sb = {}
        sb["x0"] = sbuf("x0", [128, KC * 512], F32)
        sb["x1"] = sbuf("x1", [128, KC * 512], F32)
        if swiglu:
            sb["x0s"] = sbuf("x0s", [128, KC * 512], F32)
        sb["h"] = sbuf("h_sb", [128, KC * 512], BF16)
        sb["xsq"] = sbuf("xsq_sb", [128, KC * 512], BF16)
        sb["xbf"] = sbuf("xbf_sb", [128, KC * 512], BF16)
        sb["tgu"] = sbuf("tgu_sb", [128, 32 * 512], BF16)
        sb["wslot"] = [sbuf(f"wslot{j}", [128, 8192], BF16) for j in range(4)]
        sb["sg0"] = sbuf("sg0", [128, 512], BF16)
        sb["sg1"] = sbuf("sg1", [128, 512], BF16)
        sb["tmp0"] = sbuf("tmp0", [128, 512], F32)
        sb["tmp1"] = sbuf("tmp1", [128, 512], F32)
        sb["tmp2"] = sbuf("tmp2", [128, KC * 512], F32)
        for n in ("row_a", "row_b", "row_c", "row_mu"):
            sb[n] = sbuf(n, [1, 512], F32)
        sb["row_bf"] = sbuf("row_bf", [1, 512], BF16)
        sb["row_mu_bf"] = sbuf("row_mu_bf", [1, 512], BF16)
        if swiglu:
            sb["nw"] = sbuf("nw_sb", [128, KC], F32)
        else:
            sb["gamma"] = sbuf("gamma_sb", [128, KC], F32)
            sb["beta"] = sbuf("beta_sb", [128, KC], F32)
            sb["b1"] = sbuf("b1_sb", [128, 32], F32)
            sb["b2"] = sbuf("b2_sb", [128, KC], F32)
            sb["b2h"] = sbuf("b2h_sb", [128, KC], F32)

        cst = {}
        cst["ones_col"] = sbuf("ones_col_sb", [128, 1], BF16)
        cst["ones_row"] = sbuf("ones_row_sb", [1, 128], BF16)
        cst["eps11"] = sbuf("eps11_sb", [1, 1], F32)

        ps = [E(nc.psum_tensor(f"psb{j}", [128, 512], F32))[:, :] for j in range(8)]

        semnames = ("Lw", "Lx0", "Lx1", "Lx2", "Lp0", "Lp1", "Lp2", "Lp3", "S0", "S1", "T", "V", "C")
        sems = {name: E(nc.semaphore(name)) for name in semnames}
        block = E(nc.Block())

        w = Walk(sems, serial=True)
        w.TQ, w.V, w.C = "T", "V", "C"
        bufs = {"sb": sb, "ps": ps, "cst": cst}
        dbg = None
        if debug:
            dbg = {
                "h": nc.dram_tensor("dbg_h", [128, KC * 512], BF16, kind="ExternalOutput").ap(),
                "tgu": nc.dram_tensor("dbg_tgu", [128, 32 * 512], BF16, kind="ExternalOutput").ap(),
                "xres": nc.dram_tensor("dbg_xres", [128, KC * 512], F32, kind="ExternalOutput").ap(),
            }
        cfg = {"kind": kind, "pre": "rms" if swiglu else "ln", "dbg": dbg}

        def full(walk):
            walk.op("sync", lambda e: e.dma_start(out=cst["ones_col"], in_=consts["ones_col_bf"]),
                    inc="Lw", k=16)
            walk.op("sync", lambda e: e.dma_start(out=cst["ones_row"], in_=consts["ones_row_bf"]),
                    inc="Lw", k=16)
            walk.op("sync", lambda e: e.dma_start(out=cst["eps11"], in_=consts["eps11"]),
                    inc="Lw", k=16, label="consts")
            mlp_emit(walk, nc, bufs, xin, xout, wbf, smalls, cfg, "m", xin0=xin0)
            walk.final_waits("gpsimd")

        w.plan()
        full(w)
        w.totals = dict(w.counts)

        @block.sync
        def _(q):
            w.start_emit("sync", q)
            full(w)

        @block.tensor
        def _(t):
            w.start_emit("tensor", t)
            full(w)

        @block.vector
        def _(v):
            w.start_emit("vector", v)
            full(w)

        @block.scalar
        def _(s):
            w.start_emit("scalar", s)
            full(w)

        @block.gpsimd
        def _(g):
            w.start_emit("gpsimd", g)
            full(w)

    return nc


def build_p4_program():
    nc = bass.Bass("TRN2", target_bir_lowering=False)
    nc.ctx.enter_context(nc.allow_low_precision(
        reason="bf16 matmul operands with fp32 accumulation; intentional"))
    xin = nc.dram_tensor("xin", [D, NTOK], F32, kind="ExternalInput").ap()
    xout = nc.dram_tensor("xout", [D, NTOK], F32, kind="ExternalOutput").ap()
    wbf = {n: nc.dram_tensor(n, [D, D], BF16, kind="ExternalInput").ap()
           for n in ("wq", "wk", "wv", "wo")}
    smalls = {
        "wg": nc.dram_tensor("wg", [D, 16], F32, kind="ExternalInput").ap(),
        "bg": nc.dram_tensor("bg", [1, 16], F32, kind="ExternalInput").ap(),
    }
    consts = build_constants(nc)

    with ExitStack() as stack:
        E = stack.enter_context

        def sbuf(name, shape, dt):
            return E(nc.sbuf_tensor(name, shape, dt))[:, :]

        sb = {}
        sb["x0"] = sbuf("x0", [128, KC * 512], F32)
        sb["x1"] = sbuf("x1", [128, KC * 512], F32)
        sb["xbf"] = sbuf("xbf_sb", [128, KC * 512], BF16)
        for n in ("wq", "wk", "wv", "wo"):
            sb[n] = sbuf(n + "_sb", [128, KC * 1024], BF16)
        sb["wg"] = sbuf("wg_sb", [128, KC * 16], F32)
        sb["bg"] = sbuf("bg_sb", [1, 16], F32)
        for n in ("qr", "kr", "kt", "vt", "fmin", "fexp"):
            sb[n] = sbuf(n + "_sb", [128, KC * 512], BF16)
        sb["o"] = sb["fexp"]
        sb["S_f32"] = sbuf("S_f32", [128, 512], F32)
        sb["S_bf"] = sbuf("S_bf", [128, 512], BF16)
        for n in ("gate", "logg", "LT", "NLT", "Bm", "call", "eLb"):
            sb[n] = sbuf(n + "_sb", [128, 16], F32)
        sb["Lrow"] = sbuf("Lrow_sb", [16, 128], F32)
        sb["Erow"] = sbuf("Erow_sb", [16, 128], BF16)
        for n in ("D0", "D1", "PD0", "PD1", "t10", "t11"):
            sb[n] = sbuf(n + "_sb", [128, 512], BF16)
        sb["kh0"] = sbuf("kh0_sb", [128, 256], BF16)
        sb["kh1"] = sbuf("kh1_sb", [128, 256], BF16)
        sb["qt0"] = sbuf("qt0_sb", [128, 128], BF16)
        sb["qt1"] = sbuf("qt1_sb", [128, 128], BF16)
        sb["tmp2"] = sbuf("tmp2", [128, KC * 512], F32)
        sb["xsq"] = sb["fmin"]
        sb["row_a"] = sb["tmp2"][0:1, 1536:2048]
        sb["row_b"] = sb["tmp2"][0:1, 2048:2560]
        sb["row_c"] = sb["tmp2"][0:1, 2560:3072]
        sb["row_mu"] = sb["tmp2"][0:1, 3072:3584]
        sb["row_bf"] = sbuf("row_bf", [1, 512], BF16)
        sb["row_mu_bf"] = sbuf("row_mu_bf", [1, 512], BF16)

        cst = {}
        cst["ones_col"] = sbuf("ones_col_sb", [128, 1], BF16)
        cst["ones_row"] = sbuf("ones_row_sb", [1, 128], BF16)
        cst["ones_row_f32"] = sbuf("ones_row_f_sb", [1, 128], F32)
        cst["eps11"] = sbuf("eps11_sb", [1, 1], F32)
        cst["ident_f32"] = sbuf("identf_sb", [128, 128], F32)
        cst["cums_f32"] = sbuf("cums_sb", [128, 128], F32)
        cst["row127_f32"] = sbuf("row127_sb", [128, 128], F32)
        cst["maskUT4"] = sbuf("maskUT4_sb", [128, 512], F32)
        for h in range(16):
            cst[f"oh16f_{h}"] = sbuf(f"oh16f_{h}_sb", [16, 128], F32)
        for p in range(8):
            cst[f"prsel_{p}"] = sbuf(f"prsel_{p}_sb", [16, 128], BF16)

        ps = [E(nc.psum_tensor(f"psb{j}", [128, 512], F32))[:, :] for j in range(8)]

        semnames = ("Lw", "Lx0", "Lx1", "S0", "S1", "T", "V", "C")
        sems = {name: E(nc.semaphore(name)) for name in semnames}
        block = E(nc.Block())

        w = Walk(sems, serial=True)
        w.TQ, w.V, w.C = "T", "V", "C"
        bufs = {"sb": sb, "ps": ps, "cst": cst}

        def full(walk):
            for cn, src_ap in (
                ("ones_col", consts["ones_col_bf"]), ("ones_row", consts["ones_row_bf"]),
                ("ones_row_f32", consts["ones_row_f32"]), ("eps11", consts["eps11"]),
                ("ident_f32", consts["ident_f32"]), ("cums_f32", consts["cums_f32"]),
                ("row127_f32", consts["row127_f32"]), ("maskUT4", consts["maskUT4_f32"]),
            ):
                walk.op("sync", lambda e, cn=cn, src_ap=src_ap: e.dma_start(
                    out=cst[cn], in_=src_ap), inc="Lw", k=16)
            for h in range(16):
                walk.op("sync", lambda e, h=h: e.dma_start(
                    out=cst[f"oh16f_{h}"], in_=consts["oh16f"][h]), inc="Lw", k=16)
            for p in range(8):
                walk.op("sync", lambda e, p=p: e.dma_start(
                    out=cst[f"prsel_{p}"], in_=consts["prsel"][p]), inc="Lw", k=16)
            p4_emit(walk, nc, bufs, xin, xout, wbf, smalls)
            walk.final_waits("gpsimd")

        w.plan()
        full(w)
        w.totals = dict(w.counts)

        @block.sync
        def _(q):
            w.start_emit("sync", q)
            full(w)

        @block.tensor
        def _(t):
            w.start_emit("tensor", t)
            full(w)

        @block.vector
        def _(v):
            w.start_emit("vector", v)
            full(w)

        @block.scalar
        def _(s):
            w.start_emit("scalar", s)
            full(w)

        @block.gpsimd
        def _(g):
            w.start_emit("gpsimd", g)
            full(w)

    return nc


def build_p5_program():
    nc = bass.Bass("TRN2", target_bir_lowering=False)
    nc.ctx.enter_context(nc.allow_low_precision(
        reason="bf16 matmul operands with fp32 accumulation; intentional"))
    xin = nc.dram_tensor("xin", [D, NTOK], F32, kind="ExternalInput").ap()
    xout = nc.dram_tensor("xout", [D, NTOK], F32, kind="ExternalOutput").ap()
    wbf = {
        "we": nc.dram_tensor("we", [8 * D, D], BF16, kind="ExternalInput").ap(),
        "wr": nc.dram_tensor("wr", [D, 8], F32, kind="ExternalInput").ap(),
    }
    smalls = {
        "rb": nc.dram_tensor("rb", [1, 8], F32, kind="ExternalInput").ap(),
        "eb": nc.dram_tensor("eb", [8, D], F32, kind="ExternalInput").ap(),
    }
    consts = build_constants(nc)

    with ExitStack() as stack:
        E = stack.enter_context

        def sbuf(name, shape, dt):
            return E(nc.sbuf_tensor(name, shape, dt))[:, :]

        sb = {}
        sb["x0"] = sbuf("x0", [128, KC * 512], F32)
        sb["x1"] = sb["x0"]
        sb["xbf"] = sbuf("xbf_sb", [128, KC * 512], BF16)
        sb["we"] = sbuf("we_sb", [128, 64 * 1024], BF16)
        sb["wr"] = sbuf("wr_sb", [128, 64], F32)
        sb["rb_f"] = sbuf("rb_f", [1, 8], F32)
        sb["eb_bf"] = sbuf("eb_bf", [8, D], BF16)
        sb["nmx"] = sbuf("nmx", [128, 1], F32)
        sb["srow"] = sbuf("srow", [128, 1], F32)
        sb["wrow"] = sbuf("wrow", [128, 1], F32)
        sb["eexp"] = sbuf("eexp", [128, 8], F32)
        sb["m0"] = sbuf("m0", [128, 8], F32)
        sb["m1"] = sbuf("m1", [128, 8], F32)
        sb["mT"] = sbuf("mT", [8, 512], BF16)
        sb["maskbc"] = sbuf("maskbc", [128, 8 * 512], BF16)
        sb["xm0"] = sbuf("xm0", [128, KC * 512], BF16)
        sb["xm1"] = sbuf("xm1", [128, KC * 512], BF16)
        sb["xsq"] = sb["xm1"]
        sb["tmp2"] = sbuf("tmp2", [128, KC * 512], F32)
        sb["eb_f"] = sb["tmp2"][0:8, 0:1024]
        sb["row_a"] = sb["tmp2"][0:1, 1536:2048]
        sb["row_b"] = sb["tmp2"][0:1, 2048:2560]
        sb["row_c"] = sb["tmp2"][0:1, 2560:3072]
        sb["row_mu"] = sb["tmp2"][0:1, 3072:3584]
        sb["row_bf"] = sbuf("row_bf", [1, 512], BF16)
        sb["row_mu_bf"] = sbuf("row_mu_bf", [1, 512], BF16)

        cst = {}
        cst["ones_col"] = sbuf("ones_col_sb", [128, 1], BF16)
        cst["ones_row"] = sbuf("ones_row_sb", [1, 128], BF16)
        cst["eps11"] = sbuf("eps11_sb", [1, 1], F32)
        cst["ident_f32"] = sbuf("identf_sb", [128, 128], F32)
        cst["ones_row_f32"] = sbuf("ones_row_f_sb", [1, 128], F32)
        for e in range(8):
            cst[f"oh8_{e}"] = sbuf(f"oh8_{e}_sb", [8, 128], BF16)

        ps = [E(nc.psum_tensor(f"psb{j}", [128, 512], F32))[:, :] for j in range(8)]

        semnames = ("Lw", "Lx0", "Lx1", "S0", "S1", "T", "V", "C")
        sems = {name: E(nc.semaphore(name)) for name in semnames}
        block = E(nc.Block())

        w = Walk(sems, serial=True)
        w.TQ, w.V, w.C = "T", "V", "C"
        bufs = {"sb": sb, "ps": ps, "cst": cst}

        def full(walk):
            walk.op("sync", lambda e: e.dma_start(out=cst["ones_col"], in_=consts["ones_col_bf"]),
                    inc="Lw", k=16)
            walk.op("sync", lambda e: e.dma_start(out=cst["ones_row"], in_=consts["ones_row_bf"]),
                    inc="Lw", k=16)
            walk.op("sync", lambda e: e.dma_start(out=cst["eps11"], in_=consts["eps11"]),
                    inc="Lw", k=16)
            walk.op("sync", lambda e: e.dma_start(out=cst["ident_f32"], in_=consts["ident_f32"]),
                    inc="Lw", k=16)
            walk.op("sync", lambda e: e.dma_start(out=cst["ones_row_f32"], in_=consts["ones_row_f32"]),
                    inc="Lw", k=16)
            for e2 in range(8):
                walk.op("sync", lambda e, e2=e2: e.dma_start(
                    out=cst[f"oh8_{e2}"], in_=consts["oh8"][e2]), inc="Lw", k=16)
            p5_emit(walk, nc, bufs, xin, xout, wbf, smalls)
            walk.final_waits("gpsimd")

        w.plan()
        full(w)
        w.totals = dict(w.counts)

        @block.sync
        def _(q):
            w.start_emit("sync", q)
            full(w)

        @block.tensor
        def _(t):
            w.start_emit("tensor", t)
            full(w)

        @block.vector
        def _(v):
            w.start_emit("vector", v)
            full(w)

        @block.scalar
        def _(s):
            w.start_emit("scalar", s)
            full(w)

        @block.gpsimd
        def _(g):
            w.start_emit("gpsimd", g)
            full(w)

    return nc


P0_ITEMS = [
    ("L0_attn_Wq", "b_wq0", D, D, 1.0, "cast"),
    ("L0_attn_Wk", "b_wk0", D, D, 1.0, "cast"),
    ("L0_attn_Wv", "b_wv0", D, D, 1.0, "cast"),
    ("L0_attn_Wo", "b_wo0", D, D, 1.0, "cast"),
    ("L0_attn_Wgate", "b_wg", D, 4096, 1.0, "cast"),
    ("L0_attn_Wup", "b_wu", D, 4096, 1.0, "cast"),
    ("L0_attn_Wdown", "b_wd", 4096, D, 1.0, "cast"),
    ("L0_ffn_fc1_q", "b_f1", D, 4096, "L0_ffn_fc1_scales", "dequant"),
    ("L0_ffn_fc2_q", "b_f2", 4096, D, "L0_ffn_fc2_scales", "dequant"),
    ("wqT1_f", "b_wq1", D, D, 0.125, "cast"),
    ("wkT1_f", "b_wk1", D, D, 1.0, "cast"),
    ("wvT1_f", "b_wv1", D, D, 1.0, "cast"),
    ("woT1_f", "b_wo1", D, D, 1.0, "cast"),
    ("weT_f", "b_we", 8 * D, D, 1.0, "cast"),
]


def build_p0_program():
    nc = bass.Bass("TRN2", target_bir_lowering=False)
    nc.ctx.enter_context(nc.allow_low_precision(
        reason="weights cast to bf16 for matmuls; intentional"))
    dram_in = {}
    dram_out = {}
    dram_scale = {}
    for src, dst, rows, cols, sc, kind in P0_ITEMS:
        dt_in = I32 if kind == "dequant" else F32
        dram_in[src] = nc.dram_tensor(src, [rows, cols], dt_in, kind="ExternalInput").ap()
        dram_out[dst] = nc.dram_tensor(dst, [rows, cols], BF16, kind="ExternalOutput").ap()
        if kind == "dequant":
            dram_scale[sc] = nc.dram_tensor(sc, [rows // 128, cols], F32, kind="ExternalInput").ap()
    cm128 = nc.inline_tensor(np.full((128, 1), -128.0, dtype=np.float32), name="cm128").ap()

    with ExitStack() as stack:
        E = stack.enter_context

        def sbuf(name, shape, dt):
            return E(nc.sbuf_tensor(name, shape, dt))[:, :]

        tin = [sbuf(f"tin{j}", [128, 4096], F32) for j in range(2)]
        tqn = [sbuf(f"tqn{j}", [128, 4096], I32) for j in range(2)]
        tsc = [sbuf(f"tsc{j}", [128, 4096], F32) for j in range(2)]
        tmid = [sbuf(f"tmid{j}", [128, 4096], F32) for j in range(2)]
        tout = [sbuf(f"tout{j}", [128, 4096], BF16) for j in range(2)]
        c128 = sbuf("c128_sb", [128, 1], F32)
        sems = {n: E(nc.semaphore(n)) for n in
                ("Li0", "Li1", "So0", "So1", "C", "V")}
        block = E(nc.Block())

        w = Walk(sems)
        w.TQ, w.V, w.C = None, "V", "C"

        def full(walk):
            step = 0
            for src, dst, rows, cols, sc, kind in P0_ITEMS:
                for r in range(rows // 128):
                    p = step % 2
                    ls = f"Li{p}"
                    if step >= 2:
                        walk.wait("sync", f"p0.st.{step-2}")
                    if kind == "cast":
                        walk.op("sync", lambda e, src=src, r=r, p=p, cols=cols: e.dma_start(
                            out=tin[p][:, 0:cols], in_=dram_in[src][r * 128:(r + 1) * 128, :]),
                            inc=ls, k=16, label=f"p0.ld.{step}")
                        walk.wait("scalar", f"p0.ld.{step}")
                        if step >= 2:
                            walk.wait("scalar", f"p0.st.{step-2}")
                        walk.op("scalar", lambda e, p=p, cols=cols, sc=sc: e.activation(
                            out=tout[p][:, 0:cols], in_=tin[p][:, 0:cols], func=AF.Copy,
                            scale=float(sc)),
                            inc="C", label=f"p0.cast.{step}")
                        walk.wait("gpsimd", f"p0.cast.{step}")
                    else:
                        walk.op("sync", lambda e, src=src, r=r, p=p, cols=cols: e.dma_start(
                            out=tqn[p][:, 0:cols], in_=dram_in[src][r * 128:(r + 1) * 128, :]),
                            inc=ls, k=16, label=f"p0.ldq.{step}")
                        walk.op("sync", lambda e, sc=sc, r=r, p=p, cols=cols: e.dma_start(
                            out=tsc[p][:, 0:cols],
                            in_=dram_scale[sc][r:r + 1, :].broadcast_to([128, cols])),
                            inc=ls, k=16, label=f"p0.ld.{step}")
                        walk.wait("vector", f"p0.ld.{step}")
                        walk.op("vector", lambda e, p=p, cols=cols: e.tensor_scalar_add(
                            out=tmid[p][:, 0:cols], in0=tqn[p][:, 0:cols], scalar1=-128.0),
                            inc="V", label=f"p0.mid.{step}")
                        walk.wait("vector", f"p0.mid.{step}")
                        if step >= 2:
                            walk.wait("vector", f"p0.st.{step-2}")
                        walk.op("vector", lambda e, p=p, cols=cols: e.tensor_mul(
                            out=tout[p][:, 0:cols], in0=tmid[p][:, 0:cols], in1=tsc[p][:, 0:cols]),
                            inc="V", label=f"p0.cast.{step}")
                        walk.wait("gpsimd", f"p0.cast.{step}")
                    walk.op("gpsimd", lambda e, dst=dst, r=r, p=p, cols=cols: e.dma_start(
                        out=dram_out[dst][r * 128:(r + 1) * 128, :], in_=tout[p][:, 0:cols]),
                        inc=f"So{p}", k=16, label=f"p0.st.{step}")
                    step += 1
            nst = step
            w2 = (nst + 1) // 2
            walk.wait_count("gpsimd", "So0", 16 * ((nst + 1) // 2))
            walk.wait_count("gpsimd", "So1", 16 * (nst // 2))

        w.plan()
        full(w)

        @block.sync
        def _(q):
            w.start_emit("sync", q)
            full(w)

        @block.tensor
        def _(t):
            w.start_emit("tensor", t)
            full(w)

        @block.vector
        def _(v):
            w.start_emit("vector", v)
            full(w)

        @block.scalar
        def _(s):
            w.start_emit("scalar", s)
            full(w)

        @block.gpsimd
        def _(g):
            w.start_emit("gpsimd", g)
            full(w)

    return nc


def p0_emit_fused(w, sb0, dram_in, dram_out, dram_scale):
    for src, dst, rows, cols, sc, kind in P0_ITEMS:
        for r in range(rows // 128):
            if kind == "cast":
                w.op("sync", lambda e, src=src, r=r, cols=cols: e.dma_start(
                    out=sb0["tin"][:, 0:cols], in_=dram_in[src][r * 128:(r + 1) * 128, :]),
                    inc="Lw", k=16)
                w.op("scalar", lambda e, cols=cols, sc=sc: e.activation(
                    out=sb0["tout"][:, 0:cols], in_=sb0["tin"][:, 0:cols], func=AF.Copy,
                    scale=float(sc)))
            else:
                w.op("sync", lambda e, src=src, r=r, cols=cols: e.dma_start(
                    out=sb0["tqn"][:, 0:cols], in_=dram_in[src][r * 128:(r + 1) * 128, :]),
                    inc="Lw", k=16)
                w.op("sync", lambda e, sc=sc, r=r, cols=cols: e.dma_start(
                    out=sb0["tsc"][:, 0:cols],
                    in_=dram_scale[sc][r:r + 1, :].broadcast_to([128, cols])),
                    inc="Lw", k=16)
                w.op("vector", lambda e, cols=cols: e.tensor_scalar_add(
                    out=sb0["tmid"][:, 0:cols], in0=sb0["tqn"][:, 0:cols], scalar1=-128.0))
                w.op("vector", lambda e, cols=cols: e.tensor_mul(
                    out=sb0["tout"][:, 0:cols], in0=sb0["tmid"][:, 0:cols],
                    in1=sb0["tsc"][:, 0:cols]))
            w.op("gpsimd", lambda e, dst=dst, r=r, cols=cols: e.dma_start(
                out=dram_out[dst][r * 128:(r + 1) * 128, :], in_=sb0["tout"][:, 0:cols]),
                inc="S0", k=16)


def build_fused_program(debug_streams=False):
    nc = bass.Bass("TRN2", target_bir_lowering=False)
    nc.ctx.enter_context(nc.allow_low_precision(
        reason="bf16 matmul operands with fp32 accumulation; intentional"))
    ikind = "ExternalOutput" if debug_streams else "Internal"

    dram_in = {}
    dram_scale = {}
    dram_b = {}
    for src, dst, rows, cols, sc, kind in P0_ITEMS:
        dt_in = I32 if kind == "dequant" else F32
        dram_in[src] = nc.dram_tensor(src, [rows, cols], dt_in, kind="ExternalInput").ap()
        dram_b[dst] = nc.dram_tensor(dst, [rows, cols], BF16, kind=ikind).ap()
        if kind == "dequant":
            dram_scale[sc] = nc.dram_tensor(sc, [rows // 128, cols], F32,
                                            kind="ExternalInput").ap()
    xin = nc.dram_tensor("xin", [D, NTOK], F32, kind="ExternalInput").ap()
    xout = nc.dram_tensor("xout", [D, NTOK], F32, kind="ExternalOutput").ap()
    xs1 = nc.dram_tensor("xs1", [D, NTOK], F32, kind=ikind).ap()
    xs2 = nc.dram_tensor("xs2", [D, NTOK], F32, kind=ikind).ap()
    xs3 = nc.dram_tensor("xs3", [D, NTOK], F32, kind=ikind).ap()
    xs4 = nc.dram_tensor("xs4", [D, NTOK], F32, kind=ikind).ap()
    sm = {
        "attn_norm_w": nc.dram_tensor("attn_norm_w", [128, KC], F32, kind="ExternalInput").ap(),
        "out_norm_w": nc.dram_tensor("out_norm_w", [128, 1], F32, kind="ExternalInput").ap(),
        "nw": nc.dram_tensor("nw", [128, KC], F32, kind="ExternalInput").ap(),
        "gamma": nc.dram_tensor("gamma", [128, KC], F32, kind="ExternalInput").ap(),
        "beta": nc.dram_tensor("beta", [128, KC], F32, kind="ExternalInput").ap(),
        "b1": nc.dram_tensor("b1", [128, 32], F32, kind="ExternalInput").ap(),
        "b2": nc.dram_tensor("b2", [128, KC], F32, kind="ExternalInput").ap(),
        "wg": nc.dram_tensor("wg", [D, 16], F32, kind="ExternalInput").ap(),
        "bg": nc.dram_tensor("bg", [1, 16], F32, kind="ExternalInput").ap(),
        "wr": nc.dram_tensor("wr", [D, 8], F32, kind="ExternalInput").ap(),
        "rb": nc.dram_tensor("rb", [1, 8], F32, kind="ExternalInput").ap(),
        "eb": nc.dram_tensor("eb", [8, D], F32, kind="ExternalInput").ap(),
    }
    consts = build_constants(nc)

    with ExitStack() as outer:
        E = outer.enter_context
        ps = [E(nc.psum_tensor(f"psb{j}", [128, 512], F32))[:, :] for j in range(8)]
        semnames = ("Lw", "Lx0", "Lx1", "Lx2", "Lp0", "Lp1", "Lp2", "Lp3",
                    "S0", "S1", "T", "V", "C")
        sems = {name: E(nc.semaphore(name)) for name in semnames}

        def scoped_alloc(fn):
            st = ExitStack()
            def sbuf(name, shape, dt):
                return st.enter_context(nc.sbuf_tensor(name, shape, dt))[:, :]
            r = fn(sbuf)
            st.close()
            return r

        def a0(sbuf):
            return {"tin": sbuf("z0_tin", [128, 4096], F32),
                    "tqn": sbuf("z0_tqn", [128, 4096], I32),
                    "tsc": sbuf("z0_tsc", [128, 4096], F32),
                    "tmid": sbuf("z0_tmid", [128, 4096], F32),
                    "tout": sbuf("z0_tout", [128, 4096], BF16)}
        sb0 = scoped_alloc(a0)

        def a1(sbuf):
            sb = {}
            sb["x0"] = sbuf("z1_x0", [128, KC * 512], F32)
            sb["x1"] = sbuf("z1_x1", [128, KC * 512], F32)
            for n in ("wq", "wk", "wv", "wo"):
                sb[n] = sbuf("z1_" + n, [128, KC * 1024], BF16)
            sb["h"] = sbuf("z1_h", [128, KC * 512], BF16)
            sb["xsq"] = sbuf("z1_xsq", [128, KC * 512], BF16)
            sb["q"] = sbuf("z1_q", [128, 8 * 512], BF16)
            sb["k"] = sbuf("z1_k", [128, 8 * 512], BF16)
            sb["kt"] = sbuf("z1_kt", [128, NCH * 1024], BF16)
            sb["vt"] = sbuf("z1_vt", [128, NCH * 1024], BF16)
            sb["o"] = sbuf("z1_o", [128, 8 * 512], F32)
            sb["on_bf"] = sbuf("z1_on", [128, 8 * 512], BF16)
            sb["at_bf"] = sbuf("z1_at", [128, 1024], BF16)
            sb["S_f32"] = sbuf("z1_Sf", [128, 8 * 128], F32)
            sb["S_bf"] = sbuf("z1_Sb", [128, 8 * 128], BF16)
            sb["nw1"] = sbuf("z1_nw1", [128, KC], F32)
            sb["onw"] = sbuf("z1_onw", [128, 1], F32)
            sb["row_a"] = sbuf("z1_rowa", [1, 512], F32)
            sb["row_b"] = sbuf("z1_rowb", [1, 512], F32)
            sb["row_bf"] = sbuf("z1_rowbf", [1, 512], BF16)
            cst = {}
            cst["ones_col"] = sbuf("z1_onescol", [128, 1], BF16)
            cst["ones_row"] = sbuf("z1_onesrow", [1, 128], BF16)
            cst["maskUT4"] = sbuf("z1_mask", [128, 512], F32)
            return sb, cst
        sb1, cst1 = scoped_alloc(a1)

        def a_mlp(z, swiglu):
            def f(sbuf):
                sb = {}
                sb["x0"] = sbuf(z + "_x0", [128, KC * 512], F32)
                sb["x1"] = sbuf(z + "_x1", [128, KC * 512], F32)
                if swiglu:
                    sb["x0s"] = sbuf(z + "_x0s", [128, KC * 512], F32)
                sb["h"] = sbuf(z + "_h", [128, KC * 512], BF16)
                sb["xsq"] = sbuf(z + "_xsq", [128, KC * 512], BF16)
                sb["xbf"] = sbuf(z + "_xbf", [128, KC * 512], BF16)
                sb["tgu"] = sbuf(z + "_tgu", [128, 32 * 512], BF16)
                sb["wslot"] = [sbuf(f"{z}_ws{j}", [128, 8192], BF16) for j in range(4)]
                sb["sg0"] = sbuf(z + "_sg0", [128, 512], BF16)
                sb["sg1"] = sbuf(z + "_sg1", [128, 512], BF16)
                sb["tmp0"] = sbuf(z + "_tmp0", [128, 512], F32)
                sb["tmp1"] = sbuf(z + "_tmp1", [128, 512], F32)
                sb["tmp2"] = sbuf(z + "_tmp2", [128, KC * 512], F32)
                for n in ("row_a", "row_b", "row_c", "row_mu"):
                    sb[n] = sbuf(f"{z}_{n}", [1, 512], F32)
                sb["row_bf"] = sbuf(z + "_rowbf", [1, 512], BF16)
                sb["row_mu_bf"] = sbuf(z + "_rowmubf", [1, 512], BF16)
                if swiglu:
                    sb["nw"] = sbuf(z + "_nw", [128, KC], F32)
                else:
                    sb["gamma"] = sbuf(z + "_gamma", [128, KC], F32)
                    sb["beta"] = sbuf(z + "_beta", [128, KC], F32)
                    sb["b1"] = sbuf(z + "_b1", [128, 32], F32)
                    sb["b2"] = sbuf(z + "_b2", [128, KC], F32)
                    sb["b2h"] = sbuf(z + "_b2h", [128, KC], F32)
                cst = {}
                cst["ones_col"] = sbuf(z + "_onescol", [128, 1], BF16)
                cst["ones_row"] = sbuf(z + "_onesrow", [1, 128], BF16)
                cst["eps11"] = sbuf(z + "_eps11", [1, 1], F32)
                return sb, cst
            return f
        sb2, cst2 = scoped_alloc(a_mlp("z2", True))
        sb3, cst3 = scoped_alloc(a_mlp("z3", False))

        def a4(sbuf):
            sb = {}
            sb["x0"] = sbuf("z4_x0", [128, KC * 512], F32)
            sb["x1"] = sbuf("z4_x1", [128, KC * 512], F32)
            sb["xbf"] = sbuf("z4_xbf", [128, KC * 512], BF16)
            for n in ("wq", "wk", "wv", "wo"):
                sb[n] = sbuf("z4_" + n, [128, KC * 1024], BF16)
            sb["wg"] = sbuf("z4_wg", [128, KC * 16], F32)
            sb["bg"] = sbuf("z4_bg", [1, 16], F32)
            for n in ("qr", "kr", "kt", "vt", "fmin", "fexp"):
                sb[n] = sbuf("z4_" + n, [128, KC * 512], BF16)
            sb["o"] = sb["fexp"]
            sb["S_f32"] = sbuf("z4_Sf", [128, 512], F32)
            sb["S_bf"] = sbuf("z4_Sb", [128, 512], BF16)
            for n in ("gate", "logg", "LT", "NLT", "Bm", "call", "eLb"):
                sb[n] = sbuf("z4_" + n, [128, 16], F32)
            sb["Lrow"] = sbuf("z4_Lrow", [16, 128], F32)
            sb["Erow"] = sbuf("z4_Erow", [16, 128], BF16)
            for n in ("D0", "D1", "PD0", "PD1", "t10", "t11"):
                sb[n] = sbuf("z4_" + n, [128, 512], BF16)
            sb["kh0"] = sbuf("z4_kh0", [128, 256], BF16)
            sb["kh1"] = sbuf("z4_kh1", [128, 256], BF16)
            sb["qt0"] = sbuf("z4_qt0", [128, 128], BF16)
            sb["qt1"] = sbuf("z4_qt1", [128, 128], BF16)
            sb["tmp2"] = sbuf("z4_tmp2", [128, KC * 512], F32)
            sb["xsq"] = sb["fmin"]
            sb["row_a"] = sb["tmp2"][0:1, 1536:2048]
            sb["row_b"] = sb["tmp2"][0:1, 2048:2560]
            sb["row_c"] = sb["tmp2"][0:1, 2560:3072]
            sb["row_mu"] = sb["tmp2"][0:1, 3072:3584]
            sb["row_bf"] = sbuf("z4_rowbf", [1, 512], BF16)
            sb["row_mu_bf"] = sbuf("z4_rowmubf", [1, 512], BF16)
            cst = {}
            cst["ones_col"] = sbuf("z4_onescol", [128, 1], BF16)
            cst["ones_row"] = sbuf("z4_onesrow", [1, 128], BF16)
            cst["ones_row_f32"] = sbuf("z4_onesrowf", [1, 128], F32)
            cst["eps11"] = sbuf("z4_eps11", [1, 1], F32)
            cst["ident_f32"] = sbuf("z4_identf", [128, 128], F32)
            cst["cums_f32"] = sbuf("z4_cums", [128, 128], F32)
            cst["row127_f32"] = sbuf("z4_row127", [128, 128], F32)
            cst["maskUT4"] = sbuf("z4_mask", [128, 512], F32)
            for h in range(16):
                cst[f"oh16f_{h}"] = sbuf(f"z4_oh16f_{h}", [16, 128], F32)
            for p in range(8):
                cst[f"prsel_{p}"] = sbuf(f"z4_prsel_{p}", [16, 128], BF16)
            return sb, cst
        sb4, cst4 = scoped_alloc(a4)

        def a5(sbuf):
            sb = {}
            sb["x0"] = sbuf("z5_x0", [128, KC * 512], F32)
            sb["x1"] = sb["x0"]
            sb["xbf"] = sbuf("z5_xbf", [128, KC * 512], BF16)
            sb["we"] = sbuf("z5_we", [128, 64 * 1024], BF16)
            sb["wr"] = sbuf("z5_wr", [128, 64], F32)
            sb["rb_f"] = sbuf("z5_rbf", [1, 8], F32)
            sb["eb_bf"] = sbuf("z5_ebbf", [8, D], BF16)
            sb["nmx"] = sbuf("z5_nmx", [128, 1], F32)
            sb["srow"] = sbuf("z5_srow", [128, 1], F32)
            sb["wrow"] = sbuf("z5_wrow", [128, 1], F32)
            sb["eexp"] = sbuf("z5_eexp", [128, 8], F32)
            sb["m0"] = sbuf("z5_m0", [128, 8], F32)
            sb["m1"] = sbuf("z5_m1", [128, 8], F32)
            sb["mT"] = sbuf("z5_mT", [8, 512], BF16)
            sb["maskbc"] = sbuf("z5_maskbc", [128, 8 * 512], BF16)
            sb["xm0"] = sbuf("z5_xm0", [128, KC * 512], BF16)
            sb["xm1"] = sbuf("z5_xm1", [128, KC * 512], BF16)
            sb["xsq"] = sb["xm1"]
            sb["tmp2"] = sbuf("z5_tmp2", [128, KC * 512], F32)
            sb["eb_f"] = sb["tmp2"][0:8, 0:1024]
            sb["row_a"] = sb["tmp2"][0:1, 1536:2048]
            sb["row_b"] = sb["tmp2"][0:1, 2048:2560]
            sb["row_c"] = sb["tmp2"][0:1, 2560:3072]
            sb["row_mu"] = sb["tmp2"][0:1, 3072:3584]
            sb["row_bf"] = sbuf("z5_rowbf", [1, 512], BF16)
            sb["row_mu_bf"] = sbuf("z5_rowmubf", [1, 512], BF16)
            cst = {}
            cst["ones_col"] = sbuf("z5_onescol", [128, 1], BF16)
            cst["ones_row"] = sbuf("z5_onesrow", [1, 128], BF16)
            cst["eps11"] = sbuf("z5_eps11", [1, 1], F32)
            cst["ident_f32"] = sbuf("z5_identf", [128, 128], F32)
            cst["ones_row_f32"] = sbuf("z5_onesrowf", [1, 128], F32)
            for e in range(8):
                cst[f"oh8_{e}"] = sbuf(f"z5_oh8_{e}", [8, 128], BF16)
            return sb, cst
        sb5, cst5 = scoped_alloc(a5)

        block = E(nc.Block())
        w = Walk(sems, serial=True)
        w.TQ, w.V, w.C = "T", "V", "C"

        def ld(walk, dst_ap, src_ap):
            walk.op("sync", lambda e, dst_ap=dst_ap, src_ap=src_ap: e.dma_start(
                out=dst_ap, in_=src_ap), inc="Lw", k=16)

        def full(walk):
            p0_emit_fused(walk, sb0, dram_in, dram_b, dram_scale)

            ld(walk, cst1["ones_col"], consts["ones_col_bf"])
            ld(walk, cst1["ones_row"], consts["ones_row_bf"])
            ld(walk, cst1["maskUT4"], consts["maskUT4_f32"])
            wbf1 = {"wq": dram_b["b_wq0"], "wk": dram_b["b_wk0"],
                    "wv": dram_b["b_wv0"], "wo": dram_b["b_wo0"]}
            p1_emit(walk, nc, {"sb": sb1, "ps": ps, "cst": cst1}, xin, xs1, wbf1,
                    {"attn_norm_w": sm["attn_norm_w"], "out_norm_w": sm["out_norm_w"]},
                    pfx="p1")

            ld(walk, cst2["ones_col"], consts["ones_col_bf"])
            ld(walk, cst2["ones_row"], consts["ones_row_bf"])
            ld(walk, cst2["eps11"], consts["eps11"])
            mlp_emit(walk, nc, {"sb": sb2, "ps": ps, "cst": cst2}, xs1, xs2,
                     {"wg": dram_b["b_wg"], "wu": dram_b["b_wu"], "wd": dram_b["b_wd"]},
                     {"nw": sm["nw"]}, {"kind": "swiglu", "pre": "rms", "dbg": None},
                     "p2", xin0=xin)

            ld(walk, cst3["ones_col"], consts["ones_col_bf"])
            ld(walk, cst3["ones_row"], consts["ones_row_bf"])
            ld(walk, cst3["eps11"], consts["eps11"])
            mlp_emit(walk, nc, {"sb": sb3, "ps": ps, "cst": cst3}, xs2, xs3,
                     {"wg": dram_b["b_f1"], "wd": dram_b["b_f2"]},
                     {"gamma": sm["gamma"], "beta": sm["beta"],
                      "b1": sm["b1"], "b2": sm["b2"]},
                     {"kind": "qfc", "pre": "ln", "dbg": None}, "p3")

            for cn, src_ap in (
                ("ones_col", consts["ones_col_bf"]), ("ones_row", consts["ones_row_bf"]),
                ("ones_row_f32", consts["ones_row_f32"]), ("eps11", consts["eps11"]),
                ("ident_f32", consts["ident_f32"]), ("cums_f32", consts["cums_f32"]),
                ("row127_f32", consts["row127_f32"]), ("maskUT4", consts["maskUT4_f32"]),
            ):
                ld(walk, cst4[cn], src_ap)
            for h in range(16):
                ld(walk, cst4[f"oh16f_{h}"], consts["oh16f"][h])
            for p in range(8):
                ld(walk, cst4[f"prsel_{p}"], consts["prsel"][p])
            wbf4 = {"wq": dram_b["b_wq1"], "wk": dram_b["b_wk1"],
                    "wv": dram_b["b_wv1"], "wo": dram_b["b_wo1"]}
            p4_emit(walk, nc, {"sb": sb4, "ps": ps, "cst": cst4}, xs3, xs4, wbf4,
                    {"wg": sm["wg"], "bg": sm["bg"]}, pfx="p4")

            ld(walk, cst5["ones_col"], consts["ones_col_bf"])
            ld(walk, cst5["ones_row"], consts["ones_row_bf"])
            ld(walk, cst5["eps11"], consts["eps11"])
            ld(walk, cst5["ident_f32"], consts["ident_f32"])
            ld(walk, cst5["ones_row_f32"], consts["ones_row_f32"])
            for e2 in range(8):
                ld(walk, cst5[f"oh8_{e2}"], consts["oh8"][e2])
            p5_emit(walk, nc, {"sb": sb5, "ps": ps, "cst": cst5}, xs4, xout,
                    {"we": dram_b["b_we"], "wr": sm["wr"]},
                    {"rb": sm["rb"], "eb": sm["eb"]}, pfx="p5")
            walk.final_waits("gpsimd")

        w.plan()
        full(w)
        w.totals = dict(w.counts)
        global LAST_TOTALS
        LAST_TOTALS = dict(w.totals)
        for s_, v_ in w.totals.items():
            assert v_ < 65000, (s_, v_)

        @block.sync
        def _(q):
            w.start_emit("sync", q)
            full(w)

        @block.tensor
        def _(t):
            w.start_emit("tensor", t)
            full(w)

        @block.vector
        def _(v):
            w.start_emit("vector", v)
            full(w)

        @block.scalar
        def _(s):
            w.start_emit("scalar", s)
            full(w)

        @block.gpsimd
        def _(g):
            w.start_emit("gpsimd", g)
            full(w)

    return nc


LAST_TOTALS = None


def kernel_fused_inputs(g, core_x):
    cc = np.ascontiguousarray
    m = {
        "L0_attn_Wq": cc(g["L0_attn_Wq"]), "L0_attn_Wk": cc(g["L0_attn_Wk"]),
        "L0_attn_Wv": cc(g["L0_attn_Wv"]), "L0_attn_Wo": cc(g["L0_attn_Wo"]),
        "L0_attn_Wgate": cc(g["L0_attn_Wgate"]), "L0_attn_Wup": cc(g["L0_attn_Wup"]),
        "L0_attn_Wdown": cc(g["L0_attn_Wdown"]),
        "L0_ffn_fc1_q": cc(g["L0_ffn_fc1_q"]), "L0_ffn_fc1_scales": cc(g["L0_ffn_fc1_scales"]),
        "L0_ffn_fc2_q": cc(g["L0_ffn_fc2_q"]), "L0_ffn_fc2_scales": cc(g["L0_ffn_fc2_scales"]),
        "wqT1_f": cc(g["L1_attn_Wq"].T), "wkT1_f": cc(g["L1_attn_Wk"].T),
        "wvT1_f": cc(g["L1_attn_Wv"].T), "woT1_f": cc(g["L1_attn_Wo"].T),
        "weT_f": cc(g["L1_ffn_expert_w"].transpose(0, 2, 1).reshape(8 * D, D)),
        "xin": core_x,
        "attn_norm_w": cc(g["L0_attn_attn_norm_w"].reshape(KC, 128).T),
        "out_norm_w": cc(g["L0_attn_attn_out_norm_w"].reshape(128, 1)),
        "nw": cc(g["L0_attn_mlp_norm_w"].reshape(KC, 128).T),
        "gamma": cc(g["L0_ffn_ln_gamma"].reshape(KC, 128).T),
        "beta": cc(g["L0_ffn_ln_beta"].reshape(KC, 128).T),
        "b1": cc(g["L0_ffn_fc1_bias"].reshape(32, 128).T),
        "b2": cc(g["L0_ffn_fc2_bias"].reshape(KC, 128).T),
        "wg": cc(g["L1_attn_Wg"].T),
        "bg": cc(g["L1_attn_bg"].reshape(1, 16)),
        "wr": cc(g["L1_ffn_router_w"].T),
        "rb": cc(g["L1_ffn_router_b"].reshape(1, 8)),
        "eb": cc(g["L1_ffn_expert_b"]),
    }
    return m


def kernel_fused(**inp):
    g = {k: np.asarray(v) for k, v in inp.items()}
    x = g["L0_attn_hidden_states"]
    nc = build_fused_program()
    cores = list(range(NCORE))
    in_maps = [kernel_fused_inputs(g, to_fm(x[SEQ_PER_CORE * c:SEQ_PER_CORE * (c + 1)]))
               for c in cores]
    r = run_bass_kernel_spmd(nc, in_maps, core_ids=cores)
    out = np.empty((BATCH, T, D), dtype=np.float32)
    for c in cores:
        out[SEQ_PER_CORE * c:SEQ_PER_CORE * (c + 1)] = from_fm(
            np.asarray(r.results[c]["xout"], dtype=np.float32))
    return out


def to_fm(x_core):
    s, t, d = x_core.shape
    return np.ascontiguousarray(x_core.transpose(2, 0, 1).reshape(d, s * t))


def from_fm(xT):
    d, n = xT.shape
    return np.ascontiguousarray(xT.reshape(d, n // T, T).transpose(1, 2, 0))


_PROG_CACHE = {}


def _programs():
    if "p0" not in _PROG_CACHE:
        _PROG_CACHE["p0"] = build_p0_program()
        _PROG_CACHE["p1"] = build_p1_program()
        _PROG_CACHE["p2"] = build_mlp_program("swiglu")
        _PROG_CACHE["p3"] = build_mlp_program("qfc")
        _PROG_CACHE["p4"] = build_p4_program()
        _PROG_CACHE["p5"] = build_p5_program()
    return _PROG_CACHE


def kernel_unfused(**inp):
    progs = _programs()
    cc = np.ascontiguousarray
    g = {k: np.asarray(v) for k, v in inp.items()}
    x = g["L0_attn_hidden_states"]

    p0_in = {
        "L0_attn_Wq": cc(g["L0_attn_Wq"]), "L0_attn_Wk": cc(g["L0_attn_Wk"]),
        "L0_attn_Wv": cc(g["L0_attn_Wv"]), "L0_attn_Wo": cc(g["L0_attn_Wo"]),
        "L0_attn_Wgate": cc(g["L0_attn_Wgate"]), "L0_attn_Wup": cc(g["L0_attn_Wup"]),
        "L0_attn_Wdown": cc(g["L0_attn_Wdown"]),
        "L0_ffn_fc1_q": cc(g["L0_ffn_fc1_q"]), "L0_ffn_fc1_scales": cc(g["L0_ffn_fc1_scales"]),
        "L0_ffn_fc2_q": cc(g["L0_ffn_fc2_q"]), "L0_ffn_fc2_scales": cc(g["L0_ffn_fc2_scales"]),
        "wqT1_f": cc(g["L1_attn_Wq"].T), "wkT1_f": cc(g["L1_attn_Wk"].T),
        "wvT1_f": cc(g["L1_attn_Wv"].T), "woT1_f": cc(g["L1_attn_Wo"].T),
        "weT_f": cc(g["L1_ffn_expert_w"].transpose(0, 2, 1).reshape(8 * D, D)),
    }
    res0 = run_bass_kernel_spmd(progs["p0"], [p0_in], core_ids=[0])
    wb = res0.results[0]

    cores = list(range(NCORE))
    xs0 = [to_fm(x[SEQ_PER_CORE * c:SEQ_PER_CORE * (c + 1)]) for c in cores]

    nw1 = cc(g["L0_attn_attn_norm_w"].reshape(KC, 128).T)
    onw = cc(g["L0_attn_attn_out_norm_w"].reshape(128, 1))
    in1 = [{"xin": xs0[c], "wq": wb["b_wq0"], "wk": wb["b_wk0"], "wv": wb["b_wv0"],
            "wo": wb["b_wo0"], "attn_norm_w": nw1, "out_norm_w": onw} for c in cores]
    r = run_bass_kernel_spmd(progs["p1"], in1, core_ids=cores)
    xs = [r.results[c]["xout"] for c in cores]

    nw2 = cc(g["L0_attn_mlp_norm_w"].reshape(KC, 128).T)
    in2 = [{"xin": xs[c], "xin0": xs0[c], "wg": wb["b_wg"], "wu": wb["b_wu"],
            "wd": wb["b_wd"], "nw": nw2} for c in cores]
    r = run_bass_kernel_spmd(progs["p2"], in2, core_ids=cores)
    xs = [r.results[c]["xout"] for c in cores]

    in3 = [{"xin": xs[c], "wg": wb["b_f1"], "wd": wb["b_f2"],
            "gamma": cc(g["L0_ffn_ln_gamma"].reshape(KC, 128).T),
            "beta": cc(g["L0_ffn_ln_beta"].reshape(KC, 128).T),
            "b1": cc(g["L0_ffn_fc1_bias"].reshape(32, 128).T),
            "b2": cc(g["L0_ffn_fc2_bias"].reshape(KC, 128).T)} for c in cores]
    r = run_bass_kernel_spmd(progs["p3"], in3, core_ids=cores)
    xs = [r.results[c]["xout"] for c in cores]

    in4 = [{"xin": xs[c], "wq": wb["b_wq1"], "wk": wb["b_wk1"], "wv": wb["b_wv1"],
            "wo": wb["b_wo1"], "wg": cc(g["L1_attn_Wg"].T),
            "bg": cc(g["L1_attn_bg"].reshape(1, 16))} for c in cores]
    r = run_bass_kernel_spmd(progs["p4"], in4, core_ids=cores)
    xs = [r.results[c]["xout"] for c in cores]

    in5 = [{"xin": xs[c], "we": wb["b_we"], "wr": cc(g["L1_ffn_router_w"].T),
            "rb": cc(g["L1_ffn_router_b"].reshape(1, 8)),
            "eb": cc(g["L1_ffn_expert_b"])} for c in cores]
    r = run_bass_kernel_spmd(progs["p5"], in5, core_ids=cores)

    out = np.empty((BATCH, T, D), dtype=np.float32)
    for c in cores:
        out[SEQ_PER_CORE * c:SEQ_PER_CORE * (c + 1)] = from_fm(
            np.asarray(r.results[c]["xout"], dtype=np.float32))
    return out


BATCH = 32


def kernel(**inputs):
    return kernel_fused(**inputs)


# version 05 (final)
# speedup vs baseline: 1.6634x; 1.6634x over previous
import numpy as np
import ml_dtypes
from contextlib import ExitStack

import concourse.bass as bass
import concourse.mybir as mybir
from concourse.bass_utils import run_bass_kernel_spmd

F32 = mybir.dt.float32
BF16 = mybir.dt.bfloat16
I32 = mybir.dt.int32
AF = mybir.ActivationFunctionType
ALU = mybir.AluOpType
AX = mybir.AxisListType
BF = ml_dtypes.bfloat16

D = 1024
T = 2048
NCORE = 8
SEQ_PER_CORE = 4
NTOK = SEQ_PER_CORE * T
TILE = 512
NTILE = NTOK // TILE
TPS = T // TILE
CH = 128
NCH = TILE // CH
KC = D // 128
EPS = 1e-5

ENGINES = ("sync", "tensor", "vector", "scalar", "gpsimd")


BATCH_PE = True
BATCH_PE_W = False
BATCH_DMA = False

_DEF_SEM = {"sync": ("Lw", 16), "gpsimd": ("S0", 16),
            "tensor": ("T", 1), "vector": ("V", 1), "scalar": ("C", 1)}


class Walk:

    def __init__(self, sems, serial=False):
        self.mode = "plan"
        self.eng = None
        self.sems = sems
        self.serial = serial
        self.counts = {}
        self.live = {}
        self.labels = {}
        self.waited = {}
        self.op_engines = []
        self.oi = 0
        self.batch = False

    def scan(self):
        self.mode = "scan"
        self.op_engines = []
        self.oi = 0

    def plan(self):
        self.mode = "plan"
        self.counts = {k: 0 for k in self.sems}
        self.labels = {}
        self.oi = 0

    def start_emit(self, engine_name, eng_obj):
        self.mode = engine_name
        self.eng = eng_obj
        self.live = {k: 0 for k in self.sems}
        self.oi = 0

    def _slot(self, engine, inc, k, next_e):
        if inc is not None:
            return inc, k
        if self.serial:
            if self.batch and engine == "tensor" and next_e == "tensor":
                return None, 0
            return _DEF_SEM[engine]
        return None, 0

    def op(self, engine, fn, inc=None, k=1, label=None):
        if self.mode == "scan":
            self.op_engines.append(engine)
            return
        i = self.oi
        self.oi += 1
        oe = self.op_engines
        prev_e = oe[i - 1] if (oe and i > 0) else None
        next_e = oe[i + 1] if (oe and i + 1 < len(oe)) else None
        sem, kk = self._slot(engine, inc, k, next_e)
        if self.mode == "plan":
            if sem is not None:
                self.counts[sem] += kk
                if label is not None:
                    self.labels[label] = (sem, self.counts[sem])
            else:
                assert label is None
            return
        if self.mode == engine:
            if self.serial:
                skip = (self.batch and engine in ("tensor", "sync", "gpsimd")
                        and prev_e == engine)
                if not skip:
                    for s, v in self.live.items():
                        if v > 0 and self.waited.get((engine, s), -1) < v:
                            self.waited[(engine, s)] = v
                            self.eng.wait_ge(self.sems[s], v)
            ins = fn(self.eng)
            if sem is not None:
                ins.then_inc(self.sems[sem], kk)
        if sem is not None:
            self.live[sem] = self.live.get(sem, 0) + kk

    def wait(self, engine, label, delta=0):
        if self.mode == "plan" or self.mode != engine or self.serial:
            return
        sem_name, val = self.labels[label]
        val += delta
        if val <= 0:
            return
        key = (engine, sem_name)
        if self.waited.get(key, -1) >= val:
            return
        self.waited[key] = val
        self.eng.wait_ge(self.sems[sem_name], val)

    def wait_count(self, engine, sem_name, val):
        if self.mode == "plan" or self.mode != engine:
            return
        if val <= 0:
            return
        key = (engine, sem_name)
        if self.waited.get(key, -1) >= val:
            return
        self.waited[key] = val
        self.eng.wait_ge(self.sems[sem_name], val)

    def drain(self, engine):
        if self.mode == engine:
            self.eng.drain()

    def final_waits(self, engine):
        if self.mode == "plan" or self.mode != engine:
            return
        for s, v in self.totals.items():
            if v > 0 and self.waited.get((engine, s), -1) < v:
                self.waited[(engine, s)] = v
                self.eng.wait_ge(self.sems[s], v)


def build_constants(nc):
    c = {}
    ut = (np.arange(128)[:, None] <= np.arange(128)[None, :])
    c["ones_col_bf"] = nc.inline_tensor(np.ones((128, 1), dtype=BF), name="ones_col_bf").ap()
    c["ones_row_bf"] = nc.inline_tensor(np.ones((1, 128), dtype=BF), name="ones_row_bf").ap()
    m4 = np.tile(ut.astype(np.float32), (1, 4))
    c["maskUT4_f32"] = nc.inline_tensor(m4, name="maskUT4_f32").ap()
    c["cums_bf"] = nc.inline_tensor(ut.astype(BF), name="cums_bf").ap()
    mneg = np.where(ut, 0.0, -1e30).astype(np.float32)
    c["maskNEGT_bf"] = nc.inline_tensor(np.ascontiguousarray(mneg.T).astype(BF), name="maskNEGT_bf").ap()
    c["ident_bf"] = nc.inline_tensor(np.eye(128).astype(BF), name="ident_bf").ap()
    oh = np.zeros((16, 16, 128), dtype=BF)
    for h in range(16):
        oh[h, h, :] = 1
    c["oh16"] = [nc.inline_tensor(oh[h], name=f"oh16_{h}").ap() for h in range(16)]
    r127 = np.zeros((128, 128), dtype=BF)
    r127[127, :] = 1
    c["row127_bf"] = nc.inline_tensor(r127, name="row127_bf").ap()
    c["eps11"] = nc.inline_tensor(np.full((1, 1), EPS, dtype=np.float32), name="eps11").ap()
    c["ident_f32"] = nc.inline_tensor(np.eye(128, dtype=np.float32), name="ident_f32").ap()
    c["ones_row_f32"] = nc.inline_tensor(np.ones((1, 128), dtype=np.float32), name="ones_row_f32").ap()
    oh8 = np.zeros((8, 8, 128), dtype=BF)
    for e in range(8):
        oh8[e, e, :] = 1
    c["oh8"] = [nc.inline_tensor(oh8[e], name=f"oh8_{e}").ap() for e in range(8)]
    oh16f = np.zeros((16, 16, 128), dtype=np.float32)
    for h in range(16):
        oh16f[h, h, :] = 1
    c["oh16f"] = [nc.inline_tensor(oh16f[h], name=f"oh16f_{h}").ap() for h in range(16)]
    prsel = np.zeros((8, 16, 128), dtype=BF)
    for p in range(8):
        prsel[p, 2 * p, 0:64] = 1
        prsel[p, 2 * p + 1, 64:128] = 1
    c["prsel"] = [nc.inline_tensor(prsel[p], name=f"prsel_{p}").ap() for p in range(8)]
    c["cums_f32"] = nc.inline_tensor(ut.astype(np.float32), name="cums_f32").ap()
    r127f = np.zeros((128, 128), dtype=np.float32)
    r127f[127, :] = 1
    c["row127_f32"] = nc.inline_tensor(r127f, name="row127_f32").ap()
    return c


def rstd_row(w, pfx, i, tag, ps_stat, sb, scale, evac_label_prefix=None):
    w.op("vector", lambda e: e.tensor_scalar(
        out=sb["row_a"][0:1, :], in0=ps_stat[0:1, 0:512], scalar1=scale, scalar2=EPS,
        op0=ALU.mult, op1=ALU.add), inc=w.V, label=f"{pfx}.{tag}ssn.{i}")
    w.wait("scalar", f"{pfx}.{tag}ssn.{i}")
    w.op("scalar", lambda e: e.activation(
        out=sb["row_b"][0:1, :], in_=sb["row_a"][0:1, :], func=AF.Sqrt),
        inc=w.C, label=f"{pfx}.{tag}sqr.{i}")
    w.wait("vector", f"{pfx}.{tag}sqr.{i}")
    w.op("vector", lambda e: e.reciprocal(
        out=sb["row_bf"][0:1, :], in_=sb["row_b"][0:1, :]), inc=w.V, label=f"{pfx}.{tag}rstd.{i}")


def p1_emit(w, nc, bufs, xin, xout, wbf, smalls, pfx="p1"):
    sb = bufs["sb"]
    ps = bufs["ps"]
    cst = bufs["cst"]
    TQ, V, C = w.TQ, w.V, w.C

    for name in ("wq", "wk", "wv", "wo"):
        for k in range(KC):
            w.op("sync", lambda e, n=name, k=k: e.dma_start(
                out=sb[n][:, k * 1024:(k + 1) * 1024], in_=wbf[n][k * 128:(k + 1) * 128, :]),
                inc="Lw", k=16, label=f"{pfx}.w.{name}.{k}")
    w.op("sync", lambda e: e.dma_start(
        out=sb["nw1"][:, 0:KC], in_=smalls["attn_norm_w"][:, :]),
        inc="Lw", k=16, label=f"{pfx}.w.nw1")
    w.op("sync", lambda e: e.dma_start(
        out=sb["onw"][:, 0:1], in_=smalls["out_norm_w"][:, :]),
        inc="Lw", k=16, label=f"{pfx}.w.onw")
    w.wait("tensor", f"{pfx}.w.onw")
    w.wait("vector", f"{pfx}.w.onw")
    w.wait("scalar", f"{pfx}.w.onw")

    for i in range(NTILE):
        seq_start = (i % TPS == 0)
        t0 = i * TILE
        xb = sb["x0"] if i % 2 == 0 else sb["x1"]

        if i >= 2:
            w.wait("sync", f"{pfx}.st.{i-2}.{KC-1}")
        for kc in range(KC):
            w.op("sync", lambda e, kc=kc, t0=t0, xb=xb: e.dma_start(
                out=xb[:, kc * 512:(kc + 1) * 512],
                in_=xin[kc * 128:(kc + 1) * 128, t0:t0 + TILE]),
                inc=f"Lx{i % 2}", k=16, label=f"{pfx}.x.{i}.{kc}")

        w.wait("scalar", f"{pfx}.x.{i}.{KC-1}")
        w.op("scalar", lambda e, xb=xb: e.activation(
            out=sb["xsq"][:], in_=xb[:], func=AF.Square),
            inc=C, label=f"{pfx}.sq.{i}")
        w.wait("tensor", f"{pfx}.sq.{i}")
        if i >= 1:
            w.wait("tensor", f"{pfx}.h.{i-1}")
        for kc in range(KC):
            w.op("tensor", lambda e, kc=kc: e.matmul(
                ps[6][0:1, 0:512], cst["ones_col"], sb["xsq"][:, kc * 512:(kc + 1) * 512],
                start=(kc == 0), stop=(kc == KC - 1)),
                inc=(TQ if kc == KC - 1 else None),
                label=(f"{pfx}.ss.{i}" if kc == KC - 1 else None))
        w.wait("vector", f"{pfx}.ss.{i}")
        if i >= 1:
            w.wait("vector", f"{pfx}.osqr.{i-1}.7")
            w.wait("vector", f"{pfx}.obc.{i-1}.7")
        rstd_row(w, pfx, i, "", ps[6], sb, 1.0 / D)
        w.wait("tensor", f"{pfx}.rstd.{i}")
        w.op("tensor", lambda e: e.matmul(
            ps[6][:, 0:512], cst["ones_row"], sb["row_bf"][0:1, 0:512], start=True, stop=True),
            inc=TQ, label=f"{pfx}.bc.{i}")
        w.wait("vector", f"{pfx}.bc.{i}")
        if i >= 1:
            w.wait("vector", f"{pfx}.vt.mm.{i-1}.7")
        for kc in range(KC):
            w.op("vector", lambda e, kc=kc, xb=xb: e.scalar_tensor_tensor(
                out=sb["h"][:, kc * 512:(kc + 1) * 512], in0=xb[:, kc * 512:(kc + 1) * 512],
                scalar=sb["nw1"][:, kc:kc + 1], in1=ps[6][:, 0:512],
                op0=ALU.mult, op1=ALU.mult),
                inc=(V if kc == KC - 1 else None),
                label=(f"{pfx}.h.{i}" if kc == KC - 1 else None))

        w.wait("tensor", f"{pfx}.h.{i}")

        def proj_fm(dst, wname, evac_eng, tag, prev_tag):
            for m in range(KC):
                bank = ps[m % 2]
                if m >= 2:
                    w.wait("tensor", f"{pfx}.{tag}.ev.{i}.{m-2}")
                elif prev_tag is not None:
                    w.wait("tensor", f"{pfx}.{prev_tag}.ev.{i}.{6+m}")
                for k in range(KC):
                    w.op("tensor", lambda e, m=m, k=k, wname=wname, bank=bank: e.matmul(
                        bank[:, 0:512], sb[wname][:, k * 1024 + m * 128: k * 1024 + (m + 1) * 128],
                        sb["h"][:, k * 512:(k + 1) * 512],
                        start=(k == 0), stop=(k == KC - 1)),
                        inc=(TQ if k == KC - 1 else None),
                        label=(f"{pfx}.{tag}.mm.{i}.{m}" if k == KC - 1 else None))
                w.wait(evac_eng, f"{pfx}.{tag}.mm.{i}.{m}")
                semn = V if evac_eng == "vector" else C
                if evac_eng == "vector":
                    w.op("vector", lambda e, m=m, dst=dst, bank=bank: e.tensor_copy(
                        out=dst[:, m * 512:(m + 1) * 512], in_=bank[:, 0:512]),
                        inc=semn, label=f"{pfx}.{tag}.ev.{i}.{m}")
                else:
                    w.op("scalar", lambda e, m=m, dst=dst, bank=bank: e.copy(
                        out=dst[:, m * 512:(m + 1) * 512], in_=bank[:, 0:512]),
                        inc=semn, label=f"{pfx}.{tag}.ev.{i}.{m}")

        def proj_tm(dst, wname, evac_eng, tag, prev_tag):
            for c in range(NCH):
                for hg in range(2):
                    gi = c * 2 + hg
                    bank = ps[gi % 2]
                    if gi >= 2:
                        w.wait("tensor", f"{pfx}.{tag}.ev.{i}.{gi-2}")
                    elif prev_tag is not None:
                        w.wait("tensor", f"{pfx}.{prev_tag}.ev.{i}.{6+gi}")
                    for k in range(KC):
                        w.op("tensor", lambda e, c=c, hg=hg, k=k, wname=wname, bank=bank: e.matmul(
                            bank[:, 0:512],
                            sb["h"][:, k * 512 + c * 128: k * 512 + (c + 1) * 128],
                            sb[wname][:, k * 1024 + hg * 512: k * 1024 + (hg + 1) * 512],
                            start=(k == 0), stop=(k == KC - 1)),
                            inc=(TQ if k == KC - 1 else None),
                            label=(f"{pfx}.{tag}.mm.{i}.{gi}" if k == KC - 1 else None))
                    w.wait(evac_eng, f"{pfx}.{tag}.mm.{i}.{gi}")
                    semn = V if evac_eng == "vector" else C
                    dsl = dst[:, c * 1024 + hg * 512: c * 1024 + (hg + 1) * 512]
                    if evac_eng == "vector":
                        w.op("vector", lambda e, dsl=dsl, bank=bank: e.tensor_copy(
                            out=dsl, in_=bank[:, 0:512]),
                            inc=semn, label=f"{pfx}.{tag}.ev.{i}.{gi}")
                    else:
                        w.op("scalar", lambda e, dsl=dsl, bank=bank: e.copy(
                            out=dsl, in_=bank[:, 0:512]),
                            inc=semn, label=f"{pfx}.{tag}.ev.{i}.{gi}")

        if i >= 1:
            w.wait("tensor", f"{pfx}.res.{i-1}.{KC-2}")
            w.wait("tensor", f"{pfx}.res.{i-1}.{KC-1}")
        proj_fm(sb["q"], "wq", "scalar", "q", None)
        proj_fm(sb["k"], "wk", "vector", "k", "q")
        proj_tm(sb["kt"], "wk", "scalar", "kt", "k")
        proj_tm(sb["vt"], "wv", "vector", "vt", "kt")

        w.wait("tensor", f"{pfx}.q.ev.{i}.{KC-1}")
        w.wait("tensor", f"{pfx}.k.ev.{i}.{KC-1}")
        w.wait("tensor", f"{pfx}.kt.ev.{i}.7")
        w.wait("tensor", f"{pfx}.vt.ev.{i}.7")
        for c in range(NCH):
            first = seq_start and c == 0
            for g in range(2):
                gi = c * 2 + g
                b_at = ps[gi % 2]
                b_o = ps[2 + gi % 2]
                b_sd = ps[4 + gi % 2]
                heads = range(4 * g, 4 * g + 4)

                if gi >= 2:
                    w.wait("tensor", f"{pfx}.at.ev.{i}.{gi-2}")
                elif i >= 1:
                    w.wait("tensor", f"{pfx}.at.ev.{i-1}.{6+gi}")
                for hh, h in enumerate(heads):
                    qs = sb["q"][:, h * 512 + c * 128: h * 512 + (c + 1) * 128]
                    ks = sb["k"][:, h * 512 + c * 128: h * 512 + (c + 1) * 128]
                    w.op("tensor", lambda e, ks=ks, qs=qs, hh=hh, b_at=b_at: e.matmul(
                        b_at[:, hh * 128:(hh + 1) * 128], ks, qs, start=True, stop=True),
                        inc=(TQ if hh == 3 else None),
                        label=(f"{pfx}.at.mm.{i}.{gi}" if hh == 3 else None))
                w.wait("vector", f"{pfx}.at.mm.{i}.{gi}")
                w.op("vector", lambda e, g=g, b_at=b_at: e.tensor_mul(
                    out=sb["at_bf"][:, g * 512:(g + 1) * 512], in0=b_at[:, 0:512],
                    in1=cst["maskUT4"]),
                    inc=V, label=f"{pfx}.at.ev.{i}.{gi}")

                if gi >= 2:
                    w.wait("tensor", f"{pfx}.o.ev.{i}.{gi-2}")
                elif i >= 1:
                    w.wait("tensor", f"{pfx}.on.{i-1}.{6+gi}")
                w.wait("tensor", f"{pfx}.at.ev.{i}.{gi}")
                if not first:
                    w.wait("tensor", f"{pfx}.Sbf.{i}.{gi-2}" if c > 0 else f"{pfx}.Sbf.{i-1}.{6+g}")
                for hh, h in enumerate(heads):
                    vts = sb["vt"][:, c * 1024 + h * 128: c * 1024 + (h + 1) * 128]
                    qs = sb["q"][:, h * 512 + c * 128: h * 512 + (c + 1) * 128]
                    atb = sb["at_bf"][:, g * 512 + hh * 128: g * 512 + (hh + 1) * 128]
                    Ssl_b = sb["S_bf"][:, h * 128:(h + 1) * 128]
                    if first:
                        w.op("tensor", lambda e, vts=vts, atb=atb, hh=hh, b_o=b_o: e.matmul(
                            b_o[:, hh * 128:(hh + 1) * 128], vts, atb, start=True, stop=True),
                            inc=(TQ if hh == 3 else None),
                            label=(f"{pfx}.o.mm.{i}.{gi}" if hh == 3 else None))
                    else:
                        w.op("tensor", lambda e, vts=vts, atb=atb, hh=hh, b_o=b_o: e.matmul(
                            b_o[:, hh * 128:(hh + 1) * 128], vts, atb, start=True, stop=False))
                        w.op("tensor", lambda e, qs=qs, Ssl_b=Ssl_b, hh=hh, b_o=b_o: e.matmul(
                            b_o[:, hh * 128:(hh + 1) * 128], Ssl_b, qs, start=False, stop=True),
                            inc=(TQ if hh == 3 else None),
                            label=(f"{pfx}.o.mm.{i}.{gi}" if hh == 3 else None))
                w.wait("vector", f"{pfx}.o.mm.{i}.{gi}")
                if gi == 0 and i >= 1:
                    w.wait("vector", f"{pfx}.osq.{i-1}")
                for hh, h in enumerate(heads):
                    w.op("vector", lambda e, c=c, h=h, hh=hh, b_o=b_o: e.tensor_copy(
                        out=sb["o"][:, h * 512 + c * 128: h * 512 + (c + 1) * 128],
                        in_=b_o[:, hh * 128:(hh + 1) * 128]),
                        inc=(V if hh == 3 else None),
                        label=(f"{pfx}.o.ev.{i}.{gi}" if hh == 3 else None))

                if gi >= 2:
                    w.wait("tensor", f"{pfx}.Sadd.{i}.{gi-2}")
                elif i >= 1:
                    w.wait("tensor", f"{pfx}.ossn.{i-1}.{6+gi}")
                for hh, h in enumerate(heads):
                    kts = sb["kt"][:, c * 1024 + h * 128: c * 1024 + (h + 1) * 128]
                    vts = sb["vt"][:, c * 1024 + h * 128: c * 1024 + (h + 1) * 128]
                    w.op("tensor", lambda e, kts=kts, vts=vts, hh=hh, b_sd=b_sd: e.matmul(
                        b_sd[:, hh * 128:(hh + 1) * 128], kts, vts, start=True, stop=True),
                        inc=(TQ if hh == 3 else None),
                        label=(f"{pfx}.Sd.{i}.{gi}" if hh == 3 else None))
                w.wait("vector", f"{pfx}.Sd.{i}.{gi}")
                if not (seq_start and c == 0):
                    w.wait("vector", f"{pfx}.Sadd.{i}.{gi-2}" if c > 0 else f"{pfx}.Sadd.{i-1}.{6+g}")
                Sf = sb["S_f32"][:, g * 512:(g + 1) * 512]
                Sb = sb["S_bf"][:, g * 512:(g + 1) * 512]
                if first:
                    w.op("vector", lambda e, Sf=Sf, b_sd=b_sd: e.tensor_copy(
                        out=Sf, in_=b_sd[:, 0:512]),
                        inc=V, label=f"{pfx}.Sadd.{i}.{gi}")
                else:
                    w.op("vector", lambda e, Sf=Sf, b_sd=b_sd: e.tensor_add(
                        out=Sf, in0=Sf, in1=b_sd[:, 0:512]),
                        inc=V, label=f"{pfx}.Sadd.{i}.{gi}")
                w.wait("scalar", f"{pfx}.Sadd.{i}.{gi}")
                w.op("scalar", lambda e, Sf=Sf, Sb=Sb: e.copy(out=Sb, in_=Sf),
                    inc=C, label=f"{pfx}.Sbf.{i}.{gi}")

        w.wait("scalar", f"{pfx}.o.ev.{i}.7")
        w.wait("vector", f"{pfx}.o.ev.{i}.7")
        w.op("scalar", lambda e: e.activation(
            out=sb["xsq"][:], in_=sb["o"][:], func=AF.Square),
            inc=C, label=f"{pfx}.osq.{i}")
        w.wait("tensor", f"{pfx}.osq.{i}")
        for blk in range(8):
            b_ss = ps[4 + blk % 2]
            b_bc = ps[2 + blk % 2]
            if blk >= 2:
                w.wait("tensor", f"{pfx}.ossn.{i}.{blk-2}")
                w.wait("tensor", f"{pfx}.on.{i}.{blk-2}")
            else:
                w.wait("tensor", f"{pfx}.Sadd.{i}.{6+blk}")
                w.wait("tensor", f"{pfx}.o.ev.{i}.{6+blk}")
            w.op("tensor", lambda e, blk=blk, b_ss=b_ss: e.matmul(
                b_ss[0:1, 0:512], cst["ones_col"], sb["xsq"][:, blk * 512:(blk + 1) * 512],
                start=True, stop=True),
                inc=TQ, label=f"{pfx}.oss.{i}.{blk}")
            w.wait("vector", f"{pfx}.oss.{i}.{blk}")
            if blk >= 1:
                w.wait("vector", f"{pfx}.osqr.{i}.{blk-1}")
            else:
                w.wait("vector", f"{pfx}.sqr.{i}")
            w.op("vector", lambda e, b_ss=b_ss: e.tensor_scalar(
                out=sb["row_a"][0:1, :], in0=b_ss[0:1, 0:512], scalar1=1.0 / 128, scalar2=EPS,
                op0=ALU.mult, op1=ALU.add), inc=V, label=f"{pfx}.ossn.{i}.{blk}")
            w.wait("scalar", f"{pfx}.ossn.{i}.{blk}")
            w.op("scalar", lambda e: e.activation(
                out=sb["row_b"][0:1, :], in_=sb["row_a"][0:1, :], func=AF.Sqrt),
                inc=C, label=f"{pfx}.osqr.{i}.{blk}")
            w.wait("vector", f"{pfx}.osqr.{i}.{blk}")
            if blk >= 1:
                w.wait("vector", f"{pfx}.obc.{i}.{blk-1}")
            else:
                w.wait("vector", f"{pfx}.bc.{i}")
            w.op("vector", lambda e: e.reciprocal(
                out=sb["row_bf"][0:1, :], in_=sb["row_b"][0:1, :]),
                inc=V, label=f"{pfx}.orstd.{i}.{blk}")
            w.wait("tensor", f"{pfx}.orstd.{i}.{blk}")
            w.op("tensor", lambda e, b_bc=b_bc: e.matmul(
                b_bc[:, 0:512], cst["ones_row"], sb["row_bf"][0:1, 0:512],
                start=True, stop=True),
                inc=TQ, label=f"{pfx}.obc.{i}.{blk}")
            w.wait("vector", f"{pfx}.obc.{i}.{blk}")
            w.op("vector", lambda e, blk=blk, b_bc=b_bc: e.scalar_tensor_tensor(
                out=sb["on_bf"][:, blk * 512:(blk + 1) * 512],
                in0=sb["o"][:, blk * 512:(blk + 1) * 512],
                scalar=sb["onw"][:, 0:1], in1=b_bc[:, 0:512],
                op0=ALU.mult, op1=ALU.mult),
                inc=V, label=f"{pfx}.on.{i}.{blk}")

        w.wait("tensor", f"{pfx}.on.{i}.7")
        for m in range(KC):
            bank = ps[m % 2]
            if m >= 2:
                w.wait("tensor", f"{pfx}.res.{i}.{m-2}")
            else:
                w.wait("tensor", f"{pfx}.at.ev.{i}.{6+m}")
            for h in range(KC):
                w.op("tensor", lambda e, m=m, h=h, bank=bank: e.matmul(
                    bank[:, 0:512], sb["wo"][:, h * 1024 + m * 128: h * 1024 + (m + 1) * 128],
                    sb["on_bf"][:, h * 512:(h + 1) * 512],
                    start=(h == 0), stop=(h == KC - 1)),
                    inc=(TQ if h == KC - 1 else None),
                    label=(f"{pfx}.wo.{i}.{m}" if h == KC - 1 else None))
            w.wait("vector", f"{pfx}.wo.{i}.{m}")
            w.op("vector", lambda e, m=m, xb=xb, bank=bank: e.tensor_add(
                out=xb[:, m * 512:(m + 1) * 512], in0=xb[:, m * 512:(m + 1) * 512],
                in1=bank[:, 0:512]),
                inc=V, label=f"{pfx}.res.{i}.{m}")

        for kc in range(KC):
            w.wait("gpsimd", f"{pfx}.res.{i}.{kc}")
            w.op("gpsimd", lambda e, kc=kc, t0=t0, xb=xb: e.dma_start(
                out=xout[kc * 128:(kc + 1) * 128, t0:t0 + TILE],
                in_=xb[:, kc * 512:(kc + 1) * 512]),
                inc=f"S{i % 2}", k=16, label=f"{pfx}.st.{i}.{kc}")

    w.wait_count("gpsimd", "S0", 16 * KC * (NTILE // 2))
    w.wait_count("gpsimd", "S1", 16 * KC * (NTILE // 2))


def postln_tile(w, pfx, i, sb, ps, cst, xb):
    V, C, TQ = w.V, w.C, w.TQ
    w.op("scalar", lambda e: e.copy(out=sb["xbf"][:], in_=xb[:]),
         inc=C, label=f"{pfx}.pl.cast.{i}")
    w.op("scalar", lambda e: e.activation(out=sb["xsq"][:], in_=xb[:], func=AF.Square),
         inc=C, label=f"{pfx}.pl.sq.{i}")
    w.wait("tensor", f"{pfx}.pl.sq.{i}")
    for kc in range(KC):
        w.op("tensor", lambda e, kc=kc: e.matmul(
            ps[4][0:1, 0:512], cst["ones_col"], sb["xbf"][:, kc * 512:(kc + 1) * 512],
            start=(kc == 0), stop=(kc == KC - 1)),
            inc=(TQ if kc == KC - 1 else None),
            label=(f"{pfx}.pl.sum.{i}" if kc == KC - 1 else None))
    for kc in range(KC):
        w.op("tensor", lambda e, kc=kc: e.matmul(
            ps[5][0:1, 0:512], cst["ones_col"], sb["xsq"][:, kc * 512:(kc + 1) * 512],
            start=(kc == 0), stop=(kc == KC - 1)),
            inc=(TQ if kc == KC - 1 else None),
            label=(f"{pfx}.pl.ss.{i}" if kc == KC - 1 else None))
    w.wait("vector", f"{pfx}.pl.ss.{i}")
    w.op("vector", lambda e: e.tensor_scalar(
        out=sb["row_mu"][0:1, :], in0=ps[4][0:1, 0:512], scalar1=1.0 / D, scalar2=None,
        op0=ALU.mult))
    w.op("vector", lambda e: e.tensor_scalar(
        out=sb["row_a"][0:1, :], in0=ps[5][0:1, 0:512], scalar1=1.0 / D, scalar2=None,
        op0=ALU.mult))
    w.drain("vector")
    w.op("vector", lambda e: e.tensor_mul(
        out=sb["row_c"][0:1, :], in0=sb["row_mu"][0:1, :], in1=sb["row_mu"][0:1, :]))
    w.op("vector", lambda e: e.tensor_copy(out=sb["row_mu_bf"][0:1, :], in_=sb["row_mu"][0:1, :]))
    w.drain("vector")
    w.op("vector", lambda e: e.tensor_sub(
        out=sb["row_b"][0:1, :], in0=sb["row_a"][0:1, :], in1=sb["row_c"][0:1, :]),
        inc=V, label=f"{pfx}.pl.var.{i}")
    w.wait("scalar", f"{pfx}.pl.var.{i}")
    w.op("scalar", lambda e: e.activation(
        out=sb["row_a"][0:1, :], in_=sb["row_b"][0:1, :], func=AF.Sqrt, bias=cst["eps11"][0:1, 0:1]),
        inc=C, label=f"{pfx}.pl.sqr.{i}")
    w.wait("vector", f"{pfx}.pl.sqr.{i}")
    w.op("vector", lambda e: e.reciprocal(
        out=sb["row_bf"][0:1, :], in_=sb["row_a"][0:1, :]),
        inc=V, label=f"{pfx}.pl.rstd.{i}")
    w.wait("tensor", f"{pfx}.pl.rstd.{i}")
    w.op("tensor", lambda e: e.matmul(
        ps[6][:, 0:512], cst["ones_row"], sb["row_mu_bf"][0:1, 0:512], start=True, stop=True),
        inc=TQ, label=f"{pfx}.pl.mubc.{i}")
    w.op("tensor", lambda e: e.matmul(
        ps[7][:, 0:512], cst["ones_row"], sb["row_bf"][0:1, 0:512], start=True, stop=True),
        inc=TQ, label=f"{pfx}.pl.rsbc.{i}")
    w.wait("vector", f"{pfx}.pl.rsbc.{i}")
    for kc in range(KC):
        w.op("vector", lambda e, kc=kc, xb=xb: e.tensor_sub(
            out=sb["tmp2"][:, kc * 512:(kc + 1) * 512],
            in0=xb[:, kc * 512:(kc + 1) * 512], in1=ps[6][:, 0:512]),
            inc=(V if kc == KC - 1 else None),
            label=(f"{pfx}.pl.sub.{i}" if kc == KC - 1 else None))
    w.wait("vector", f"{pfx}.pl.sub.{i}")
    for kc in range(KC):
        w.op("vector", lambda e, kc=kc, xb=xb: e.tensor_mul(
            out=xb[:, kc * 512:(kc + 1) * 512], in0=sb["tmp2"][:, kc * 512:(kc + 1) * 512],
            in1=ps[7][:, 0:512]),
            inc=(V if kc == KC - 1 else None),
            label=(f"{pfx}.pl.app.{i}" if kc == KC - 1 else None))


def mlp_emit(w, nc, bufs, xin, xout, wbf, smalls, cfg, pfx, xin0=None):
    sb = bufs["sb"]
    ps = bufs["ps"]
    cst = bufs["cst"]
    TQ, V, C = w.TQ, w.V, w.C
    swiglu = cfg["kind"] == "swiglu"
    NQ = 4
    if swiglu:
        parcels = []
        for q in range(NQ):
            parcels += [("wg", q), ("wu", q)]
        parcels += [("wd", q) for q in range(NQ)]
    else:
        parcels = [("wg", q) for q in range(NQ)] + [("wd", q) for q in range(NQ)]
    NP = len(parcels)

    def pdone_label(ii, jj):
        name, q = parcels[jj]
        if name == "wg":
            return f"{pfx}.g.mm.{ii}.{q*8+7}"
        if name == "wu":
            return f"{pfx}.u.mm.{ii}.{q*8+7}"
        return f"{pfx}.dq.{ii}.{q}"

    for name, ap in smalls.items():
        w.op("sync", lambda e, name=name, ap=ap: e.dma_start(
            out=sb[name][:, :], in_=ap), inc="Lw", k=16, label=f"{pfx}.w.{name}")
    last_small = list(smalls.keys())[-1]
    w.wait("vector", f"{pfx}.w.{last_small}")
    w.wait("scalar", f"{pfx}.w.{last_small}")
    if not swiglu:
        w.op("vector", lambda e: e.tensor_scalar(
            out=sb["b2h"][:, :], in0=sb["b2"][:, :], scalar1=0.5, scalar2=None, op0=ALU.mult),
            inc=V, label=f"{pfx}.b2h")
        w.wait("scalar", f"{pfx}.b2h")

    def parcel_load(i, j):
        P = i * NP + j
        name, q = parcels[j]
        slot = sb["wslot"][P % 4]
        if P >= 4:
            jj = (P - 4) % NP
            ii = (P - 4) // NP
            w.wait("sync", pdone_label(ii, jj))
        lsem = f"Lp{P % 4}"
        if name == "wd":
            srcw = wbf["wd"]
            for kk in range(8):
                w.op("sync", lambda e, kk=kk, q=q, slot=slot, srcw=srcw: e.dma_start(
                    out=slot[:, kk * 1024:(kk + 1) * 1024],
                    in_=srcw[q * 1024 + kk * 128: q * 1024 + (kk + 1) * 128, :]),
                    inc=lsem, k=16, label=(f"{pfx}.pld.{i}.{j}" if kk == 7 else None))
        else:
            srcw = wbf[name]
            for kk in range(8):
                w.op("sync", lambda e, kk=kk, q=q, slot=slot, srcw=srcw: e.dma_start(
                    out=slot[:, kk * 1024:(kk + 1) * 1024],
                    in_=srcw[kk * 128:(kk + 1) * 128, q * 1024:(q + 1) * 1024]),
                    inc=lsem, k=16, label=(f"{pfx}.pld.{i}.{j}" if kk == 7 else None))

    for i in range(NTILE):
        t0 = i * TILE
        xb = sb["x0"] if i % 2 == 0 else sb["x1"]

        if i >= 2:
            w.wait("sync", f"{pfx}.st.{i-2}.{KC-1}")
        for kc in range(KC):
            w.op("sync", lambda e, kc=kc, t0=t0, xb=xb: e.dma_start(
                out=xb[:, kc * 512:(kc + 1) * 512],
                in_=xin[kc * 128:(kc + 1) * 128, t0:t0 + TILE]),
                inc=f"Lx{i % 2}", k=16, label=f"{pfx}.x.{i}.{kc}")
        if xin0 is not None:
            if i >= 1:
                w.wait("sync", f"{pfx}.res2.{i-1}")
            for kc in range(KC):
                w.op("sync", lambda e, kc=kc, t0=t0: e.dma_start(
                    out=sb["x0s"][:, kc * 512:(kc + 1) * 512],
                    in_=xin0[kc * 128:(kc + 1) * 128, t0:t0 + TILE]),
                    inc="Lx2", k=16, label=f"{pfx}.x0.{i}.{kc}")

        w.wait("scalar", f"{pfx}.x.{i}.{KC-1}")
        if cfg["pre"] == "rms":
            w.op("scalar", lambda e, xb=xb: e.activation(
                out=sb["xsq"][:], in_=xb[:], func=AF.Square),
                inc=C, label=f"{pfx}.sq.{i}")
            w.wait("tensor", f"{pfx}.sq.{i}")
            if i >= 1:
                w.wait("tensor", f"{pfx}.pl.app.{i-1}")
                w.wait("tensor", f"{pfx}.h.{i-1}")
            for kc in range(KC):
                w.op("tensor", lambda e, kc=kc: e.matmul(
                    ps[6][0:1, 0:512], cst["ones_col"], sb["xsq"][:, kc * 512:(kc + 1) * 512],
                    start=(kc == 0), stop=(kc == KC - 1)),
                    inc=(TQ if kc == KC - 1 else None),
                    label=(f"{pfx}.ss.{i}" if kc == KC - 1 else None))
            w.wait("vector", f"{pfx}.ss.{i}")
            rstd_row(w, pfx, i, "", ps[6], sb, 1.0 / D)
            w.wait("tensor", f"{pfx}.rstd.{i}")
            w.op("tensor", lambda e: e.matmul(
                ps[6][:, 0:512], cst["ones_row"], sb["row_bf"][0:1, 0:512], start=True, stop=True),
                inc=TQ, label=f"{pfx}.bc.{i}")
            w.wait("vector", f"{pfx}.bc.{i}")
            if i >= 1:
                w.wait("vector", f"{pfx}.u.mm.{i-1}.31")
            for kc in range(KC):
                w.op("vector", lambda e, kc=kc, xb=xb: e.scalar_tensor_tensor(
                    out=sb["h"][:, kc * 512:(kc + 1) * 512], in0=xb[:, kc * 512:(kc + 1) * 512],
                    scalar=sb["nw"][:, kc:kc + 1], in1=ps[6][:, 0:512],
                    op0=ALU.mult, op1=ALU.mult),
                    inc=(V if kc == KC - 1 else None),
                    label=(f"{pfx}.h.{i}" if kc == KC - 1 else None))
        else:
            w.op("scalar", lambda e, xb=xb: e.copy(out=sb["xbf"][:], in_=xb[:]),
                 inc=C, label=f"{pfx}.cast.{i}")
            w.op("scalar", lambda e, xb=xb: e.activation(
                out=sb["xsq"][:], in_=xb[:], func=AF.Square),
                inc=C, label=f"{pfx}.sq.{i}")
            w.wait("tensor", f"{pfx}.sq.{i}")
            if i >= 1:
                w.wait("tensor", f"{pfx}.pl.app.{i-1}")
                w.wait("tensor", f"{pfx}.h.{i-1}")
            for kc in range(KC):
                w.op("tensor", lambda e, kc=kc: e.matmul(
                    ps[4][0:1, 0:512], cst["ones_col"], sb["xbf"][:, kc * 512:(kc + 1) * 512],
                    start=(kc == 0), stop=(kc == KC - 1)),
                    inc=(TQ if kc == KC - 1 else None),
                    label=(f"{pfx}.lnsum.{i}" if kc == KC - 1 else None))
            for kc in range(KC):
                w.op("tensor", lambda e, kc=kc: e.matmul(
                    ps[5][0:1, 0:512], cst["ones_col"], sb["xsq"][:, kc * 512:(kc + 1) * 512],
                    start=(kc == 0), stop=(kc == KC - 1)),
                    inc=(TQ if kc == KC - 1 else None),
                    label=(f"{pfx}.lnss.{i}" if kc == KC - 1 else None))
            w.wait("vector", f"{pfx}.lnss.{i}")
            w.op("vector", lambda e: e.tensor_scalar(
                out=sb["row_mu"][0:1, :], in0=ps[4][0:1, 0:512], scalar1=1.0 / D, scalar2=None,
                op0=ALU.mult))
            w.op("vector", lambda e: e.tensor_scalar(
                out=sb["row_a"][0:1, :], in0=ps[5][0:1, 0:512], scalar1=1.0 / D, scalar2=None,
                op0=ALU.mult))
            w.drain("vector")
            w.op("vector", lambda e: e.tensor_mul(
                out=sb["row_c"][0:1, :], in0=sb["row_mu"][0:1, :], in1=sb["row_mu"][0:1, :]))
            w.op("vector", lambda e: e.tensor_copy(out=sb["row_mu_bf"][0:1, :], in_=sb["row_mu"][0:1, :]))
            w.drain("vector")
            w.op("vector", lambda e: e.tensor_sub(
                out=sb["row_b"][0:1, :], in0=sb["row_a"][0:1, :], in1=sb["row_c"][0:1, :]),
                inc=V, label=f"{pfx}.lnvar.{i}")
            w.wait("scalar", f"{pfx}.lnvar.{i}")
            w.op("scalar", lambda e: e.activation(
                out=sb["row_a"][0:1, :], in_=sb["row_b"][0:1, :], func=AF.Sqrt,
                bias=cst["eps11"][0:1, 0:1]),
                inc=C, label=f"{pfx}.lnsqr.{i}")
            w.wait("vector", f"{pfx}.lnsqr.{i}")
            w.op("vector", lambda e: e.reciprocal(
                out=sb["row_bf"][0:1, :], in_=sb["row_a"][0:1, :]),
                inc=V, label=f"{pfx}.lnrstd.{i}")
            w.wait("tensor", f"{pfx}.lnrstd.{i}")
            w.op("tensor", lambda e: e.matmul(
                ps[6][:, 0:512], cst["ones_row"], sb["row_mu_bf"][0:1, 0:512], start=True, stop=True),
                inc=TQ, label=f"{pfx}.lnmubc.{i}")
            w.op("tensor", lambda e: e.matmul(
                ps[7][:, 0:512], cst["ones_row"], sb["row_bf"][0:1, 0:512], start=True, stop=True),
                inc=TQ, label=f"{pfx}.lnrsbc.{i}")
            w.wait("vector", f"{pfx}.lnrsbc.{i}")
            for kc in range(KC):
                w.op("vector", lambda e, kc=kc, xb=xb: e.tensor_sub(
                    out=sb["tmp2"][:, kc * 512:(kc + 1) * 512],
                    in0=xb[:, kc * 512:(kc + 1) * 512], in1=ps[6][:, 0:512]),
                    inc=(V if kc == KC - 1 else None),
                    label=(f"{pfx}.lnsub.{i}" if kc == KC - 1 else None))
            w.wait("vector", f"{pfx}.lnsub.{i}")
            for kc in range(KC):
                w.op("vector", lambda e, kc=kc: e.tensor_mul(
                    out=sb["tmp2"][:, kc * 512:(kc + 1) * 512],
                    in0=sb["tmp2"][:, kc * 512:(kc + 1) * 512],
                    in1=ps[7][:, 0:512]),
                    inc=(V if kc == KC - 1 else None),
                    label=(f"{pfx}.lnz.{i}" if kc == KC - 1 else None))
            w.wait("vector", f"{pfx}.lnz.{i}")
            if i >= 1:
                w.wait("vector", f"{pfx}.g.mm.{i-1}.31")
            for kc in range(KC):
                w.op("vector", lambda e, kc=kc: e.tensor_scalar(
                    out=sb["h"][:, kc * 512:(kc + 1) * 512],
                    in0=sb["tmp2"][:, kc * 512:(kc + 1) * 512],
                    scalar1=sb["gamma"][:, kc:kc + 1], scalar2=sb["beta"][:, kc:kc + 1],
                    op0=ALU.mult, op1=ALU.add),
                    inc=(V if kc == KC - 1 else None),
                    label=(f"{pfx}.h.{i}" if kc == KC - 1 else None))

        dbg = cfg.get("dbg")
        if dbg is not None and i == 0:
            for kc in range(KC):
                w.op("gpsimd", lambda e, kc=kc: e.dma_start(
                    out=dbg["h"][:, kc * 512:(kc + 1) * 512],
                    in_=sb["h"][:, kc * 512:(kc + 1) * 512]),
                    inc="S1", k=16)

        w.wait("tensor", f"{pfx}.h.{i}")
        for q in range(NQ):
            jg = q * 2 if swiglu else q
            parcel_load(i, jg)
            if swiglu:
                parcel_load(i, jg + 1)
            slot_g = sb["wslot"][(i * NP + jg) % 4]
            slot_u = sb["wslot"][(i * NP + jg + 1) % 4] if swiglu else None
            w.wait("tensor", f"{pfx}.pld.{i}.{jg}")
            if swiglu:
                w.wait("tensor", f"{pfx}.pld.{i}.{jg+1}")
            for ii in range(8):
                gi = q * 8 + ii
                bank_g = ps[gi % 2]
                bank_u = ps[2 + gi % 2] if swiglu else None
                if gi >= 2:
                    w.wait("tensor", f"{pfx}.tgu.{i}.{gi-2}")
                elif i >= 1:
                    w.wait("tensor", f"{pfx}.pl.app.{i-1}")
                for k in range(KC):
                    w.op("tensor", lambda e, k=k, ii=ii, slot_g=slot_g, bank_g=bank_g: e.matmul(
                        bank_g[:, 0:512], slot_g[:, k * 1024 + ii * 128: k * 1024 + (ii + 1) * 128],
                        sb["h"][:, k * 512:(k + 1) * 512],
                        start=(k == 0), stop=(k == KC - 1)),
                        inc=(TQ if k == KC - 1 else None),
                        label=(f"{pfx}.g.mm.{i}.{gi}" if k == KC - 1 else None))
                if swiglu:
                    for k in range(KC):
                        w.op("tensor", lambda e, k=k, ii=ii, slot_u=slot_u, bank_u=bank_u: e.matmul(
                            bank_u[:, 0:512], slot_u[:, k * 1024 + ii * 128: k * 1024 + (ii + 1) * 128],
                            sb["h"][:, k * 512:(k + 1) * 512],
                            start=(k == 0), stop=(k == KC - 1)),
                            inc=(TQ if k == KC - 1 else None),
                            label=(f"{pfx}.u.mm.{i}.{gi}" if k == KC - 1 else None))
                if swiglu:
                    w.wait("scalar", f"{pfx}.g.mm.{i}.{gi}")
                    if gi >= 2:
                        w.wait("scalar", f"{pfx}.tgu.{i}.{gi-2}")
                    sgb = sb["sg0"] if gi % 2 == 0 else sb["sg1"]
                    w.op("scalar", lambda e, sgb=sgb, bank_g=bank_g: e.activation(
                        out=sgb[:, 0:512], in_=bank_g[:, 0:512], func=AF.Silu),
                        inc=C, label=f"{pfx}.sg.{i}.{gi}")
                    w.wait("vector", f"{pfx}.sg.{i}.{gi}")
                    w.wait("vector", f"{pfx}.u.mm.{i}.{gi}")
                    if gi <= 1 and i >= 1:
                        w.wait("vector", f"{pfx}.dq.{i-1}.3")
                    w.op("vector", lambda e, gi=gi, sgb=sgb, bank_u=bank_u: e.tensor_mul(
                        out=sb["tgu"][:, gi * 512:(gi + 1) * 512], in0=sgb[:, 0:512],
                        in1=bank_u[:, 0:512]),
                        inc=V, label=f"{pfx}.tgu.{i}.{gi}")
                else:
                    w.wait("scalar", f"{pfx}.g.mm.{i}.{gi}")
                    if gi <= 1 and i >= 1:
                        w.wait("scalar", f"{pfx}.dq.{i-1}.3")
                    w.op("scalar", lambda e, gi=gi, bank_g=bank_g: e.activation(
                        out=sb["tgu"][:, gi * 512:(gi + 1) * 512], in_=bank_g[:, 0:512],
                        func=AF.Silu, bias=sb["b1"][:, gi:gi + 1]),
                        inc=C, label=f"{pfx}.tgu.{i}.{gi}")

        if dbg is not None and i == 0:
            for b in range(4):
                w.op("gpsimd", lambda e, b=b: e.dma_start(
                    out=dbg["tgu"][:, b * 4096:(b + 1) * 4096],
                    in_=sb["tgu"][:, b * 4096:(b + 1) * 4096]),
                    inc="S1", k=16)

        tgudone = f"{pfx}.tgu.{i}.31"
        w.wait("tensor", tgudone)
        dnlab = f"{pfx}.dq.{i}.3"
        if i >= 1:
            w.wait("tensor", f"{pfx}.pl.app.{i-1}")
            w.wait("tensor", f"{pfx}.res.{i-1}.{KC-1}")
        for q in range(NQ):
            jd = (2 * NQ if swiglu else NQ) + q
            parcel_load(i, jd)
            slot_d = sb["wslot"][(i * NP + jd) % 4]
            w.wait("tensor", f"{pfx}.pld.{i}.{jd}")
            for ii in range(8):
                gi = q * 8 + ii
                for o in range(KC):
                    w.op("tensor", lambda e, ii=ii, o=o, gi=gi, slot_d=slot_d: e.matmul(
                        ps[o][:, 0:512], slot_d[:, ii * 1024 + o * 128: ii * 1024 + (o + 1) * 128],
                        sb["tgu"][:, gi * 512:(gi + 1) * 512],
                        start=(gi == 0), stop=(gi == 31)),
                        inc=(TQ if (ii == 7 and o == KC - 1) else None),
                        label=(f"{pfx}.dq.{i}.{q}" if (ii == 7 and o == KC - 1) else None))
        w.wait("vector", f"{pfx}.dq.{i}.3")
        w.wait("scalar", f"{pfx}.dq.{i}.3")
        for o in range(KC):
            if swiglu:
                w.op("vector", lambda e, o=o, xb=xb: e.tensor_add(
                    out=xb[:, o * 512:(o + 1) * 512], in0=xb[:, o * 512:(o + 1) * 512],
                    in1=ps[o][:, 0:512]),
                    inc=V, label=f"{pfx}.res.{i}.{o}")
            else:
                w.op("vector", lambda e, o=o: e.tensor_scalar(
                    out=sb["tmp2"][:, o * 512:(o + 1) * 512], in0=ps[o][:, 0:512],
                    scalar1=0.5, scalar2=sb["b2h"][:, o:o + 1],
                    op0=ALU.mult, op1=ALU.add),
                    inc=V, label=f"{pfx}.zh.{i}.{o}")
        if not swiglu:
            w.wait("vector", f"{pfx}.zh.{i}.{KC-1}")
            for o in range(KC):
                w.op("vector", lambda e, o=o, xb=xb: e.scalar_tensor_tensor(
                    out=xb[:, o * 512:(o + 1) * 512], in0=xb[:, o * 512:(o + 1) * 512],
                    scalar=2.0, in1=sb["tmp2"][:, o * 512:(o + 1) * 512],
                    op0=ALU.mult, op1=ALU.add),
                    inc=V, label=f"{pfx}.res.{i}.{o}")

        if dbg is not None and i == 0:
            for kc in range(KC):
                w.op("gpsimd", lambda e, kc=kc, xb=xb: e.dma_start(
                    out=dbg["xres"][:, kc * 512:(kc + 1) * 512],
                    in_=xb[:, kc * 512:(kc + 1) * 512]),
                    inc="S1", k=16)

        if xin0 is not None:
            w.wait("vector", f"{pfx}.x0.{i}.{KC-1}")
            w.wait("vector", f"{pfx}.res.{i}.{KC-1}")
            for o in range(KC):
                w.op("vector", lambda e, o=o, xb=xb: e.tensor_add(
                    out=xb[:, o * 512:(o + 1) * 512], in0=xb[:, o * 512:(o + 1) * 512],
                    in1=sb["x0s"][:, o * 512:(o + 1) * 512]),
                    inc=(V if o == KC - 1 else None),
                    label=(f"{pfx}.res2.{i}" if o == KC - 1 else None))
            resdone = f"{pfx}.res2.{i}"
        else:
            resdone = f"{pfx}.res.{i}.{KC-1}"

        w.wait("scalar", resdone)
        w.wait("tensor", resdone)
        postln_tile(w, pfx, i, sb, ps, cst, xb)

        w.wait("gpsimd", f"{pfx}.pl.app.{i}")
        for kc in range(KC):
            w.op("gpsimd", lambda e, kc=kc, t0=t0, xb=xb: e.dma_start(
                out=xout[kc * 128:(kc + 1) * 128, t0:t0 + TILE],
                in_=xb[:, kc * 512:(kc + 1) * 512]),
                inc=f"S{i % 2}", k=16, label=f"{pfx}.st.{i}.{kc}")

    w.wait_count("gpsimd", "S0", 16 * KC * (NTILE // 2))
    w.wait_count("gpsimd", "S1", 16 * KC * (NTILE // 2))


def build_p1_program():
    nc = bass.Bass("TRN2", target_bir_lowering=False)
    nc.ctx.enter_context(nc.allow_low_precision(
        reason="bf16 matmul operands with fp32 accumulation; intentional"))
    xin = nc.dram_tensor("xin", [D, NTOK], F32, kind="ExternalInput").ap()
    xout = nc.dram_tensor("xout", [D, NTOK], F32, kind="ExternalOutput").ap()
    wbf = {n: nc.dram_tensor(n, [D, D], BF16, kind="ExternalInput").ap()
           for n in ("wq", "wk", "wv", "wo")}
    smalls = {
        "attn_norm_w": nc.dram_tensor("attn_norm_w", [128, KC], F32, kind="ExternalInput").ap(),
        "out_norm_w": nc.dram_tensor("out_norm_w", [128, 1], F32, kind="ExternalInput").ap(),
    }
    consts = build_constants(nc)

    with ExitStack() as stack:
        E = stack.enter_context

        def sbuf(name, shape, dt):
            return E(nc.sbuf_tensor(name, shape, dt))[:, :]

        sb = {}
        sb["x0"] = sbuf("x0", [128, KC * 512], F32)
        sb["x1"] = sbuf("x1", [128, KC * 512], F32)
        for n in ("wq", "wk", "wv", "wo"):
            sb[n] = sbuf(n + "_sb", [128, KC * 1024], BF16)
        sb["h"] = sbuf("h_sb", [128, KC * 512], BF16)
        sb["xsq"] = sbuf("xsq_sb", [128, KC * 512], BF16)
        sb["q"] = sbuf("q_sb", [128, 8 * 512], BF16)
        sb["k"] = sbuf("k_sb", [128, 8 * 512], BF16)
        sb["kt"] = sbuf("kt_sb", [128, NCH * 1024], BF16)
        sb["vt"] = sbuf("vt_sb", [128, NCH * 1024], BF16)
        sb["o"] = sbuf("o_sb", [128, 8 * 512], F32)
        sb["on_bf"] = sbuf("on_sb", [128, 8 * 512], BF16)
        sb["at_bf"] = sbuf("at_sb", [128, 1024], BF16)
        sb["S_f32"] = sbuf("S_f32", [128, 8 * 128], F32)
        sb["S_bf"] = sbuf("S_bf", [128, 8 * 128], BF16)
        sb["nw1"] = sbuf("nw1", [128, KC], F32)
        sb["onw"] = sbuf("onw", [128, 1], F32)
        sb["row_a"] = sbuf("row_a", [1, 512], F32)
        sb["row_b"] = sbuf("row_b", [1, 512], F32)
        sb["row_bf"] = sbuf("row_bf", [1, 512], BF16)

        cst = {}
        cst["ones_col"] = sbuf("ones_col_sb", [128, 1], BF16)
        cst["ones_row"] = sbuf("ones_row_sb", [1, 128], BF16)
        cst["maskUT4"] = sbuf("maskUT4_sb", [128, 512], F32)

        ps = [E(nc.psum_tensor(f"psb{j}", [128, 512], F32))[:, :] for j in range(7)]

        semnames = ("Lw", "Lx0", "Lx1", "S0", "S1", "T", "V", "C")
        sems = {name: E(nc.semaphore(name)) for name in semnames}
        block = E(nc.Block())

        w = Walk(sems)
        w.TQ, w.V, w.C = "T", "V", "C"
        bufs = {"sb": sb, "ps": ps, "cst": cst}

        def full(walk):
            walk.op("sync", lambda e: e.dma_start(out=cst["ones_col"], in_=consts["ones_col_bf"]),
                    inc="Lw", k=16)
            walk.op("sync", lambda e: e.dma_start(out=cst["ones_row"], in_=consts["ones_row_bf"]),
                    inc="Lw", k=16)
            walk.op("sync", lambda e: e.dma_start(out=cst["maskUT4"], in_=consts["maskUT4_f32"]),
                    inc="Lw", k=16, label="consts")
            p1_emit(walk, nc, bufs, xin, xout, wbf, smalls)

        w.plan()
        full(w)

        @block.sync
        def _(q):
            w.start_emit("sync", q)
            full(w)

        @block.tensor
        def _(t):
            w.start_emit("tensor", t)
            full(w)

        @block.vector
        def _(v):
            w.start_emit("vector", v)
            full(w)

        @block.scalar
        def _(s):
            w.start_emit("scalar", s)
            full(w)

        @block.gpsimd
        def _(g):
            w.start_emit("gpsimd", g)
            full(w)

    return nc


def p4_emit(w, nc, bufs, xin, xout, wbf, smalls, pfx="p4"):
    sb = bufs["sb"]
    ps = bufs["ps"]
    cst = bufs["cst"]
    TQ, V, C = w.TQ, w.V, w.C

    for name in ("wq", "wk", "wv", "wo"):
        for k in range(KC):
            w.op("sync", lambda e, n=name, k=k: e.dma_start(
                out=sb[n][:, k * 1024:(k + 1) * 1024], in_=wbf[n][k * 128:(k + 1) * 128, :]),
                inc="Lw", k=16)
    for k in range(KC):
        w.op("sync", lambda e, k=k: e.dma_start(
            out=sb["wg"][:, k * 16:(k + 1) * 16], in_=smalls["wg"][k * 128:(k + 1) * 128, :]),
            inc="Lw", k=16)
    w.op("sync", lambda e: e.dma_start(out=sb["bg"][:, :], in_=smalls["bg"]),
         inc="Lw", k=16, label=f"{pfx}.w.done")
    w.wait("tensor", f"{pfx}.w.done")
    w.wait("vector", f"{pfx}.w.done")
    w.wait("scalar", f"{pfx}.w.done")

    for i in range(NTILE):
        seq_start = (i % TPS == 0)
        t0 = i * TILE
        xb = sb["x0"] if i % 2 == 0 else sb["x1"]

        if i >= 2:
            w.wait("sync", f"{pfx}.st.{i-2}.{KC-1}")
        for kc in range(KC):
            w.op("sync", lambda e, kc=kc, t0=t0, xb=xb: e.dma_start(
                out=xb[:, kc * 512:(kc + 1) * 512],
                in_=xin[kc * 128:(kc + 1) * 128, t0:t0 + TILE]),
                inc=f"Lx{i % 2}", k=16, label=f"{pfx}.x.{i}.{kc}")
        w.wait("scalar", f"{pfx}.x.{i}.{KC-1}")
        if i >= 1:
            w.wait("scalar", f"{pfx}.vt.mm.{i-1}.7")
        w.op("scalar", lambda e, xb=xb: e.copy(out=sb["xbf"][:], in_=xb[:]),
             inc=C, label=f"{pfx}.cast.{i}")

        w.wait("tensor", f"{pfx}.cast.{i}")
        if i >= 1:
            w.wait("tensor", f"{pfx}.res.{i-1}.{KC-2}")
            w.wait("tensor", f"{pfx}.res.{i-1}.{KC-1}")

        def proj_fm(dst, wname, evac_eng, tag, prev_tag):
            for m in range(KC):
                bank = ps[m % 2]
                if m >= 2:
                    w.wait("tensor", f"{pfx}.{tag}.ev.{i}.{m-2}")
                elif prev_tag is not None:
                    w.wait("tensor", f"{pfx}.{prev_tag}.ev.{i}.{6+m}")
                for k in range(KC):
                    w.op("tensor", lambda e, m=m, k=k, wname=wname, bank=bank: e.matmul(
                        bank[:, 0:512], sb[wname][:, k * 1024 + m * 128: k * 1024 + (m + 1) * 128],
                        sb["xbf"][:, k * 512:(k + 1) * 512],
                        start=(k == 0), stop=(k == KC - 1)),
                        inc=(TQ if k == KC - 1 else None),
                        label=(f"{pfx}.{tag}.mm.{i}.{m}" if k == KC - 1 else None))
                w.wait(evac_eng, f"{pfx}.{tag}.mm.{i}.{m}")
                semn = V if evac_eng == "vector" else C
                if evac_eng == "vector":
                    w.op("vector", lambda e, m=m, dst=dst, bank=bank: e.tensor_copy(
                        out=dst[:, m * 512:(m + 1) * 512], in_=bank[:, 0:512]),
                        inc=semn, label=f"{pfx}.{tag}.ev.{i}.{m}")
                else:
                    w.op("scalar", lambda e, m=m, dst=dst, bank=bank: e.copy(
                        out=dst[:, m * 512:(m + 1) * 512], in_=bank[:, 0:512]),
                        inc=semn, label=f"{pfx}.{tag}.ev.{i}.{m}")

        def proj_tm(dst, wname, evac_eng, tag, prev_tag):
            for c in range(NCH):
                for hg in range(2):
                    gi = c * 2 + hg
                    bank = ps[gi % 2]
                    if gi >= 2:
                        w.wait("tensor", f"{pfx}.{tag}.ev.{i}.{gi-2}")
                    elif prev_tag is not None:
                        w.wait("tensor", f"{pfx}.{prev_tag}.ev.{i}.{6+gi}")
                    for k in range(KC):
                        w.op("tensor", lambda e, c=c, hg=hg, k=k, wname=wname, bank=bank: e.matmul(
                            bank[:, 0:512],
                            sb["xbf"][:, k * 512 + c * 128: k * 512 + (c + 1) * 128],
                            sb[wname][:, k * 1024 + hg * 512: k * 1024 + (hg + 1) * 512],
                            start=(k == 0), stop=(k == KC - 1)),
                            inc=(TQ if k == KC - 1 else None),
                            label=(f"{pfx}.{tag}.mm.{i}.{gi}" if k == KC - 1 else None))
                    w.wait(evac_eng, f"{pfx}.{tag}.mm.{i}.{gi}")
                    semn = V if evac_eng == "vector" else C
                    dsl = dst[:, c * 1024 + hg * 512: c * 1024 + (hg + 1) * 512]
                    if evac_eng == "vector":
                        w.op("vector", lambda e, dsl=dsl, bank=bank: e.tensor_copy(
                            out=dsl, in_=bank[:, 0:512]),
                            inc=semn, label=f"{pfx}.{tag}.ev.{i}.{gi}")
                    else:
                        w.op("scalar", lambda e, dsl=dsl, bank=bank: e.copy(
                            out=dsl, in_=bank[:, 0:512]),
                            inc=semn, label=f"{pfx}.{tag}.ev.{i}.{gi}")

        proj_fm(sb["qr"], "wq", "scalar", "q", None)
        proj_fm(sb["kr"], "wk", "vector", "k", "q")
        proj_tm(sb["kt"], "wk", "scalar", "kt", "k")
        proj_tm(sb["vt"], "wv", "vector", "vt", "kt")

        prev_fm = None
        for tag, buf in (("q", "qr"), ("k", "kr"), ("kt", "kt")):
            w.wait("vector", f"{pfx}.{tag}.ev.{i}.7")
            if prev_fm is not None:
                w.wait("vector", f"{pfx}.fm2.{prev_fm}.{i}")
            elif i >= 1:
                w.wait("vector", f"{pfx}.fm2.kt.{i-1}")
                w.wait("vector", f"{pfx}.pl.ss.{i-1}")
            prev_fm = tag
            w.op("vector", lambda e, buf=buf: e.tensor_scalar_min(
                out=sb["fmin"][:], in0=sb[buf][:], scalar1=0.0),
                inc=V, label=f"{pfx}.fm1.{tag}.{i}")
            w.wait("scalar", f"{pfx}.fm1.{tag}.{i}")
            if prev_fm == "q" and i >= 1:
                w.wait("scalar", f"{pfx}.wo.{i-1}.{KC-1}")
            w.op("scalar", lambda e: e.activation(
                out=sb["fexp"][:], in_=sb["fmin"][:], func=AF.Exp),
                inc=C, label=f"{pfx}.fm2.{tag}.{i}")
            w.wait("vector", f"{pfx}.fm2.{tag}.{i}")
            w.op("vector", lambda e, buf=buf: e.scalar_tensor_tensor(
                out=sb[buf][:], in0=sb[buf][:], scalar=0.0, in1=sb["fexp"][:],
                op0=ALU.max, op1=ALU.add),
                inc=V, label=f"{pfx}.fm3.{tag}.{i}")

        w.wait("tensor", f"{pfx}.fm3.kt.{i}")
        w.wait("tensor", f"{pfx}.vt.ev.{i}.7")
        for c in range(NCH):
            cq = i * NCH + c
            b_g = ps[4 + c % 2]

            if cq >= 1:
                w.wait("tensor", f"{pfx}.Sadd.{cq-1}.{2 + c % 2}")
            w.wait("tensor", f"{pfx}.cast.{i}")
            for k in range(KC):
                w.op("tensor", lambda e, c=c, k=k, b_g=b_g, xb=xb: e.matmul(
                    b_g[:, 0:16], xb[:, k * 512 + c * 128: k * 512 + (c + 1) * 128],
                    sb["wg"][:, k * 16:(k + 1) * 16],
                    start=(k == 0), stop=False))
            w.op("tensor", lambda e, b_g=b_g: e.matmul(
                b_g[:, 0:16], cst["ones_row_f32"], sb["bg"][0:1, 0:16],
                start=False, stop=True),
                inc=TQ, label=f"{pfx}.gmm.{cq}")
            w.wait("scalar", f"{pfx}.gmm.{cq}")
            if cq >= 1:
                w.wait("scalar", f"{pfx}.LT.{cq-1}")
            w.op("scalar", lambda e, b_g=b_g: e.activation(
                out=sb["gate"][:, 0:16], in_=b_g[:, 0:16], func=AF.Sigmoid),
                inc=C, label=f"{pfx}.sig.{cq}")
            w.wait("scalar", f"{pfx}.sig.{cq}")
            w.op("scalar", lambda e: e.activation(
                out=sb["logg"][:, 0:16], in_=sb["gate"][:, 0:16], func=AF.Ln),
                inc=C, label=f"{pfx}.logg.{cq}")
            w.wait("tensor", f"{pfx}.logg.{cq}")
            if cq >= 1:
                w.wait("tensor", f"{pfx}.eLb.{cq-1}")
                w.wait("tensor", f"{pfx}.qt.{cq-1}.7")
                w.wait("tensor", f"{pfx}.Bsub.{cq-1}")
            w.op("tensor", lambda e, b_g=b_g: e.matmul(
                ps[6][:, 0:16], cst["cums_f32"], sb["logg"][:, 0:16], start=True, stop=True),
                inc=TQ, label=f"{pfx}.LTmm.{cq}")
            w.wait("vector", f"{pfx}.LTmm.{cq}")
            if cq >= 1:
                w.wait("vector", f"{pfx}.Dpre.{cq-1}.3")
                w.wait("vector", f"{pfx}.Bsub.{cq-1}")
            w.op("vector", lambda e: e.tensor_copy(out=sb["LT"][:, 0:16], in_=ps[6][:, 0:16]),
                 inc=V, label=f"{pfx}.LT.{cq}")
            w.op("vector", lambda e: e.tensor_scalar_mul(
                out=sb["NLT"][:, 0:16], in0=ps[6][:, 0:16], scalar1=-1.0),
                inc=V, label=f"{pfx}.NLT.{cq}")
            w.wait("tensor", f"{pfx}.LT.{cq}")
            if cq >= 1:
                w.wait("tensor", f"{pfx}.qt.{cq-1}.6")
                w.wait("tensor", f"{pfx}.Erow.{cq-1}")
                w.wait("tensor", f"{pfx}.Lrow.{cq-1}")
            w.op("tensor", lambda e: e.transpose(
                ps[7][0:16, 0:128], sb["LT"][:, 0:16], cst["ident_f32"]),
                inc=TQ, label=f"{pfx}.Lrowmm.{cq}")
            w.wait("vector", f"{pfx}.Lrowmm.{cq}")
            if cq >= 1:
                w.wait("vector", f"{pfx}.Amm.{cq-1}.3")
            w.op("vector", lambda e: e.tensor_copy(out=sb["Lrow"][0:16, 0:128], in_=ps[7][0:16, 0:128]),
                 inc=V, label=f"{pfx}.Lrow.{cq}")
            w.wait("scalar", f"{pfx}.Lrowmm.{cq}")
            if cq >= 1:
                w.wait("scalar", f"{pfx}.Ebcmm.{cq-1}.7")
            w.op("scalar", lambda e: e.activation(
                out=sb["Erow"][0:16, 0:128], in_=ps[7][0:16, 0:128], func=AF.Exp),
                inc=C, label=f"{pfx}.Erow.{cq}")
            w.wait("tensor", f"{pfx}.NLT.{cq}")
            w.op("tensor", lambda e: e.matmul(
                ps[6][:, 0:16], cst["row127_f32"], sb["LT"][:, 0:16], start=True, stop=True),
                inc=TQ, label=f"{pfx}.Bmm.{cq}")
            w.wait("vector", f"{pfx}.Bmm.{cq}")
            if cq >= 1:
                w.wait("vector", f"{pfx}.khat.{cq-1}.3")
            w.op("vector", lambda e: e.tensor_sub(
                out=sb["Bm"][:, 0:16], in0=ps[6][:, 0:16], in1=sb["LT"][:, 0:16]),
                inc=V, label=f"{pfx}.Bsub.{cq}")
            w.wait("scalar", f"{pfx}.Bsub.{cq}")
            if cq >= 1:
                w.wait("scalar", f"{pfx}.Sadd.{cq-1}.3")
            w.op("scalar", lambda e: e.activation(
                out=sb["call"][:, 0:16], in_=sb["Bm"][:, 0:16], func=AF.Exp),
                inc=C, label=f"{pfx}.call.{cq}")
            w.op("scalar", lambda e: e.activation(
                out=sb["eLb"][:, 0:16], in_=ps[6][:, 0:16], func=AF.Exp),
                inc=C, label=f"{pfx}.eLb.{cq}")

            for q4 in range(4):
                quad = cq * 4 + q4
                heads = range(4 * q4, 4 * q4 + 4)
                b_pt = ps[q4 % 2]
                b_ay = ps[2 + q4 % 2]
                b_sd = ps[4 + q4 % 2]
                Dq = sb["D0"] if q4 % 2 == 0 else sb["D1"]
                PDq = sb["PD0"] if q4 % 2 == 0 else sb["PD1"]
                t1q = sb["t10"] if q4 % 2 == 0 else sb["t11"]
                khq = sb["kh0"] if q4 % 2 == 0 else sb["kh1"]

                if quad >= 2:
                    w.wait("tensor", f"{pfx}.Yev.{quad-2}")
                w.wait("tensor", f"{pfx}.Lrow.{cq}")
                for hh, h in enumerate(heads):
                    w.op("tensor", lambda e, hh=hh, h=h, b_ay=b_ay: e.matmul(
                        b_ay[:, hh * 128:(hh + 1) * 128], cst[f"oh16f_{h}"],
                        sb["Lrow"][0:16, 0:128], start=True, stop=True),
                        inc=(TQ if hh == 3 else None),
                        label=(f"{pfx}.Amm.{cq}.{q4}" if hh == 3 else None))
                w.wait("vector", f"{pfx}.Amm.{cq}.{q4}")
                w.wait("vector", f"{pfx}.NLT.{cq}")
                if quad >= 2:
                    w.wait("vector", f"{pfx}.D.{quad-2}")
                dpre = sb["tmp2"][:, (q4 % 2) * 512:(q4 % 2) * 512 + 512]
                for hh, h in enumerate(heads):
                    w.op("vector", lambda e, hh=hh, h=h, b_ay=b_ay, dpre=dpre: e.tensor_scalar(
                        out=dpre[:, hh * 128:(hh + 1) * 128],
                        in0=b_ay[:, hh * 128:(hh + 1) * 128],
                        scalar1=sb["NLT"][:, h:h + 1], scalar2=0.0,
                        op0=ALU.add, op1=ALU.min),
                        inc=(V if hh == 3 else None),
                        label=(f"{pfx}.Dpre.{cq}.{q4}" if hh == 3 else None))
                w.wait("scalar", f"{pfx}.Dpre.{cq}.{q4}")
                if quad >= 2:
                    w.wait("scalar", f"{pfx}.PD.{quad-2}")
                w.op("scalar", lambda e, Dq=Dq, dpre=dpre: e.activation(
                    out=Dq[:], in_=dpre, func=AF.Exp),
                    inc=C, label=f"{pfx}.D.{quad}")

                if quad >= 2:
                    w.wait("tensor", f"{pfx}.PD.{quad-2}")
                elif i >= 1 and c == 0 and q4 < 2:
                    pass
                w.wait("tensor", f"{pfx}.fm3.k.{i}")
                w.wait("tensor", f"{pfx}.fm3.q.{i}")
                for hh, h in enumerate(heads):
                    base = (h % 2) * 64
                    mc = h // 2
                    qs = sb["qr"][base:base + 64, mc * 512 + c * 128: mc * 512 + (c + 1) * 128]
                    ks = sb["kr"][base:base + 64, mc * 512 + c * 128: mc * 512 + (c + 1) * 128]
                    w.op("tensor", lambda e, hh=hh, ks=ks, qs=qs, b_pt=b_pt: e.matmul(
                        b_pt[:, hh * 128:(hh + 1) * 128], ks, qs, start=True, stop=True),
                        inc=(TQ if hh == 3 else None),
                        label=(f"{pfx}.PT.{cq}.{q4}" if hh == 3 else None))
                w.wait("vector", f"{pfx}.PT.{cq}.{q4}")
                w.wait("vector", f"{pfx}.D.{quad}")
                if quad >= 2:
                    w.wait("vector", f"{pfx}.Yi.{quad-2}.3")
                w.op("vector", lambda e, b_pt=b_pt, t1q=t1q: e.tensor_mul(
                    out=t1q[:], in0=b_pt[:, 0:512], in1=cst["maskUT4"]),
                    inc=V, label=f"{pfx}.PDa.{quad}")
                w.wait("vector", f"{pfx}.PDa.{quad}")
                w.op("vector", lambda e, t1q=t1q, PDq=PDq, Dq=Dq: e.tensor_mul(
                    out=PDq[:], in0=t1q[:], in1=Dq[:]),
                    inc=V, label=f"{pfx}.PD.{quad}")

                pairs = (2 * q4, 2 * q4 + 1)
                for pp, pr in enumerate(pairs):
                    b_e = ps[7] if pr % 2 == 0 else ps[6]
                    if pr >= 2:
                        w.wait("tensor", f"{pfx}.qt.{cq}.{pr-2}")
                    elif pr == 0:
                        w.wait("tensor", f"{pfx}.Lrow.{cq}")
                        if cq >= 1:
                            w.wait("tensor", f"{pfx}.qt.{cq-1}.6")
                    else:
                        w.wait("tensor", f"{pfx}.Bsub.{cq}")
                        w.wait("tensor", f"{pfx}.eLb.{cq}")
                        if cq >= 1:
                            w.wait("tensor", f"{pfx}.qt.{cq-1}.7")
                    w.wait("tensor", f"{pfx}.Erow.{cq}")
                    w.op("tensor", lambda e, pr=pr, b_e=b_e: e.matmul(
                        b_e[:, 0:128], cst[f"prsel_{pr}"], sb["Erow"][0:16, 0:128],
                        start=True, stop=True),
                        inc=TQ, label=f"{pfx}.Ebcmm.{cq}.{pr}")
                    w.wait("vector", f"{pfx}.Ebcmm.{cq}.{pr}")
                    qtb = sb["qt0"] if pr % 2 == 0 else sb["qt1"]
                    if pr >= 2:
                        w.wait("vector", f"{pfx}.Ypair.{cq}.{pr-2}")
                    elif cq >= 1:
                        w.wait("vector", f"{pfx}.Ypair.{cq-1}.{6 + pr}")
                    mc = pr
                    w.op("vector", lambda e, mc=mc, c=c, b_e=b_e, qtb=qtb: e.tensor_mul(
                        out=qtb[:], in0=sb["qr"][:, mc * 512 + c * 128: mc * 512 + (c + 1) * 128],
                        in1=b_e[:, 0:128]),
                        inc=V, label=f"{pfx}.qt.{cq}.{pr}")

                w.wait("tensor", f"{pfx}.Dpre.{cq}.{q4}")
                w.wait("tensor", f"{pfx}.PD.{quad}")
                first = seq_start and c == 0
                if not first:
                    w.wait("tensor", f"{pfx}.Sbf.{cq-1}")
                for hh, h in enumerate(heads):
                    pr = h // 2
                    base = (h % 2) * 64
                    vslice = sb["vt"][:, c * 1024 + h * 64: c * 1024 + (h + 1) * 64]
                    ysl = b_ay[base:base + 64, (hh // 2) * 128:(hh // 2) * 128 + 128]
                    if first:
                        w.op("tensor", lambda e, vslice=vslice, PDq=PDq, hh=hh, ysl=ysl: e.matmul(
                            ysl, vslice, PDq[:, hh * 128:(hh + 1) * 128], start=True, stop=True),
                            inc=(TQ if hh == 3 else None),
                            label=(f"{pfx}.Yi.{quad}.{hh}" if hh == 3 else None))
                    else:
                        qtb = sb["qt0"] if pr % 2 == 0 else sb["qt1"]
                        w.wait("tensor", f"{pfx}.qt.{cq}.{pr}")
                        Ssl = sb["S_bf"][base:base + 64, (h // 2) * 64:(h // 2) * 64 + 64]
                        w.op("tensor", lambda e, vslice=vslice, PDq=PDq, hh=hh, ysl=ysl: e.matmul(
                            ysl, vslice, PDq[:, hh * 128:(hh + 1) * 128], start=True, stop=False))
                        w.op("tensor", lambda e, Ssl=Ssl, qtb=qtb, base=base, ysl=ysl: e.matmul(
                            ysl, Ssl, qtb[base:base + 64, :], start=False, stop=True),
                            inc=(TQ if hh == 3 else None),
                            label=(f"{pfx}.Yi.{quad}.{hh}" if hh == 3 else None))
                if w.mode == "plan":
                    w.labels[f"{pfx}.Ypair.{cq}.{2*q4}"] = w.labels[f"{pfx}.Yi.{quad}.3"]
                    w.labels[f"{pfx}.Ypair.{cq}.{2*q4+1}"] = w.labels[f"{pfx}.Yi.{quad}.3"]
                w.wait("scalar", f"{pfx}.Yi.{quad}.3")
                for pp, pr in enumerate(pairs):
                    w.op("scalar", lambda e, pp=pp, pr=pr, c=c, b_ay=b_ay: e.copy(
                        out=sb["o"][:, pr * 512 + c * 128: pr * 512 + (c + 1) * 128],
                        in_=b_ay[:, pp * 128:(pp + 1) * 128]),
                        inc=(C if pp == 1 else None),
                        label=(f"{pfx}.Yev.{quad}" if pp == 1 else None))

                w.wait("vector", f"{pfx}.call.{cq}")
                if quad >= 2:
                    w.wait("vector", f"{pfx}.Sd.{quad-2}")
                for hh, h in enumerate(heads):
                    w.op("vector", lambda e, hh=hh, h=h, c=c, khq=khq: e.tensor_scalar_mul(
                        out=khq[:, hh * 64:(hh + 1) * 64],
                        in0=sb["kt"][:, c * 1024 + h * 64: c * 1024 + (h + 1) * 64],
                        scalar1=sb["call"][:, h:h + 1]),
                        inc=(V if hh == 3 else None),
                        label=(f"{pfx}.khat.{cq}.{q4}" if hh == 3 else None))
                if quad >= 2:
                    w.wait("tensor", f"{pfx}.Sadd.{cq}.{q4-2}" if q4 >= 2 else f"{pfx}.Sadd.{cq-1}.{2+q4}")
                w.wait("tensor", f"{pfx}.khat.{cq}.{q4}")
                for hh, h in enumerate(heads):
                    base = (h % 2) * 64
                    vslice = sb["vt"][:, c * 1024 + h * 64: c * 1024 + (h + 1) * 64]
                    sdsl = b_sd[base:base + 64, (hh // 2) * 64:(hh // 2) * 64 + 64]
                    w.op("tensor", lambda e, hh=hh, khq=khq, vslice=vslice, sdsl=sdsl: e.matmul(
                        sdsl, khq[:, hh * 64:(hh + 1) * 64], vslice, start=True, stop=True),
                        inc=(TQ if hh == 3 else None),
                        label=(f"{pfx}.Sd.{quad}" if hh == 3 else None))
                w.wait("vector", f"{pfx}.Sd.{quad}")
                w.wait("vector", f"{pfx}.eLb.{cq}")
                if not (seq_start and c == 0):
                    w.wait("vector", f"{pfx}.Sadd.{cq-1}.{q4}")
                for hh, h in enumerate(heads):
                    base = (h % 2) * 64
                    Sfs = sb["S_f32"][base:base + 64, (h // 2) * 64:(h // 2) * 64 + 64]
                    sdsl = b_sd[base:base + 64, (hh // 2) * 64:(hh // 2) * 64 + 64]
                    if seq_start and c == 0:
                        w.op("vector", lambda e, Sfs=Sfs, sdsl=sdsl: e.tensor_copy(
                            out=Sfs, in_=sdsl),
                            inc=(V if hh == 3 else None),
                            label=(f"{pfx}.Sadd.{cq}.{q4}" if hh == 3 else None))
                    else:
                        w.op("vector", lambda e, Sfs=Sfs, sdsl=sdsl, h=h, base=base: e.scalar_tensor_tensor(
                            out=Sfs, in0=Sfs, scalar=sb["eLb"][base:base + 64, h:h + 1], in1=sdsl,
                            op0=ALU.mult, op1=ALU.add),
                            inc=(V if hh == 3 else None),
                            label=(f"{pfx}.Sadd.{cq}.{q4}" if hh == 3 else None))

            w.wait("scalar", f"{pfx}.Sadd.{cq}.3")
            w.wait("scalar", f"{pfx}.Sadd.{cq}.2")
            w.op("scalar", lambda e: e.copy(out=sb["S_bf"][:, 0:512], in_=sb["S_f32"][:, 0:512]),
                 inc=C, label=f"{pfx}.Sbf.{cq}")

        w.wait("tensor", f"{pfx}.Yev.{(i * NCH + NCH - 1) * 4 + 3}")
        for m in range(KC):
            bank = ps[m % 2]
            if m >= 2:
                w.wait("tensor", f"{pfx}.res.{i}.{m-2}")
            else:
                w.wait("tensor", f"{pfx}.PD.{(i * NCH + NCH - 1) * 4 + 2 + m}")
            for h in range(KC):
                w.op("tensor", lambda e, m=m, h=h, bank=bank: e.matmul(
                    bank[:, 0:512], sb["wo"][:, h * 1024 + m * 128: h * 1024 + (m + 1) * 128],
                    sb["o"][:, h * 512:(h + 1) * 512],
                    start=(h == 0), stop=(h == KC - 1)),
                    inc=(TQ if h == KC - 1 else None),
                    label=(f"{pfx}.wo.{i}.{m}" if h == KC - 1 else None))
            w.wait("vector", f"{pfx}.wo.{i}.{m}")
            w.op("vector", lambda e, m=m, xb=xb, bank=bank: e.tensor_add(
                out=xb[:, m * 512:(m + 1) * 512], in0=xb[:, m * 512:(m + 1) * 512],
                in1=bank[:, 0:512]),
                inc=V, label=f"{pfx}.res.{i}.{m}")

        w.wait("scalar", f"{pfx}.res.{i}.{KC-1}")
        w.wait("tensor", f"{pfx}.res.{i}.{KC-1}")
        postln_tile(w, pfx, i, sb, ps, cst, xb)
        w.wait("gpsimd", f"{pfx}.pl.app.{i}")
        for kc in range(KC):
            w.op("gpsimd", lambda e, kc=kc, t0=t0, xb=xb: e.dma_start(
                out=xout[kc * 128:(kc + 1) * 128, t0:t0 + TILE],
                in_=xb[:, kc * 512:(kc + 1) * 512]),
                inc=f"S{i % 2}", k=16, label=f"{pfx}.st.{i}.{kc}")

    w.wait_count("gpsimd", "S0", 16 * KC * (NTILE // 2))
    w.wait_count("gpsimd", "S1", 16 * KC * (NTILE // 2))


def p5_emit(w, nc, bufs, xin, xout, wbf, smalls, pfx="p5"):
    sb = bufs["sb"]
    ps = bufs["ps"]
    cst = bufs["cst"]
    TQ, V, C = w.TQ, w.V, w.C

    for e in range(8):
        for k in range(KC):
            w.op("sync", lambda e_, e2=e, k=k: e_.dma_start(
                out=sb["we"][:, (e2 * 8 + k) * 1024:(e2 * 8 + k + 1) * 1024],
                in_=wbf["we"][e2 * 1024 + k * 128: e2 * 1024 + (k + 1) * 128, :]),
                inc="Lw", k=16)
    for k in range(KC):
        w.op("sync", lambda e_, k=k: e_.dma_start(
            out=sb["wr"][:, k * 8:(k + 1) * 8], in_=wbf["wr"][k * 128:(k + 1) * 128, :]),
            inc="Lw", k=16)
    w.op("sync", lambda e_: e_.dma_start(out=sb["rb_f"][:, :], in_=smalls["rb"]),
         inc="Lw", k=16)
    w.op("sync", lambda e_: e_.dma_start(out=sb["eb_f"][:, :], in_=smalls["eb"]),
         inc="Lw", k=16, label=f"{pfx}.w.done")
    w.wait("scalar", f"{pfx}.w.done")
    w.op("scalar", lambda e_: e_.copy(out=sb["eb_bf"][:, :], in_=sb["eb_f"][:, :]),
         inc=C, label=f"{pfx}.ebcast")

    for i in range(NTILE):
        t0 = i * TILE
        xb = sb["x0"]

        if i >= 1:
            w.wait("sync", f"{pfx}.st.{i-1}.{KC-1}")
        for kc in range(KC):
            w.op("sync", lambda e_, kc=kc, t0=t0, xb=xb: e_.dma_start(
                out=xb[:, kc * 512:(kc + 1) * 512],
                in_=xin[kc * 128:(kc + 1) * 128, t0:t0 + TILE]),
                inc=f"Lx{i % 2}", k=16, label=f"{pfx}.x.{i}.{kc}")

        w.wait("scalar", f"{pfx}.x.{i}.{KC-1}")
        if i >= 1:
            w.wait("scalar", f"{pfx}.ymm.{i-1}")
        w.op("scalar", lambda e_, xb=xb: e_.copy(out=sb["xbf"][:], in_=xb[:]),
             inc=C, label=f"{pfx}.cast.{i}")

        w.wait("tensor", f"{pfx}.cast.{i}")
        w.wait("tensor", f"{pfx}.w.done")
        if i >= 1:
            w.wait("tensor", f"{pfx}.pl.app.{i-1}")
        for c in range(NCH):
            b_r = ps[4 + c % 2]
            if c >= 2:
                w.wait("tensor", f"{pfx}.m00.{i}.{c-2}")
            for k in range(KC):
                w.op("tensor", lambda e_, c=c, k=k, b_r=b_r, xb=xb: e_.matmul(
                    b_r[:, 0:8], xb[:, k * 512 + c * 128: k * 512 + (c + 1) * 128],
                    sb["wr"][:, k * 8:(k + 1) * 8],
                    start=(k == 0), stop=False))
            w.op("tensor", lambda e_, c=c, b_r=b_r: e_.matmul(
                b_r[:, 0:8], cst["ones_row_f32"], sb["rb_f"][0:1, 0:8],
                start=False, stop=True),
                inc=TQ, label=f"{pfx}.rmm.{i}.{c}")
            w.wait("vector", f"{pfx}.rmm.{i}.{c}")
            if c >= 1:
                w.wait("vector", f"{pfx}.eexp.{i}.{c-1}")
                w.wait("vector", f"{pfx}.mtp.{i}.{c-1}")
            w.op("vector", lambda e_, b_r=b_r: e_.tensor_reduce(
                out=sb["nmx"][:, 0:1], in_=b_r[:, 0:8], axis=AX.X, op=ALU.max, negate=True),
                inc=V, label=f"{pfx}.nmx.{i}.{c}")
            w.wait("scalar", f"{pfx}.nmx.{i}.{c}")
            w.op("scalar", lambda e_, b_r=b_r: e_.activation(
                out=sb["eexp"][:, 0:8], in_=b_r[:, 0:8], func=AF.Exp, bias=sb["nmx"][:, 0:1]),
                inc=C, label=f"{pfx}.eexp.{i}.{c}")
            w.wait("vector", f"{pfx}.eexp.{i}.{c}")
            w.op("vector", lambda e_: e_.tensor_reduce(
                out=sb["srow"][:, 0:1], in_=sb["eexp"][:, 0:8], axis=AX.X, op=ALU.add),
                inc=V, label=f"{pfx}.srow.{i}.{c}")
            w.wait("vector", f"{pfx}.srow.{i}.{c}")
            w.op("vector", lambda e_: e_.reciprocal(
                out=sb["wrow"][:, 0:1], in_=sb["srow"][:, 0:1]),
                inc=V, label=f"{pfx}.wrec.{i}.{c}")
            w.wait("vector", f"{pfx}.wrec.{i}.{c}")
            w.op("vector", lambda e_, b_r=b_r: e_.tensor_scalar(
                out=sb["m0"][:, 0:8], in0=b_r[:, 0:8], scalar1=sb["nmx"][:, 0:1], scalar2=0.0,
                op0=ALU.add, op1=ALU.is_equal),
                inc=V, label=f"{pfx}.m00.{i}.{c}")
            w.wait("vector", f"{pfx}.m00.{i}.{c}")
            w.op("vector", lambda e_: e_.tensor_scalar(
                out=sb["m1"][:, 0:8], in0=sb["m0"][:, 0:8], scalar1=sb["wrow"][:, 0:1],
                scalar2=None, op0=ALU.mult),
                inc=V, label=f"{pfx}.m0.{i}.{c}")
            w.wait("tensor", f"{pfx}.m0.{i}.{c}")
            w.op("tensor", lambda e_, c=c: e_.transpose(
                ps[7][0:8, c * 128:(c + 1) * 128], sb["m1"][:, 0:8], cst["ident_f32"]),
                inc=TQ, label=f"{pfx}.mtp.{i}.{c}")
        w.wait("vector", f"{pfx}.mtp.{i}.{NCH-1}")
        w.op("vector", lambda e_: e_.tensor_copy(out=sb["mT"][0:8, 0:512], in_=ps[7][0:8, 0:512]),
             inc=V, label=f"{pfx}.mt.{i}")

        w.wait("tensor", f"{pfx}.mt.{i}")
        for e in range(8):
            b_m = ps[6] if e % 2 == 0 else ps[4]
            if e >= 2:
                w.wait("tensor", f"{pfx}.mbc.ev.{i}.{e-2}")
            w.op("tensor", lambda e_, e2=e, b_m=b_m: e_.matmul(
                b_m[:, 0:512], cst[f"oh8_{e2}"], sb["mT"][0:8, 0:512], start=True, stop=True),
                inc=TQ, label=f"{pfx}.mbc.{i}.{e}")
            w.wait("scalar", f"{pfx}.mbc.{i}.{e}")
            if i >= 1:
                w.wait("scalar", f"{pfx}.xm.{i-1}.{e}")
            w.op("scalar", lambda e_, e2=e, b_m=b_m: e_.copy(
                out=sb["maskbc"][:, e2 * 512:(e2 + 1) * 512], in_=b_m[:, 0:512]),
                inc=C, label=f"{pfx}.mbc.ev.{i}.{e}")

        w.wait("tensor", f"{pfx}.mbc.ev.{i}.6")
        w.wait("tensor", f"{pfx}.mbc.ev.{i}.7")
        w.wait("tensor", f"{pfx}.ebcast")
        if i >= 1:
            w.wait("tensor", f"{pfx}.res.{i-1}.{KC-1}")
        for o in range(KC):
            w.op("tensor", lambda e_, o=o: e_.matmul(
                ps[o][:, 0:512], sb["eb_bf"][0:8, o * 128:(o + 1) * 128], sb["mT"][0:8, 0:512],
                start=True, stop=False))
        for e in range(8):
            xm = sb["xm0"] if e % 2 == 0 else sb["xm1"]
            w.wait("vector", f"{pfx}.mbc.ev.{i}.{e}")
            if e >= 2:
                w.wait("vector", f"{pfx}.emm.{i}.{e-2}")
            elif i >= 1:
                w.wait("vector", f"{pfx}.emm.{i-1}.{6+e}")
                if e == 1:
                    w.wait("vector", f"{pfx}.pl.ss.{i-1}")
            for kc in range(KC):
                w.op("vector", lambda e_, e2=e, kc=kc, xm=xm: e_.tensor_mul(
                    out=xm[:, kc * 512:(kc + 1) * 512], in0=sb["xbf"][:, kc * 512:(kc + 1) * 512],
                    in1=sb["maskbc"][:, e2 * 512:(e2 + 1) * 512]),
                    inc=(V if kc == KC - 1 else None),
                    label=(f"{pfx}.xm.{i}.{e}" if kc == KC - 1 else None))
            w.wait("tensor", f"{pfx}.xm.{i}.{e}")
            for o in range(KC):
                for k in range(KC):
                    w.op("tensor", lambda e_, e2=e, o=o, k=k, xm=xm: e_.matmul(
                        ps[o][:, 0:512],
                        sb["we"][:, (e2 * 8 + k) * 1024 + o * 128: (e2 * 8 + k) * 1024 + (o + 1) * 128],
                        xm[:, k * 512:(k + 1) * 512],
                        start=False, stop=(e2 == 7 and k == KC - 1)),
                        inc=(TQ if (o == KC - 1 and k == KC - 1) else None),
                        label=(f"{pfx}.emm.{i}.{e}" if (o == KC - 1 and k == KC - 1) else None))
        if w.mode == "plan":
            w.labels[f"{pfx}.ymm.{i}"] = w.labels[f"{pfx}.emm.{i}.7"]

        w.wait("vector", f"{pfx}.emm.{i}.7")
        for o in range(KC):
            w.op("vector", lambda e_, o=o, xb=xb: e_.tensor_add(
                out=xb[:, o * 512:(o + 1) * 512], in0=xb[:, o * 512:(o + 1) * 512],
                in1=ps[o][:, 0:512]),
                inc=V, label=f"{pfx}.res.{i}.{o}")
        w.wait("scalar", f"{pfx}.res.{i}.{KC-1}")
        w.wait("tensor", f"{pfx}.res.{i}.{KC-1}")
        postln_tile(w, pfx, i, sb, ps, cst, xb)
        w.wait("gpsimd", f"{pfx}.pl.app.{i}")
        for kc in range(KC):
            w.op("gpsimd", lambda e_, kc=kc, t0=t0, xb=xb: e_.dma_start(
                out=xout[kc * 128:(kc + 1) * 128, t0:t0 + TILE],
                in_=xb[:, kc * 512:(kc + 1) * 512]),
                inc=f"S{i % 2}", k=16, label=f"{pfx}.st.{i}.{kc}")

    w.wait_count("gpsimd", "S0", 16 * KC * (NTILE // 2))
    w.wait_count("gpsimd", "S1", 16 * KC * (NTILE // 2))


def build_mlp_program(kind, debug=False):
    swiglu = kind == "swiglu"
    nc = bass.Bass("TRN2", target_bir_lowering=False)
    nc.ctx.enter_context(nc.allow_low_precision(
        reason="bf16 matmul operands with fp32 accumulation; intentional"))
    xin = nc.dram_tensor("xin", [D, NTOK], F32, kind="ExternalInput").ap()
    xin0 = (nc.dram_tensor("xin0", [D, NTOK], F32, kind="ExternalInput").ap()
            if swiglu else None)
    xout = nc.dram_tensor("xout", [D, NTOK], F32, kind="ExternalOutput").ap()
    wbf = {}
    wbf["wg"] = nc.dram_tensor("wg", [D, 4096], BF16, kind="ExternalInput").ap()
    if swiglu:
        wbf["wu"] = nc.dram_tensor("wu", [D, 4096], BF16, kind="ExternalInput").ap()
    wbf["wd"] = nc.dram_tensor("wd", [4096, D], BF16, kind="ExternalInput").ap()
    smalls = {}
    if swiglu:
        smalls["nw"] = nc.dram_tensor("nw", [128, KC], F32, kind="ExternalInput").ap()
    else:
        smalls["gamma"] = nc.dram_tensor("gamma", [128, KC], F32, kind="ExternalInput").ap()
        smalls["beta"] = nc.dram_tensor("beta", [128, KC], F32, kind="ExternalInput").ap()
        smalls["b1"] = nc.dram_tensor("b1", [128, 32], F32, kind="ExternalInput").ap()
        smalls["b2"] = nc.dram_tensor("b2", [128, KC], F32, kind="ExternalInput").ap()
    consts = build_constants(nc)

    with ExitStack() as stack:
        E = stack.enter_context

        def sbuf(name, shape, dt):
            return E(nc.sbuf_tensor(name, shape, dt))[:, :]

        sb = {}
        sb["x0"] = sbuf("x0", [128, KC * 512], F32)
        sb["x1"] = sbuf("x1", [128, KC * 512], F32)
        if swiglu:
            sb["x0s"] = sbuf("x0s", [128, KC * 512], F32)
        sb["h"] = sbuf("h_sb", [128, KC * 512], BF16)
        sb["xsq"] = sbuf("xsq_sb", [128, KC * 512], BF16)
        sb["xbf"] = sbuf("xbf_sb", [128, KC * 512], BF16)
        sb["tgu"] = sbuf("tgu_sb", [128, 32 * 512], BF16)
        sb["wslot"] = [sbuf(f"wslot{j}", [128, 8192], BF16) for j in range(4)]
        sb["sg0"] = sbuf("sg0", [128, 512], BF16)
        sb["sg1"] = sbuf("sg1", [128, 512], BF16)
        sb["tmp0"] = sbuf("tmp0", [128, 512], F32)
        sb["tmp1"] = sbuf("tmp1", [128, 512], F32)
        sb["tmp2"] = sbuf("tmp2", [128, KC * 512], F32)
        for n in ("row_a", "row_b", "row_c", "row_mu"):
            sb[n] = sbuf(n, [1, 512], F32)
        sb["row_bf"] = sbuf("row_bf", [1, 512], BF16)
        sb["row_mu_bf"] = sbuf("row_mu_bf", [1, 512], BF16)
        if swiglu:
            sb["nw"] = sbuf("nw_sb", [128, KC], F32)
        else:
            sb["gamma"] = sbuf("gamma_sb", [128, KC], F32)
            sb["beta"] = sbuf("beta_sb", [128, KC], F32)
            sb["b1"] = sbuf("b1_sb", [128, 32], F32)
            sb["b2"] = sbuf("b2_sb", [128, KC], F32)
            sb["b2h"] = sbuf("b2h_sb", [128, KC], F32)

        cst = {}
        cst["ones_col"] = sbuf("ones_col_sb", [128, 1], BF16)
        cst["ones_row"] = sbuf("ones_row_sb", [1, 128], BF16)
        cst["eps11"] = sbuf("eps11_sb", [1, 1], F32)

        ps = [E(nc.psum_tensor(f"psb{j}", [128, 512], F32))[:, :] for j in range(8)]

        semnames = ("Lw", "Lx0", "Lx1", "Lx2", "Lp0", "Lp1", "Lp2", "Lp3", "S0", "S1", "T", "V", "C")
        sems = {name: E(nc.semaphore(name)) for name in semnames}
        block = E(nc.Block())

        w = Walk(sems, serial=True)
        w.TQ, w.V, w.C = "T", "V", "C"
        bufs = {"sb": sb, "ps": ps, "cst": cst}
        dbg = None
        if debug:
            dbg = {
                "h": nc.dram_tensor("dbg_h", [128, KC * 512], BF16, kind="ExternalOutput").ap(),
                "tgu": nc.dram_tensor("dbg_tgu", [128, 32 * 512], BF16, kind="ExternalOutput").ap(),
                "xres": nc.dram_tensor("dbg_xres", [128, KC * 512], F32, kind="ExternalOutput").ap(),
            }
        cfg = {"kind": kind, "pre": "rms" if swiglu else "ln", "dbg": dbg}

        def full(walk):
            walk.op("sync", lambda e: e.dma_start(out=cst["ones_col"], in_=consts["ones_col_bf"]),
                    inc="Lw", k=16)
            walk.op("sync", lambda e: e.dma_start(out=cst["ones_row"], in_=consts["ones_row_bf"]),
                    inc="Lw", k=16)
            walk.op("sync", lambda e: e.dma_start(out=cst["eps11"], in_=consts["eps11"]),
                    inc="Lw", k=16, label="consts")
            mlp_emit(walk, nc, bufs, xin, xout, wbf, smalls, cfg, "m", xin0=xin0)
            walk.final_waits("gpsimd")

        w.plan()
        full(w)
        w.totals = dict(w.counts)

        @block.sync
        def _(q):
            w.start_emit("sync", q)
            full(w)

        @block.tensor
        def _(t):
            w.start_emit("tensor", t)
            full(w)

        @block.vector
        def _(v):
            w.start_emit("vector", v)
            full(w)

        @block.scalar
        def _(s):
            w.start_emit("scalar", s)
            full(w)

        @block.gpsimd
        def _(g):
            w.start_emit("gpsimd", g)
            full(w)

    return nc


def build_p4_program():
    nc = bass.Bass("TRN2", target_bir_lowering=False)
    nc.ctx.enter_context(nc.allow_low_precision(
        reason="bf16 matmul operands with fp32 accumulation; intentional"))
    xin = nc.dram_tensor("xin", [D, NTOK], F32, kind="ExternalInput").ap()
    xout = nc.dram_tensor("xout", [D, NTOK], F32, kind="ExternalOutput").ap()
    wbf = {n: nc.dram_tensor(n, [D, D], BF16, kind="ExternalInput").ap()
           for n in ("wq", "wk", "wv", "wo")}
    smalls = {
        "wg": nc.dram_tensor("wg", [D, 16], F32, kind="ExternalInput").ap(),
        "bg": nc.dram_tensor("bg", [1, 16], F32, kind="ExternalInput").ap(),
    }
    consts = build_constants(nc)

    with ExitStack() as stack:
        E = stack.enter_context

        def sbuf(name, shape, dt):
            return E(nc.sbuf_tensor(name, shape, dt))[:, :]

        sb = {}
        sb["x0"] = sbuf("x0", [128, KC * 512], F32)
        sb["x1"] = sbuf("x1", [128, KC * 512], F32)
        sb["xbf"] = sbuf("xbf_sb", [128, KC * 512], BF16)
        for n in ("wq", "wk", "wv", "wo"):
            sb[n] = sbuf(n + "_sb", [128, KC * 1024], BF16)
        sb["wg"] = sbuf("wg_sb", [128, KC * 16], F32)
        sb["bg"] = sbuf("bg_sb", [1, 16], F32)
        for n in ("qr", "kr", "kt", "vt", "fmin", "fexp"):
            sb[n] = sbuf(n + "_sb", [128, KC * 512], BF16)
        sb["o"] = sb["fexp"]
        sb["S_f32"] = sbuf("S_f32", [128, 512], F32)
        sb["S_bf"] = sbuf("S_bf", [128, 512], BF16)
        for n in ("gate", "logg", "LT", "NLT", "Bm", "call", "eLb"):
            sb[n] = sbuf(n + "_sb", [128, 16], F32)
        sb["Lrow"] = sbuf("Lrow_sb", [16, 128], F32)
        sb["Erow"] = sbuf("Erow_sb", [16, 128], BF16)
        for n in ("D0", "D1", "PD0", "PD1", "t10", "t11"):
            sb[n] = sbuf(n + "_sb", [128, 512], BF16)
        sb["kh0"] = sbuf("kh0_sb", [128, 256], BF16)
        sb["kh1"] = sbuf("kh1_sb", [128, 256], BF16)
        sb["qt0"] = sbuf("qt0_sb", [128, 128], BF16)
        sb["qt1"] = sbuf("qt1_sb", [128, 128], BF16)
        sb["tmp2"] = sbuf("tmp2", [128, KC * 512], F32)
        sb["xsq"] = sb["fmin"]
        sb["row_a"] = sb["tmp2"][0:1, 1536:2048]
        sb["row_b"] = sb["tmp2"][0:1, 2048:2560]
        sb["row_c"] = sb["tmp2"][0:1, 2560:3072]
        sb["row_mu"] = sb["tmp2"][0:1, 3072:3584]
        sb["row_bf"] = sbuf("row_bf", [1, 512], BF16)
        sb["row_mu_bf"] = sbuf("row_mu_bf", [1, 512], BF16)

        cst = {}
        cst["ones_col"] = sbuf("ones_col_sb", [128, 1], BF16)
        cst["ones_row"] = sbuf("ones_row_sb", [1, 128], BF16)
        cst["ones_row_f32"] = sbuf("ones_row_f_sb", [1, 128], F32)
        cst["eps11"] = sbuf("eps11_sb", [1, 1], F32)
        cst["ident_f32"] = sbuf("identf_sb", [128, 128], F32)
        cst["cums_f32"] = sbuf("cums_sb", [128, 128], F32)
        cst["row127_f32"] = sbuf("row127_sb", [128, 128], F32)
        cst["maskUT4"] = sbuf("maskUT4_sb", [128, 512], F32)
        for h in range(16):
            cst[f"oh16f_{h}"] = sbuf(f"oh16f_{h}_sb", [16, 128], F32)
        for p in range(8):
            cst[f"prsel_{p}"] = sbuf(f"prsel_{p}_sb", [16, 128], BF16)

        ps = [E(nc.psum_tensor(f"psb{j}", [128, 512], F32))[:, :] for j in range(8)]

        semnames = ("Lw", "Lx0", "Lx1", "S0", "S1", "T", "V", "C")
        sems = {name: E(nc.semaphore(name)) for name in semnames}
        block = E(nc.Block())

        w = Walk(sems, serial=True)
        w.TQ, w.V, w.C = "T", "V", "C"
        bufs = {"sb": sb, "ps": ps, "cst": cst}

        def full(walk):
            for cn, src_ap in (
                ("ones_col", consts["ones_col_bf"]), ("ones_row", consts["ones_row_bf"]),
                ("ones_row_f32", consts["ones_row_f32"]), ("eps11", consts["eps11"]),
                ("ident_f32", consts["ident_f32"]), ("cums_f32", consts["cums_f32"]),
                ("row127_f32", consts["row127_f32"]), ("maskUT4", consts["maskUT4_f32"]),
            ):
                walk.op("sync", lambda e, cn=cn, src_ap=src_ap: e.dma_start(
                    out=cst[cn], in_=src_ap), inc="Lw", k=16)
            for h in range(16):
                walk.op("sync", lambda e, h=h: e.dma_start(
                    out=cst[f"oh16f_{h}"], in_=consts["oh16f"][h]), inc="Lw", k=16)
            for p in range(8):
                walk.op("sync", lambda e, p=p: e.dma_start(
                    out=cst[f"prsel_{p}"], in_=consts["prsel"][p]), inc="Lw", k=16)
            p4_emit(walk, nc, bufs, xin, xout, wbf, smalls)
            walk.final_waits("gpsimd")

        w.plan()
        full(w)
        w.totals = dict(w.counts)

        @block.sync
        def _(q):
            w.start_emit("sync", q)
            full(w)

        @block.tensor
        def _(t):
            w.start_emit("tensor", t)
            full(w)

        @block.vector
        def _(v):
            w.start_emit("vector", v)
            full(w)

        @block.scalar
        def _(s):
            w.start_emit("scalar", s)
            full(w)

        @block.gpsimd
        def _(g):
            w.start_emit("gpsimd", g)
            full(w)

    return nc


def build_p5_program():
    nc = bass.Bass("TRN2", target_bir_lowering=False)
    nc.ctx.enter_context(nc.allow_low_precision(
        reason="bf16 matmul operands with fp32 accumulation; intentional"))
    xin = nc.dram_tensor("xin", [D, NTOK], F32, kind="ExternalInput").ap()
    xout = nc.dram_tensor("xout", [D, NTOK], F32, kind="ExternalOutput").ap()
    wbf = {
        "we": nc.dram_tensor("we", [8 * D, D], BF16, kind="ExternalInput").ap(),
        "wr": nc.dram_tensor("wr", [D, 8], F32, kind="ExternalInput").ap(),
    }
    smalls = {
        "rb": nc.dram_tensor("rb", [1, 8], F32, kind="ExternalInput").ap(),
        "eb": nc.dram_tensor("eb", [8, D], F32, kind="ExternalInput").ap(),
    }
    consts = build_constants(nc)

    with ExitStack() as stack:
        E = stack.enter_context

        def sbuf(name, shape, dt):
            return E(nc.sbuf_tensor(name, shape, dt))[:, :]

        sb = {}
        sb["x0"] = sbuf("x0", [128, KC * 512], F32)
        sb["x1"] = sb["x0"]
        sb["xbf"] = sbuf("xbf_sb", [128, KC * 512], BF16)
        sb["we"] = sbuf("we_sb", [128, 64 * 1024], BF16)
        sb["wr"] = sbuf("wr_sb", [128, 64], F32)
        sb["rb_f"] = sbuf("rb_f", [1, 8], F32)
        sb["eb_bf"] = sbuf("eb_bf", [8, D], BF16)
        sb["nmx"] = sbuf("nmx", [128, 1], F32)
        sb["srow"] = sbuf("srow", [128, 1], F32)
        sb["wrow"] = sbuf("wrow", [128, 1], F32)
        sb["eexp"] = sbuf("eexp", [128, 8], F32)
        sb["m0"] = sbuf("m0", [128, 8], F32)
        sb["m1"] = sbuf("m1", [128, 8], F32)
        sb["mT"] = sbuf("mT", [8, 512], BF16)
        sb["maskbc"] = sbuf("maskbc", [128, 8 * 512], BF16)
        sb["xm0"] = sbuf("xm0", [128, KC * 512], BF16)
        sb["xm1"] = sbuf("xm1", [128, KC * 512], BF16)
        sb["xsq"] = sb["xm1"]
        sb["tmp2"] = sbuf("tmp2", [128, KC * 512], F32)
        sb["eb_f"] = sb["tmp2"][0:8, 0:1024]
        sb["row_a"] = sb["tmp2"][0:1, 1536:2048]
        sb["row_b"] = sb["tmp2"][0:1, 2048:2560]
        sb["row_c"] = sb["tmp2"][0:1, 2560:3072]
        sb["row_mu"] = sb["tmp2"][0:1, 3072:3584]
        sb["row_bf"] = sbuf("row_bf", [1, 512], BF16)
        sb["row_mu_bf"] = sbuf("row_mu_bf", [1, 512], BF16)

        cst = {}
        cst["ones_col"] = sbuf("ones_col_sb", [128, 1], BF16)
        cst["ones_row"] = sbuf("ones_row_sb", [1, 128], BF16)
        cst["eps11"] = sbuf("eps11_sb", [1, 1], F32)
        cst["ident_f32"] = sbuf("identf_sb", [128, 128], F32)
        cst["ones_row_f32"] = sbuf("ones_row_f_sb", [1, 128], F32)
        for e in range(8):
            cst[f"oh8_{e}"] = sbuf(f"oh8_{e}_sb", [8, 128], BF16)

        ps = [E(nc.psum_tensor(f"psb{j}", [128, 512], F32))[:, :] for j in range(8)]

        semnames = ("Lw", "Lx0", "Lx1", "S0", "S1", "T", "V", "C")
        sems = {name: E(nc.semaphore(name)) for name in semnames}
        block = E(nc.Block())

        w = Walk(sems, serial=True)
        w.TQ, w.V, w.C = "T", "V", "C"
        bufs = {"sb": sb, "ps": ps, "cst": cst}

        def full(walk):
            walk.op("sync", lambda e: e.dma_start(out=cst["ones_col"], in_=consts["ones_col_bf"]),
                    inc="Lw", k=16)
            walk.op("sync", lambda e: e.dma_start(out=cst["ones_row"], in_=consts["ones_row_bf"]),
                    inc="Lw", k=16)
            walk.op("sync", lambda e: e.dma_start(out=cst["eps11"], in_=consts["eps11"]),
                    inc="Lw", k=16)
            walk.op("sync", lambda e: e.dma_start(out=cst["ident_f32"], in_=consts["ident_f32"]),
                    inc="Lw", k=16)
            walk.op("sync", lambda e: e.dma_start(out=cst["ones_row_f32"], in_=consts["ones_row_f32"]),
                    inc="Lw", k=16)
            for e2 in range(8):
                walk.op("sync", lambda e, e2=e2: e.dma_start(
                    out=cst[f"oh8_{e2}"], in_=consts["oh8"][e2]), inc="Lw", k=16)
            p5_emit(walk, nc, bufs, xin, xout, wbf, smalls)
            walk.final_waits("gpsimd")

        w.plan()
        full(w)
        w.totals = dict(w.counts)

        @block.sync
        def _(q):
            w.start_emit("sync", q)
            full(w)

        @block.tensor
        def _(t):
            w.start_emit("tensor", t)
            full(w)

        @block.vector
        def _(v):
            w.start_emit("vector", v)
            full(w)

        @block.scalar
        def _(s):
            w.start_emit("scalar", s)
            full(w)

        @block.gpsimd
        def _(g):
            w.start_emit("gpsimd", g)
            full(w)

    return nc


P0_ITEMS = [
    ("L0_attn_Wq", "b_wq0", D, D, 1.0, "cast"),
    ("L0_attn_Wk", "b_wk0", D, D, 1.0, "cast"),
    ("L0_attn_Wv", "b_wv0", D, D, 1.0, "cast"),
    ("L0_attn_Wo", "b_wo0", D, D, 1.0, "cast"),
    ("L0_attn_Wgate", "b_wg", D, 4096, 1.0, "cast"),
    ("L0_attn_Wup", "b_wu", D, 4096, 1.0, "cast"),
    ("L0_attn_Wdown", "b_wd", 4096, D, 1.0, "cast"),
    ("L0_ffn_fc1_q", "b_f1", D, 4096, "L0_ffn_fc1_scales", "dequant"),
    ("L0_ffn_fc2_q", "b_f2", 4096, D, "L0_ffn_fc2_scales", "dequant"),
    ("wqT1_f", "b_wq1", D, D, 0.125, "cast"),
    ("wkT1_f", "b_wk1", D, D, 1.0, "cast"),
    ("wvT1_f", "b_wv1", D, D, 1.0, "cast"),
    ("woT1_f", "b_wo1", D, D, 1.0, "cast"),
    ("weT_f", "b_we", 8 * D, D, 1.0, "cast"),
]


def build_p0_program():
    nc = bass.Bass("TRN2", target_bir_lowering=False)
    nc.ctx.enter_context(nc.allow_low_precision(
        reason="weights cast to bf16 for matmuls; intentional"))
    dram_in = {}
    dram_out = {}
    dram_scale = {}
    for src, dst, rows, cols, sc, kind in P0_ITEMS:
        dt_in = I32 if kind == "dequant" else F32
        dram_in[src] = nc.dram_tensor(src, [rows, cols], dt_in, kind="ExternalInput").ap()
        dram_out[dst] = nc.dram_tensor(dst, [rows, cols], BF16, kind="ExternalOutput").ap()
        if kind == "dequant":
            dram_scale[sc] = nc.dram_tensor(sc, [rows // 128, cols], F32, kind="ExternalInput").ap()
    cm128 = nc.inline_tensor(np.full((128, 1), -128.0, dtype=np.float32), name="cm128").ap()

    with ExitStack() as stack:
        E = stack.enter_context

        def sbuf(name, shape, dt):
            return E(nc.sbuf_tensor(name, shape, dt))[:, :]

        tin = [sbuf(f"tin{j}", [128, 4096], F32) for j in range(2)]
        tqn = [sbuf(f"tqn{j}", [128, 4096], I32) for j in range(2)]
        tsc = [sbuf(f"tsc{j}", [128, 4096], F32) for j in range(2)]
        tmid = [sbuf(f"tmid{j}", [128, 4096], F32) for j in range(2)]
        tout = [sbuf(f"tout{j}", [128, 4096], BF16) for j in range(2)]
        c128 = sbuf("c128_sb", [128, 1], F32)
        sems = {n: E(nc.semaphore(n)) for n in
                ("Li0", "Li1", "So0", "So1", "C", "V")}
        block = E(nc.Block())

        w = Walk(sems)
        w.TQ, w.V, w.C = None, "V", "C"

        def full(walk):
            step = 0
            for src, dst, rows, cols, sc, kind in P0_ITEMS:
                for r in range(rows // 128):
                    p = step % 2
                    ls = f"Li{p}"
                    if step >= 2:
                        walk.wait("sync", f"p0.st.{step-2}")
                    if kind == "cast":
                        walk.op("sync", lambda e, src=src, r=r, p=p, cols=cols: e.dma_start(
                            out=tin[p][:, 0:cols], in_=dram_in[src][r * 128:(r + 1) * 128, :]),
                            inc=ls, k=16, label=f"p0.ld.{step}")
                        walk.wait("scalar", f"p0.ld.{step}")
                        if step >= 2:
                            walk.wait("scalar", f"p0.st.{step-2}")
                        walk.op("scalar", lambda e, p=p, cols=cols, sc=sc: e.activation(
                            out=tout[p][:, 0:cols], in_=tin[p][:, 0:cols], func=AF.Copy,
                            scale=float(sc)),
                            inc="C", label=f"p0.cast.{step}")
                        walk.wait("gpsimd", f"p0.cast.{step}")
                    else:
                        walk.op("sync", lambda e, src=src, r=r, p=p, cols=cols: e.dma_start(
                            out=tqn[p][:, 0:cols], in_=dram_in[src][r * 128:(r + 1) * 128, :]),
                            inc=ls, k=16, label=f"p0.ldq.{step}")
                        walk.op("sync", lambda e, sc=sc, r=r, p=p, cols=cols: e.dma_start(
                            out=tsc[p][:, 0:cols],
                            in_=dram_scale[sc][r:r + 1, :].broadcast_to([128, cols])),
                            inc=ls, k=16, label=f"p0.ld.{step}")
                        walk.wait("vector", f"p0.ld.{step}")
                        walk.op("vector", lambda e, p=p, cols=cols: e.tensor_scalar_add(
                            out=tmid[p][:, 0:cols], in0=tqn[p][:, 0:cols], scalar1=-128.0),
                            inc="V", label=f"p0.mid.{step}")
                        walk.wait("vector", f"p0.mid.{step}")
                        if step >= 2:
                            walk.wait("vector", f"p0.st.{step-2}")
                        walk.op("vector", lambda e, p=p, cols=cols: e.tensor_mul(
                            out=tout[p][:, 0:cols], in0=tmid[p][:, 0:cols], in1=tsc[p][:, 0:cols]),
                            inc="V", label=f"p0.cast.{step}")
                        walk.wait("gpsimd", f"p0.cast.{step}")
                    walk.op("gpsimd", lambda e, dst=dst, r=r, p=p, cols=cols: e.dma_start(
                        out=dram_out[dst][r * 128:(r + 1) * 128, :], in_=tout[p][:, 0:cols]),
                        inc=f"So{p}", k=16, label=f"p0.st.{step}")
                    step += 1
            nst = step
            w2 = (nst + 1) // 2
            walk.wait_count("gpsimd", "So0", 16 * ((nst + 1) // 2))
            walk.wait_count("gpsimd", "So1", 16 * (nst // 2))

        w.plan()
        full(w)

        @block.sync
        def _(q):
            w.start_emit("sync", q)
            full(w)

        @block.tensor
        def _(t):
            w.start_emit("tensor", t)
            full(w)

        @block.vector
        def _(v):
            w.start_emit("vector", v)
            full(w)

        @block.scalar
        def _(s):
            w.start_emit("scalar", s)
            full(w)

        @block.gpsimd
        def _(g):
            w.start_emit("gpsimd", g)
            full(w)

    return nc


def p0_emit_fused(w, sb0, dram_in, dram_out, dram_scale):
    for src, dst, rows, cols, sc, kind in P0_ITEMS:
        for r in range(rows // 128):
            if kind == "cast":
                w.op("sync", lambda e, src=src, r=r, cols=cols: e.dma_start(
                    out=sb0["tin"][:, 0:cols], in_=dram_in[src][r * 128:(r + 1) * 128, :]),
                    inc="Lw", k=16)
                w.op("scalar", lambda e, cols=cols, sc=sc: e.activation(
                    out=sb0["tout"][:, 0:cols], in_=sb0["tin"][:, 0:cols], func=AF.Copy,
                    scale=float(sc)))
            else:
                w.op("sync", lambda e, src=src, r=r, cols=cols: e.dma_start(
                    out=sb0["tqn"][:, 0:cols], in_=dram_in[src][r * 128:(r + 1) * 128, :]),
                    inc="Lw", k=16)
                w.op("sync", lambda e, sc=sc, r=r, cols=cols: e.dma_start(
                    out=sb0["tsc"][:, 0:cols],
                    in_=dram_scale[sc][r:r + 1, :].broadcast_to([128, cols])),
                    inc="Lw", k=16)
                w.op("vector", lambda e, cols=cols: e.tensor_scalar_add(
                    out=sb0["tmid"][:, 0:cols], in0=sb0["tqn"][:, 0:cols], scalar1=-128.0))
                w.op("vector", lambda e, cols=cols: e.tensor_mul(
                    out=sb0["tout"][:, 0:cols], in0=sb0["tmid"][:, 0:cols],
                    in1=sb0["tsc"][:, 0:cols]))
            w.op("gpsimd", lambda e, dst=dst, r=r, cols=cols: e.dma_start(
                out=dram_out[dst][r * 128:(r + 1) * 128, :], in_=sb0["tout"][:, 0:cols]),
                inc="S0", k=16)


def build_fused_program(debug_streams=False,
                        batch_phases=('p0', 'p1', 'p2', 'p3', 'p5')):
    nc = bass.Bass("TRN2", target_bir_lowering=False)
    nc.ctx.enter_context(nc.allow_low_precision(
        reason="bf16 matmul operands with fp32 accumulation; intentional"))
    ikind = "ExternalOutput" if debug_streams else "Internal"

    dram_in = {}
    dram_scale = {}
    dram_b = {}
    for src, dst, rows, cols, sc, kind in P0_ITEMS:
        dt_in = I32 if kind == "dequant" else F32
        dram_in[src] = nc.dram_tensor(src, [rows, cols], dt_in, kind="ExternalInput").ap()
        dram_b[dst] = nc.dram_tensor(dst, [rows, cols], BF16, kind=ikind).ap()
        if kind == "dequant":
            dram_scale[sc] = nc.dram_tensor(sc, [rows // 128, cols], F32,
                                            kind="ExternalInput").ap()
    xin = nc.dram_tensor("xin", [D, NTOK], F32, kind="ExternalInput").ap()
    xout = nc.dram_tensor("xout", [D, NTOK], F32, kind="ExternalOutput").ap()
    xs1 = nc.dram_tensor("xs1", [D, NTOK], F32, kind=ikind).ap()
    xs2 = nc.dram_tensor("xs2", [D, NTOK], F32, kind=ikind).ap()
    xs3 = nc.dram_tensor("xs3", [D, NTOK], F32, kind=ikind).ap()
    xs4 = nc.dram_tensor("xs4", [D, NTOK], F32, kind=ikind).ap()
    sm = {
        "attn_norm_w": nc.dram_tensor("attn_norm_w", [128, KC], F32, kind="ExternalInput").ap(),
        "out_norm_w": nc.dram_tensor("out_norm_w", [128, 1], F32, kind="ExternalInput").ap(),
        "nw": nc.dram_tensor("nw", [128, KC], F32, kind="ExternalInput").ap(),
        "gamma": nc.dram_tensor("gamma", [128, KC], F32, kind="ExternalInput").ap(),
        "beta": nc.dram_tensor("beta", [128, KC], F32, kind="ExternalInput").ap(),
        "b1": nc.dram_tensor("b1", [128, 32], F32, kind="ExternalInput").ap(),
        "b2": nc.dram_tensor("b2", [128, KC], F32, kind="ExternalInput").ap(),
        "wg": nc.dram_tensor("wg", [D, 16], F32, kind="ExternalInput").ap(),
        "bg": nc.dram_tensor("bg", [1, 16], F32, kind="ExternalInput").ap(),
        "wr": nc.dram_tensor("wr", [D, 8], F32, kind="ExternalInput").ap(),
        "rb": nc.dram_tensor("rb", [1, 8], F32, kind="ExternalInput").ap(),
        "eb": nc.dram_tensor("eb", [8, D], F32, kind="ExternalInput").ap(),
    }
    consts = build_constants(nc)

    with ExitStack() as outer:
        E = outer.enter_context
        ps = [E(nc.psum_tensor(f"psb{j}", [128, 512], F32))[:, :] for j in range(8)]
        semnames = ("Lw", "Lx0", "Lx1", "Lx2", "Lp0", "Lp1", "Lp2", "Lp3",
                    "S0", "S1", "T", "V", "C")
        sems = {name: E(nc.semaphore(name)) for name in semnames}

        def scoped_alloc(fn):
            st = ExitStack()
            def sbuf(name, shape, dt):
                return st.enter_context(nc.sbuf_tensor(name, shape, dt))[:, :]
            r = fn(sbuf)
            st.close()
            return r

        def a0(sbuf):
            return {"tin": sbuf("z0_tin", [128, 4096], F32),
                    "tqn": sbuf("z0_tqn", [128, 4096], I32),
                    "tsc": sbuf("z0_tsc", [128, 4096], F32),
                    "tmid": sbuf("z0_tmid", [128, 4096], F32),
                    "tout": sbuf("z0_tout", [128, 4096], BF16)}
        sb0 = scoped_alloc(a0)

        def a1(sbuf):
            sb = {}
            sb["x0"] = sbuf("z1_x0", [128, KC * 512], F32)
            sb["x1"] = sbuf("z1_x1", [128, KC * 512], F32)
            for n in ("wq", "wk", "wv", "wo"):
                sb[n] = sbuf("z1_" + n, [128, KC * 1024], BF16)
            sb["h"] = sbuf("z1_h", [128, KC * 512], BF16)
            sb["xsq"] = sbuf("z1_xsq", [128, KC * 512], BF16)
            sb["q"] = sbuf("z1_q", [128, 8 * 512], BF16)
            sb["k"] = sbuf("z1_k", [128, 8 * 512], BF16)
            sb["kt"] = sbuf("z1_kt", [128, NCH * 1024], BF16)
            sb["vt"] = sbuf("z1_vt", [128, NCH * 1024], BF16)
            sb["o"] = sbuf("z1_o", [128, 8 * 512], F32)
            sb["on_bf"] = sbuf("z1_on", [128, 8 * 512], BF16)
            sb["at_bf"] = sbuf("z1_at", [128, 1024], BF16)
            sb["S_f32"] = sbuf("z1_Sf", [128, 8 * 128], F32)
            sb["S_bf"] = sbuf("z1_Sb", [128, 8 * 128], BF16)
            sb["nw1"] = sbuf("z1_nw1", [128, KC], F32)
            sb["onw"] = sbuf("z1_onw", [128, 1], F32)
            sb["row_a"] = sbuf("z1_rowa", [1, 512], F32)
            sb["row_b"] = sbuf("z1_rowb", [1, 512], F32)
            sb["row_bf"] = sbuf("z1_rowbf", [1, 512], BF16)
            cst = {}
            cst["ones_col"] = sbuf("z1_onescol", [128, 1], BF16)
            cst["ones_row"] = sbuf("z1_onesrow", [1, 128], BF16)
            cst["maskUT4"] = sbuf("z1_mask", [128, 512], F32)
            return sb, cst
        sb1, cst1 = scoped_alloc(a1)

        def a_mlp(z, swiglu):
            def f(sbuf):
                sb = {}
                sb["x0"] = sbuf(z + "_x0", [128, KC * 512], F32)
                sb["x1"] = sbuf(z + "_x1", [128, KC * 512], F32)
                if swiglu:
                    sb["x0s"] = sbuf(z + "_x0s", [128, KC * 512], F32)
                sb["h"] = sbuf(z + "_h", [128, KC * 512], BF16)
                sb["xsq"] = sbuf(z + "_xsq", [128, KC * 512], BF16)
                sb["xbf"] = sbuf(z + "_xbf", [128, KC * 512], BF16)
                sb["tgu"] = sbuf(z + "_tgu", [128, 32 * 512], BF16)
                sb["wslot"] = [sbuf(f"{z}_ws{j}", [128, 8192], BF16) for j in range(4)]
                sb["sg0"] = sbuf(z + "_sg0", [128, 512], BF16)
                sb["sg1"] = sbuf(z + "_sg1", [128, 512], BF16)
                sb["tmp0"] = sbuf(z + "_tmp0", [128, 512], F32)
                sb["tmp1"] = sbuf(z + "_tmp1", [128, 512], F32)
                sb["tmp2"] = sbuf(z + "_tmp2", [128, KC * 512], F32)
                for n in ("row_a", "row_b", "row_c", "row_mu"):
                    sb[n] = sbuf(f"{z}_{n}", [1, 512], F32)
                sb["row_bf"] = sbuf(z + "_rowbf", [1, 512], BF16)
                sb["row_mu_bf"] = sbuf(z + "_rowmubf", [1, 512], BF16)
                if swiglu:
                    sb["nw"] = sbuf(z + "_nw", [128, KC], F32)
                else:
                    sb["gamma"] = sbuf(z + "_gamma", [128, KC], F32)
                    sb["beta"] = sbuf(z + "_beta", [128, KC], F32)
                    sb["b1"] = sbuf(z + "_b1", [128, 32], F32)
                    sb["b2"] = sbuf(z + "_b2", [128, KC], F32)
                    sb["b2h"] = sbuf(z + "_b2h", [128, KC], F32)
                cst = {}
                cst["ones_col"] = sbuf(z + "_onescol", [128, 1], BF16)
                cst["ones_row"] = sbuf(z + "_onesrow", [1, 128], BF16)
                cst["eps11"] = sbuf(z + "_eps11", [1, 1], F32)
                return sb, cst
            return f
        sb2, cst2 = scoped_alloc(a_mlp("z2", True))
        sb3, cst3 = scoped_alloc(a_mlp("z3", False))

        def a4(sbuf):
            sb = {}
            sb["x0"] = sbuf("z4_x0", [128, KC * 512], F32)
            sb["x1"] = sbuf("z4_x1", [128, KC * 512], F32)
            sb["xbf"] = sbuf("z4_xbf", [128, KC * 512], BF16)
            for n in ("wq", "wk", "wv", "wo"):
                sb[n] = sbuf("z4_" + n, [128, KC * 1024], BF16)
            sb["wg"] = sbuf("z4_wg", [128, KC * 16], F32)
            sb["bg"] = sbuf("z4_bg", [1, 16], F32)
            for n in ("qr", "kr", "kt", "vt", "fmin", "fexp"):
                sb[n] = sbuf("z4_" + n, [128, KC * 512], BF16)
            sb["o"] = sb["fexp"]
            sb["S_f32"] = sbuf("z4_Sf", [128, 512], F32)
            sb["S_bf"] = sbuf("z4_Sb", [128, 512], BF16)
            for n in ("gate", "logg", "LT", "NLT", "Bm", "call", "eLb"):
                sb[n] = sbuf("z4_" + n, [128, 16], F32)
            sb["Lrow"] = sbuf("z4_Lrow", [16, 128], F32)
            sb["Erow"] = sbuf("z4_Erow", [16, 128], BF16)
            for n in ("D0", "D1", "PD0", "PD1", "t10", "t11"):
                sb[n] = sbuf("z4_" + n, [128, 512], BF16)
            sb["kh0"] = sbuf("z4_kh0", [128, 256], BF16)
            sb["kh1"] = sbuf("z4_kh1", [128, 256], BF16)
            sb["qt0"] = sbuf("z4_qt0", [128, 128], BF16)
            sb["qt1"] = sbuf("z4_qt1", [128, 128], BF16)
            sb["tmp2"] = sbuf("z4_tmp2", [128, KC * 512], F32)
            sb["xsq"] = sb["fmin"]
            sb["row_a"] = sb["tmp2"][0:1, 1536:2048]
            sb["row_b"] = sb["tmp2"][0:1, 2048:2560]
            sb["row_c"] = sb["tmp2"][0:1, 2560:3072]
            sb["row_mu"] = sb["tmp2"][0:1, 3072:3584]
            sb["row_bf"] = sbuf("z4_rowbf", [1, 512], BF16)
            sb["row_mu_bf"] = sbuf("z4_rowmubf", [1, 512], BF16)
            cst = {}
            cst["ones_col"] = sbuf("z4_onescol", [128, 1], BF16)
            cst["ones_row"] = sbuf("z4_onesrow", [1, 128], BF16)
            cst["ones_row_f32"] = sbuf("z4_onesrowf", [1, 128], F32)
            cst["eps11"] = sbuf("z4_eps11", [1, 1], F32)
            cst["ident_f32"] = sbuf("z4_identf", [128, 128], F32)
            cst["cums_f32"] = sbuf("z4_cums", [128, 128], F32)
            cst["row127_f32"] = sbuf("z4_row127", [128, 128], F32)
            cst["maskUT4"] = sbuf("z4_mask", [128, 512], F32)
            for h in range(16):
                cst[f"oh16f_{h}"] = sbuf(f"z4_oh16f_{h}", [16, 128], F32)
            for p in range(8):
                cst[f"prsel_{p}"] = sbuf(f"z4_prsel_{p}", [16, 128], BF16)
            return sb, cst
        sb4, cst4 = scoped_alloc(a4)

        def a5(sbuf):
            sb = {}
            sb["x0"] = sbuf("z5_x0", [128, KC * 512], F32)
            sb["x1"] = sb["x0"]
            sb["xbf"] = sbuf("z5_xbf", [128, KC * 512], BF16)
            sb["we"] = sbuf("z5_we", [128, 64 * 1024], BF16)
            sb["wr"] = sbuf("z5_wr", [128, 64], F32)
            sb["rb_f"] = sbuf("z5_rbf", [1, 8], F32)
            sb["eb_bf"] = sbuf("z5_ebbf", [8, D], BF16)
            sb["nmx"] = sbuf("z5_nmx", [128, 1], F32)
            sb["srow"] = sbuf("z5_srow", [128, 1], F32)
            sb["wrow"] = sbuf("z5_wrow", [128, 1], F32)
            sb["eexp"] = sbuf("z5_eexp", [128, 8], F32)
            sb["m0"] = sbuf("z5_m0", [128, 8], F32)
            sb["m1"] = sbuf("z5_m1", [128, 8], F32)
            sb["mT"] = sbuf("z5_mT", [8, 512], BF16)
            sb["maskbc"] = sbuf("z5_maskbc", [128, 8 * 512], BF16)
            sb["xm0"] = sbuf("z5_xm0", [128, KC * 512], BF16)
            sb["xm1"] = sbuf("z5_xm1", [128, KC * 512], BF16)
            sb["xsq"] = sb["xm1"]
            sb["tmp2"] = sbuf("z5_tmp2", [128, KC * 512], F32)
            sb["eb_f"] = sb["tmp2"][0:8, 0:1024]
            sb["row_a"] = sb["tmp2"][0:1, 1536:2048]
            sb["row_b"] = sb["tmp2"][0:1, 2048:2560]
            sb["row_c"] = sb["tmp2"][0:1, 2560:3072]
            sb["row_mu"] = sb["tmp2"][0:1, 3072:3584]
            sb["row_bf"] = sbuf("z5_rowbf", [1, 512], BF16)
            sb["row_mu_bf"] = sbuf("z5_rowmubf", [1, 512], BF16)
            cst = {}
            cst["ones_col"] = sbuf("z5_onescol", [128, 1], BF16)
            cst["ones_row"] = sbuf("z5_onesrow", [1, 128], BF16)
            cst["eps11"] = sbuf("z5_eps11", [1, 1], F32)
            cst["ident_f32"] = sbuf("z5_identf", [128, 128], F32)
            cst["ones_row_f32"] = sbuf("z5_onesrowf", [1, 128], F32)
            for e in range(8):
                cst[f"oh8_{e}"] = sbuf(f"z5_oh8_{e}", [8, 128], BF16)
            return sb, cst
        sb5, cst5 = scoped_alloc(a5)

        block = E(nc.Block())
        w = Walk(sems, serial=True)
        w.TQ, w.V, w.C = "T", "V", "C"

        def ld(walk, dst_ap, src_ap):
            walk.op("sync", lambda e, dst_ap=dst_ap, src_ap=src_ap: e.dma_start(
                out=dst_ap, in_=src_ap), inc="Lw", k=16)

        def full(walk):
            walk.batch = "p0" in batch_phases
            p0_emit_fused(walk, sb0, dram_in, dram_b, dram_scale)

            walk.batch = "p1" in batch_phases
            ld(walk, cst1["ones_col"], consts["ones_col_bf"])
            ld(walk, cst1["ones_row"], consts["ones_row_bf"])
            ld(walk, cst1["maskUT4"], consts["maskUT4_f32"])
            wbf1 = {"wq": dram_b["b_wq0"], "wk": dram_b["b_wk0"],
                    "wv": dram_b["b_wv0"], "wo": dram_b["b_wo0"]}
            p1_emit(walk, nc, {"sb": sb1, "ps": ps, "cst": cst1}, xin, xs1, wbf1,
                    {"attn_norm_w": sm["attn_norm_w"], "out_norm_w": sm["out_norm_w"]},
                    pfx="p1")

            walk.batch = "p2" in batch_phases
            ld(walk, cst2["ones_col"], consts["ones_col_bf"])
            ld(walk, cst2["ones_row"], consts["ones_row_bf"])
            ld(walk, cst2["eps11"], consts["eps11"])
            mlp_emit(walk, nc, {"sb": sb2, "ps": ps, "cst": cst2}, xs1, xs2,
                     {"wg": dram_b["b_wg"], "wu": dram_b["b_wu"], "wd": dram_b["b_wd"]},
                     {"nw": sm["nw"]}, {"kind": "swiglu", "pre": "rms", "dbg": None},
                     "p2", xin0=xin)

            walk.batch = "p3" in batch_phases
            ld(walk, cst3["ones_col"], consts["ones_col_bf"])
            ld(walk, cst3["ones_row"], consts["ones_row_bf"])
            ld(walk, cst3["eps11"], consts["eps11"])
            mlp_emit(walk, nc, {"sb": sb3, "ps": ps, "cst": cst3}, xs2, xs3,
                     {"wg": dram_b["b_f1"], "wd": dram_b["b_f2"]},
                     {"gamma": sm["gamma"], "beta": sm["beta"],
                      "b1": sm["b1"], "b2": sm["b2"]},
                     {"kind": "qfc", "pre": "ln", "dbg": None}, "p3")

            walk.batch = "p4" in batch_phases
            for cn, src_ap in (
                ("ones_col", consts["ones_col_bf"]), ("ones_row", consts["ones_row_bf"]),
                ("ones_row_f32", consts["ones_row_f32"]), ("eps11", consts["eps11"]),
                ("ident_f32", consts["ident_f32"]), ("cums_f32", consts["cums_f32"]),
                ("row127_f32", consts["row127_f32"]), ("maskUT4", consts["maskUT4_f32"]),
            ):
                ld(walk, cst4[cn], src_ap)
            for h in range(16):
                ld(walk, cst4[f"oh16f_{h}"], consts["oh16f"][h])
            for p in range(8):
                ld(walk, cst4[f"prsel_{p}"], consts["prsel"][p])
            wbf4 = {"wq": dram_b["b_wq1"], "wk": dram_b["b_wk1"],
                    "wv": dram_b["b_wv1"], "wo": dram_b["b_wo1"]}
            p4_emit(walk, nc, {"sb": sb4, "ps": ps, "cst": cst4}, xs3, xs4, wbf4,
                    {"wg": sm["wg"], "bg": sm["bg"]}, pfx="p4")

            walk.batch = "p5" in batch_phases
            ld(walk, cst5["ones_col"], consts["ones_col_bf"])
            ld(walk, cst5["ones_row"], consts["ones_row_bf"])
            ld(walk, cst5["eps11"], consts["eps11"])
            ld(walk, cst5["ident_f32"], consts["ident_f32"])
            ld(walk, cst5["ones_row_f32"], consts["ones_row_f32"])
            for e2 in range(8):
                ld(walk, cst5[f"oh8_{e2}"], consts["oh8"][e2])
            p5_emit(walk, nc, {"sb": sb5, "ps": ps, "cst": cst5}, xs4, xout,
                    {"we": dram_b["b_we"], "wr": sm["wr"]},
                    {"rb": sm["rb"], "eb": sm["eb"]}, pfx="p5")
            walk.final_waits("gpsimd")

        w.scan()
        full(w)
        n_scan = len(w.op_engines)
        w.plan()
        full(w)
        assert w.oi == n_scan, (w.oi, n_scan)
        w.totals = dict(w.counts)
        global LAST_TOTALS
        LAST_TOTALS = dict(w.totals)
        for s_, v_ in w.totals.items():
            assert v_ < 65000, (s_, v_)

        @block.sync
        def _(q):
            w.start_emit("sync", q)
            full(w)

        @block.tensor
        def _(t):
            w.start_emit("tensor", t)
            full(w)

        @block.vector
        def _(v):
            w.start_emit("vector", v)
            full(w)

        @block.scalar
        def _(s):
            w.start_emit("scalar", s)
            full(w)

        @block.gpsimd
        def _(g):
            w.start_emit("gpsimd", g)
            full(w)
            assert w.oi == n_scan, ("gpsimd", w.oi, n_scan)

    return nc


LAST_TOTALS = None


def kernel_fused_inputs(g, core_x):
    cc = np.ascontiguousarray
    m = {
        "L0_attn_Wq": cc(g["L0_attn_Wq"]), "L0_attn_Wk": cc(g["L0_attn_Wk"]),
        "L0_attn_Wv": cc(g["L0_attn_Wv"]), "L0_attn_Wo": cc(g["L0_attn_Wo"]),
        "L0_attn_Wgate": cc(g["L0_attn_Wgate"]), "L0_attn_Wup": cc(g["L0_attn_Wup"]),
        "L0_attn_Wdown": cc(g["L0_attn_Wdown"]),
        "L0_ffn_fc1_q": cc(g["L0_ffn_fc1_q"]), "L0_ffn_fc1_scales": cc(g["L0_ffn_fc1_scales"]),
        "L0_ffn_fc2_q": cc(g["L0_ffn_fc2_q"]), "L0_ffn_fc2_scales": cc(g["L0_ffn_fc2_scales"]),
        "wqT1_f": cc(g["L1_attn_Wq"].T), "wkT1_f": cc(g["L1_attn_Wk"].T),
        "wvT1_f": cc(g["L1_attn_Wv"].T), "woT1_f": cc(g["L1_attn_Wo"].T),
        "weT_f": cc(g["L1_ffn_expert_w"].transpose(0, 2, 1).reshape(8 * D, D)),
        "xin": core_x,
        "attn_norm_w": cc(g["L0_attn_attn_norm_w"].reshape(KC, 128).T),
        "out_norm_w": cc(g["L0_attn_attn_out_norm_w"].reshape(128, 1)),
        "nw": cc(g["L0_attn_mlp_norm_w"].reshape(KC, 128).T),
        "gamma": cc(g["L0_ffn_ln_gamma"].reshape(KC, 128).T),
        "beta": cc(g["L0_ffn_ln_beta"].reshape(KC, 128).T),
        "b1": cc(g["L0_ffn_fc1_bias"].reshape(32, 128).T),
        "b2": cc(g["L0_ffn_fc2_bias"].reshape(KC, 128).T),
        "wg": cc(g["L1_attn_Wg"].T),
        "bg": cc(g["L1_attn_bg"].reshape(1, 16)),
        "wr": cc(g["L1_ffn_router_w"].T),
        "rb": cc(g["L1_ffn_router_b"].reshape(1, 8)),
        "eb": cc(g["L1_ffn_expert_b"]),
    }
    return m


def kernel_fused(**inp):
    g = {k: np.asarray(v) for k, v in inp.items()}
    x = g["L0_attn_hidden_states"]
    nc = build_fused_program()
    cores = list(range(NCORE))
    in_maps = [kernel_fused_inputs(g, to_fm(x[SEQ_PER_CORE * c:SEQ_PER_CORE * (c + 1)]))
               for c in cores]
    r = run_bass_kernel_spmd(nc, in_maps, core_ids=cores)
    out = np.empty((BATCH, T, D), dtype=np.float32)
    for c in cores:
        out[SEQ_PER_CORE * c:SEQ_PER_CORE * (c + 1)] = from_fm(
            np.asarray(r.results[c]["xout"], dtype=np.float32))
    return out


def to_fm(x_core):
    s, t, d = x_core.shape
    return np.ascontiguousarray(x_core.transpose(2, 0, 1).reshape(d, s * t))


def from_fm(xT):
    d, n = xT.shape
    return np.ascontiguousarray(xT.reshape(d, n // T, T).transpose(1, 2, 0))


_PROG_CACHE = {}


def _programs():
    if "p0" not in _PROG_CACHE:
        _PROG_CACHE["p0"] = build_p0_program()
        _PROG_CACHE["p1"] = build_p1_program()
        _PROG_CACHE["p2"] = build_mlp_program("swiglu")
        _PROG_CACHE["p3"] = build_mlp_program("qfc")
        _PROG_CACHE["p4"] = build_p4_program()
        _PROG_CACHE["p5"] = build_p5_program()
    return _PROG_CACHE


def kernel_unfused(**inp):
    progs = _programs()
    cc = np.ascontiguousarray
    g = {k: np.asarray(v) for k, v in inp.items()}
    x = g["L0_attn_hidden_states"]

    p0_in = {
        "L0_attn_Wq": cc(g["L0_attn_Wq"]), "L0_attn_Wk": cc(g["L0_attn_Wk"]),
        "L0_attn_Wv": cc(g["L0_attn_Wv"]), "L0_attn_Wo": cc(g["L0_attn_Wo"]),
        "L0_attn_Wgate": cc(g["L0_attn_Wgate"]), "L0_attn_Wup": cc(g["L0_attn_Wup"]),
        "L0_attn_Wdown": cc(g["L0_attn_Wdown"]),
        "L0_ffn_fc1_q": cc(g["L0_ffn_fc1_q"]), "L0_ffn_fc1_scales": cc(g["L0_ffn_fc1_scales"]),
        "L0_ffn_fc2_q": cc(g["L0_ffn_fc2_q"]), "L0_ffn_fc2_scales": cc(g["L0_ffn_fc2_scales"]),
        "wqT1_f": cc(g["L1_attn_Wq"].T), "wkT1_f": cc(g["L1_attn_Wk"].T),
        "wvT1_f": cc(g["L1_attn_Wv"].T), "woT1_f": cc(g["L1_attn_Wo"].T),
        "weT_f": cc(g["L1_ffn_expert_w"].transpose(0, 2, 1).reshape(8 * D, D)),
    }
    res0 = run_bass_kernel_spmd(progs["p0"], [p0_in], core_ids=[0])
    wb = res0.results[0]

    cores = list(range(NCORE))
    xs0 = [to_fm(x[SEQ_PER_CORE * c:SEQ_PER_CORE * (c + 1)]) for c in cores]

    nw1 = cc(g["L0_attn_attn_norm_w"].reshape(KC, 128).T)
    onw = cc(g["L0_attn_attn_out_norm_w"].reshape(128, 1))
    in1 = [{"xin": xs0[c], "wq": wb["b_wq0"], "wk": wb["b_wk0"], "wv": wb["b_wv0"],
            "wo": wb["b_wo0"], "attn_norm_w": nw1, "out_norm_w": onw} for c in cores]
    r = run_bass_kernel_spmd(progs["p1"], in1, core_ids=cores)
    xs = [r.results[c]["xout"] for c in cores]

    nw2 = cc(g["L0_attn_mlp_norm_w"].reshape(KC, 128).T)
    in2 = [{"xin": xs[c], "xin0": xs0[c], "wg": wb["b_wg"], "wu": wb["b_wu"],
            "wd": wb["b_wd"], "nw": nw2} for c in cores]
    r = run_bass_kernel_spmd(progs["p2"], in2, core_ids=cores)
    xs = [r.results[c]["xout"] for c in cores]

    in3 = [{"xin": xs[c], "wg": wb["b_f1"], "wd": wb["b_f2"],
            "gamma": cc(g["L0_ffn_ln_gamma"].reshape(KC, 128).T),
            "beta": cc(g["L0_ffn_ln_beta"].reshape(KC, 128).T),
            "b1": cc(g["L0_ffn_fc1_bias"].reshape(32, 128).T),
            "b2": cc(g["L0_ffn_fc2_bias"].reshape(KC, 128).T)} for c in cores]
    r = run_bass_kernel_spmd(progs["p3"], in3, core_ids=cores)
    xs = [r.results[c]["xout"] for c in cores]

    in4 = [{"xin": xs[c], "wq": wb["b_wq1"], "wk": wb["b_wk1"], "wv": wb["b_wv1"],
            "wo": wb["b_wo1"], "wg": cc(g["L1_attn_Wg"].T),
            "bg": cc(g["L1_attn_bg"].reshape(1, 16))} for c in cores]
    r = run_bass_kernel_spmd(progs["p4"], in4, core_ids=cores)
    xs = [r.results[c]["xout"] for c in cores]

    in5 = [{"xin": xs[c], "we": wb["b_we"], "wr": cc(g["L1_ffn_router_w"].T),
            "rb": cc(g["L1_ffn_router_b"].reshape(1, 8)),
            "eb": cc(g["L1_ffn_expert_b"])} for c in cores]
    r = run_bass_kernel_spmd(progs["p5"], in5, core_ids=cores)

    out = np.empty((BATCH, T, D), dtype=np.float32)
    for c in cores:
        out[SEQ_PER_CORE * c:SEQ_PER_CORE * (c + 1)] = from_fm(
            np.asarray(r.results[c]["xout"], dtype=np.float32))
    return out


BATCH = 32


def kernel(**inputs):
    return kernel_fused(**inputs)
